# version 16
# baseline (speedup 1.0000x reference)
import contextlib
from functools import partial

import numpy as np
import ml_dtypes

import concourse.bass as bass
import concourse.mybir as mybir
from concourse.bass_utils import run_bass_kernel_spmd

F32 = mybir.dt.float32
BF16 = mybir.dt.bfloat16
FP8 = mybir.dt.float8e4
AF = mybir.ActivationFunctionType
PM_DR = mybir.MatmulPerfMode.DoubleRow
ALU = mybir.AluOpType
AX = mybir.AxisListType

ENGINES = ("sync", "gpsimd", "tensor", "vector", "scalar")

DBG = []
STRICT = [False]


class Sem:
    __slots__ = ("h", "n")

    def __init__(self, h):
        self.h = h
        self.n = 0


def _w(e, waits):
    for (h, v) in waits:
        e.wait_ge(h, v)


def E_dma(e, out, in_, sem, waits=()):
    _w(e, waits)
    e.dma_start(out=out, in_=in_).then_inc(sem, 16)


def E_mm(e, out, l, r, start, stop, sem=None, tp=None, pm=None, waits=()):
    _w(e, waits)
    kw = {}
    if tp is not None:
        kw["tile_position"] = tp
    if pm is not None:
        kw["perf_mode"] = pm
    i = e.matmul(out, l, r, start=start, stop=stop, **kw)
    if sem is not None:
        i.then_inc(sem, 1)


def E_tr(e, out, in_, ident, sem=None, waits=()):
    _w(e, waits)
    i = e.transpose(out, in_, ident)
    if sem is not None:
        i.then_inc(sem, 1)


def E_act(e, out, in_, func, bias=None, scale=None, sem=None, waits=()):
    _w(e, waits)
    kw = {}
    if bias is not None:
        kw["bias"] = bias
    if scale is not None:
        kw["scale"] = scale
    i = e.activation(out, in_, func, **kw)
    if sem is not None:
        i.then_inc(sem, 1)


def E_copy(e, out, in_, sem=None, waits=()):
    _w(e, waits)
    i = e.tensor_copy(out=out, in_=in_)
    if sem is not None:
        i.then_inc(sem, 1)


def E_tt(e, out, a, b, op, sem=None, waits=()):
    _w(e, waits)
    i = e.tensor_tensor(out=out, in0=a, in1=b, op=op)
    if sem is not None:
        i.then_inc(sem, 1)


def E_ts(e, out, in_, s1, op0, sem=None, waits=()):
    _w(e, waits)
    i = e.tensor_scalar(out=out, in0=in_, scalar1=s1, scalar2=None, op0=op0)
    if sem is not None:
        i.then_inc(sem, 1)


def E_stt(e, out, in0, scalar, in1, op0, op1, sem=None, waits=()):
    _w(e, waits)
    i = e.scalar_tensor_tensor(out=out, in0=in0, scalar=scalar, in1=in1,
                               op0=op0, op1=op1)
    if sem is not None:
        i.then_inc(sem, 1)


def E_red(e, out, in_, op, absv=False, sem=None, waits=()):
    _w(e, waits)
    i = e.tensor_reduce(out=out, in_=in_, axis=AX.X, op=op,
                        apply_absolute_value=absv)
    if sem is not None:
        i.then_inc(sem, 1)


def E_recip(e, out, in_, sem=None, waits=()):
    _w(e, waits)
    i = e.reciprocal(out=out, in_=in_)
    if sem is not None:
        i.then_inc(sem, 1)


def E_memset(e, ap, val, sem=None, waits=()):
    _w(e, waits)
    i = e.memset(ap, val)
    if sem is not None and i is not None:
        i.then_inc(sem, 1)


def E_cc(e, kind, op, groups, ins, outs, sem, waits=()):
    _w(e, waits)
    e.collective_compute(kind, op, replica_groups=groups, ins=ins,
                         outs=outs).then_inc(sem, 1)


def E_wait(e, waits):
    _w(e, waits)


def build_nc(D=2048, T=2048, ncores=8):
    ND = D // 128
    NTC = T // 512
    NTT = T // 128
    NQ4 = NTT // 4
    assert NTC <= 4
    LN_INV = 1.0 / D
    INV448 = float(np.float32(1.0) / np.float32(448.0))

    nc = bass.Bass("TRN2", target_bir_lowering=False)

    x_d = nc.dram_tensor("x", [D, T], F32, kind="ExternalInput").ap()
    w_d = {}
    for w in ("wq", "wk", "wv", "wo"):
        w_d[w] = nc.dram_tensor(w, [ND, 128, D], F32, kind="ExternalInput").ap()
    wf_d = nc.dram_tensor("wf", [ND, 128, D], BF16, kind="ExternalInput").ap()
    qb_d = nc.dram_tensor("qb", [128, ND], F32, kind="ExternalInput").ap()
    kb_d = nc.dram_tensor("kb", [128, ND], F32, kind="ExternalInput").ap()
    vb_d = nc.dram_tensor("vb", [128, ND], F32, kind="ExternalInput").ap()
    ob_d = nc.dram_tensor("ob", [128, ND], F32, kind="ExternalInput").ap()
    ones_d = nc.dram_tensor("onesb", [128, 1], BF16, kind="ExternalInput").ap()
    onp_d = nc.dram_tensor("onp", [128, 128], F32, kind="ExternalInput").ap()
    idb_d = nc.dram_tensor("idb", [128, 128], BF16, kind="ExternalInput").ap()

    y_d = nc.dram_tensor("y", [D, T], F32, kind="ExternalOutput").ap()

    x1_d = nc.dram_tensor("x1s", [D, T], F32)
    xn2_d = nc.dram_tensor("xn2s", [D, T], F32)
    kvin_d = nc.dram_tensor("kvin", [128 * (D + ND)], F32)
    kvout_d = nc.dram_tensor("kvout", [128 * (D + ND)], F32)
    amin_d = nc.dram_tensor("amin", [8], F32)
    amout_d = nc.dram_tensor("amout", [8], F32)

    dbg_d = {}
    for name, shape, dt in (
        ("xn", [128, ND * T], BF16), ("vtm", [128, ND * T], BF16),
        ("kvf", [128, D], F32), ("ks", [128, ND], F32),
        ("qp", [128, ND * T], BF16), ("attn", [128, ND * T], BF16),
        ("rows", [128, 1536], F32), ("sc", [1, 32], F32),
        ("xq", [128, ND * T], FP8),
    ):
        if name in DBG:
            dbg_d[name] = nc.dram_tensor("dbg_" + name, shape, dt,
                                         kind="ExternalOutput").ap()

    pairs = [[2 * i, 2 * i + 1] for i in range(max(1, ncores // 2))]
    allg = [list(range(ncores))]

    prog = {k: [] for k in ENGINES}

    with contextlib.ExitStack() as ctx:
        ent = ctx.enter_context
        pool_a = ent(nc.sbuf_tensor("pool_a", [128, ND * T], BF16))
        pool_b = ent(nc.sbuf_tensor("pool_b", [128, ND * T], BF16))
        Mt = ent(nc.sbuf_tensor("Mt", [128, T], F32))
        Rt = ent(nc.sbuf_tensor("Rt", [128, T], F32))
        sf = [ent(nc.sbuf_tensor(f"sf{i}", [128, T], F32)) for i in range(4)]
        wblk = [ent(nc.sbuf_tensor(f"wblk{i}", [128, max(D, 2048)], BF16))
                for i in range(2)]
        kstp = [ent(nc.sbuf_tensor(f"kstp{i}", [128, T], BF16)) for i in range(2)]
        stage = [ent(nc.sbuf_tensor(f"stage{i}", [128, T], BF16)) for i in range(2)]
        rows = ent(nc.sbuf_tensor("rows", [128, 1536], F32))
        sc = ent(nc.sbuf_tensor("sc", [1, 32], F32))
        ksacc = ent(nc.sbuf_tensor("ksacc", [128, ND], F32))
        kst = ent(nc.sbuf_tensor("kst", [128, 4], F32))
        ksb = ent(nc.sbuf_tensor("ksb", [128, ND], BF16))

        amx = ent(nc.sbuf_tensor("amx", [128, 1], F32))
        wmx = ent(nc.sbuf_tensor("wmx", [128, 1], F32))
        tmx = ent(nc.sbuf_tensor("tmx", [128, 1], F32))
        rxv = ent(nc.sbuf_tensor("rxv", [128, 1], F32))
        s4v = ent(nc.sbuf_tensor("s4v", [128, 1], F32))
        rxq = ent(nc.sbuf_tensor("rxq", [128, 1], F32))
        qbs = ent(nc.sbuf_tensor("qbs", [128, ND], F32))
        kbs = ent(nc.sbuf_tensor("kbs", [128, ND], F32))
        vbs = ent(nc.sbuf_tensor("vbs", [128, ND], F32))
        obs = ent(nc.sbuf_tensor("obs", [128, ND], F32))
        onesb = ent(nc.sbuf_tensor("onesbs", [128, 1], BF16))
        onp = ent(nc.sbuf_tensor("onps", [128, 128], F32))
        idb = ent(nc.sbuf_tensor("idbs", [128, 128], BF16))

        xq8 = pool_a[:, 0:ND * T // 2].bitcast(FP8)
        kvf = sf[0][:, 0:D]
        paf = pool_a[:].bitcast(F32)
        pbf = pool_b[:].bitcast(F32)
        fb = kstp[0][:, 0:512]
        ztA = pool_a[:, 0:512]
        zt = stage[1][:, 1536:2048]
        kvb = stage[0][:, 0:D]
        ZR = [kstp[i][:, 0:1024].bitcast(F32) for i in range(2)]
        wq8 = pool_a[:, ND * T // 2:ND * T].bitcast(FP8)
        wblkF = [wblk[i][:].bitcast(F32) for i in range(2)]
        kstpF = [kstp[i][:].bitcast(F32) for i in range(2)]

        pA = ent(nc.psum_tensor("pA", [128, 512], F32))
        pB = ent(nc.psum_tensor("pB", [128, 512], F32))
        pC = ent(nc.psum_tensor("pC", [128, 512], F32))
        pD = ent(nc.psum_tensor("pD", [128, 512], F32))
        pE = ent(nc.psum_tensor("pE", [128, 512], F32))
        ptrb = ent(nc.psum_tensor("ptrb", [128, 2048], BF16))
        pG = ent(nc.psum_tensor("pG", [128, 512], F32))
        pbank = [pA, pB, pC]

        s_lc = Sem(ent(nc.semaphore("s_lc")))
        s_x = [Sem(ent(nc.semaphore(f"s_x{i}"))) for i in range(2)]
        s_f = [Sem(ent(nc.semaphore(f"s_f{i}"))) for i in range(2)]
        s_y = [Sem(ent(nc.semaphore(f"s_y{i}"))) for i in range(2)]
        s_w = [Sem(ent(nc.semaphore(f"s_w{i}"))) for i in range(2)]
        s_u = [Sem(ent(nc.semaphore(f"s_u{i}"))) for i in range(2)]
        s_p = [Sem(ent(nc.semaphore(f"s_p{i}"))) for i in range(2)]
        s_cd = Sem(ent(nc.semaphore("s_cd")))
        s_gp = Sem(ent(nc.semaphore("s_gp")))
        s_cc = Sem(ent(nc.semaphore("s_cc")))
        s_mm = Sem(ent(nc.semaphore("s_mm")))
        s_se = Sem(ent(nc.semaphore("s_se")))
        s_ve = Sem(ent(nc.semaphore("s_ve")))

        block = ent(nc.Block())

        def A(eng, fn):
            prog[eng].append(fn)

        def dma(qname, sem, out, in_, waits=()):
            A(qname, partial(E_dma, out=out, in_=in_, sem=sem.h,
                             waits=tuple(waits)))
            sem.n += 16
            return sem.n

        def mm(out, l, r, start, stop, inc=False, tp=None, pm=None, waits=()):
            if STRICT[0]:
                waits = list(waits) + [(s_mm.h, s_mm.n)]
                inc = True
            A("tensor", partial(E_mm, out=out, l=l, r=r, start=start, stop=stop,
                                sem=(s_mm.h if inc else None), tp=tp, pm=pm,
                                waits=tuple(waits)))
            if inc:
                s_mm.n += 1
            return s_mm.n

        def tr(out, in_, ident, inc=True, waits=()):
            if STRICT[0]:
                waits = list(waits) + [(s_mm.h, s_mm.n)]
                inc = True
            A("tensor", partial(E_tr, out=out, in_=in_, ident=ident,
                                sem=(s_mm.h if inc else None),
                                waits=tuple(waits)))
            if inc:
                s_mm.n += 1
            return s_mm.n

        def act(out, in_, func, bias=None, scale=None, inc=True, waits=()):
            waits = list(waits) + [(s_se.h, s_se.n)]
            inc = True
            A("scalar", partial(E_act, out=out, in_=in_, func=func, bias=bias,
                                scale=scale, sem=(s_se.h if inc else None),
                                waits=tuple(waits)))
            if inc:
                s_se.n += 1
            return s_se.n

        def gop(fn, waits=(), **kw):
            A("gpsimd", partial(fn, sem=s_gp.h,
                                waits=tuple(list(waits) + [(s_gp.h, s_gp.n)]),
                                **kw))
            s_gp.n += 1
            return s_gp.n

        def vop(fn, inc=True, waits=(), **kw):
            waits = list(waits) + [(s_ve.h, s_ve.n)]
            inc = True
            A("vector", partial(fn, sem=(s_ve.h if inc else None),
                                waits=tuple(waits), **kw))
            if inc:
                s_ve.n += 1
            return s_ve.n

        for (dst, src) in ((qbs, qb_d), (kbs, kb_d), (vbs, vb_d), (obs, ob_d),
                           (onesb, ones_d), (onp, onp_d), (idb, idb_d)):
            dma("sync", s_lc, dst[:], src[:, :])
        LD_CONSTS = s_lc.n
        vop(E_memset, ap=ztA, val=0.0)
        vop(E_memset, ap=sc[:], val=0.0)
        vop(E_memset, ap=ksacc[:], val=0.0)
        vop(E_memset, ap=amx[:], val=0.0)
        vop(E_memset, ap=wmx[:], val=0.0)

        ptrbF = ptrb[:, 0:1024].bitcast(F32)
        ptrbF2 = ptrb[:, 1024:2048].bitcast(F32)
        SITES = {
            0: [(pD[:, :], 0), (pD[:, :], 32), (pD[:, :], 64), (pG[:, :], 0)],
            1: [(pE[:, :], 0), (pE[:, :], 32), (pE[:, :], 64), (ptrbF, 0)],
        }
        ROWP = [0, 32, 64, 0]

        def stat_zero_pass(waits, z=None):
            first = True
            for bi in (0, 1):
                for tc in range(NTC):
                    bank, p = SITES[bi][tc]
                    mm(bank[p:p + 1, :], onesb[:],
                       (z if z is not None else zt),
                       start=True, stop=False, tp=(0, p),
                       waits=(waits if first else []))
                    first = False

        ld_wf0 = []
        ld_x = []
        se_cast = []
        stat_done = []
        stat_zero_pass([(s_lc.h, LD_CONSTS), (s_ve.h, s_ve.n)], z=ztA)
        for dc in range(ND):
            sem = s_x[dc % 2]
            w = [(s_se.h, se_cast[dc - 2])] if dc >= 2 else []
            ld_x.append(dma("sync", sem, sf[dc % 2][:],
                            x_d[dc * 128:(dc + 1) * 128, :], w))
            w = [(s_lc.h, LD_CONSTS if dc == 0 else ld_wf0[dc - 1])]
            ld_wf0.append(dma("sync", s_lc,
                              pool_b[:, dc * D:(dc + 1) * D],
                              wf_d[dc, :, :], w))
            rv0 = vop(E_red, out=kst[:, 2:3],
                      in_=pool_b[:, dc * D:(dc + 1) * D],
                      op=ALU.max, absv=True, waits=[(s_lc.h, ld_wf0[dc])])
            vop(E_tt, out=wmx[:], a=wmx[:], b=kst[:, 2:3], op=ALU.max,
                waits=[(s_ve.h, rv0)], inc=False)
            w = [(sem.h, ld_x[dc])]
            if dc >= 2:
                w.append((s_mm.h, stat_done[dc - 2]))
            se_cast.append(act(stage[dc % 2][:], sf[dc % 2][:], AF.Copy,
                               waits=w))
            wv = [(s_se.h, se_cast[dc])]
            if dc >= 2:
                wv.append((s_mm.h, stat_done[dc - 2]))
            ve_sq = vop(E_tt, out=kstp[dc % 2][:], a=stage[dc % 2][:],
                        b=stage[dc % 2][:], op=ALU.mult, waits=wv)
            for tc in range(NTC):
                for bi, src in enumerate((stage[dc % 2], kstp[dc % 2])):
                    bank, p = SITES[bi][tc]
                    last = (tc == NTC - 1 and bi == 1)
                    w = []
                    if tc == 0 and bi == 0:
                        w = [(s_se.h, se_cast[dc]), (s_ve.h, ve_sq)]
                    mm(bank[p:p + 1, :], onesb[:],
                       src[:, tc * 512:(tc + 1) * 512],
                       start=False, stop=(dc == ND - 1),
                       tp=(0, p), inc=last, waits=w)
            stat_done.append(s_mm.n)
        GP_WMX = s_gp.n
        VE_WMX = s_ve.n

        def ln_rows(eps):
            mmv = s_mm.n
            for tc in range(NTC):
                p = ROWP[tc]
                rt = rows if tc < 3 else sf[1]
                bs, ps = SITES[0][tc]
                bq, pq = SITES[1][tc]
                r_m = rt[p:p + 1, 0:512]
                r_1 = rt[p:p + 1, 512:1024]
                r_r = rt[p:p + 1, 1024:1536]
                vop(E_ts, out=r_m, in_=bs[ps:ps + 1, :], s1=LN_INV,
                    op0=ALU.mult, waits=[(s_mm.h, mmv)])
                vop(E_ts, out=r_1, in_=bq[pq:pq + 1, :], s1=LN_INV,
                    op0=ALU.mult)
                vop(E_tt, out=r_r, a=r_m, b=r_m, op=ALU.mult)
                vop(E_tt, out=r_1, a=r_1, b=r_r, op=ALU.subtract)
                sv = vop(E_ts, out=r_r, in_=r_1, s1=eps, op0=ALU.add)
                act(r_1, r_r, AF.Ln, waits=[(s_ve.h, sv)], inc=False)
                act(r_r, r_1, AF.Exp, scale=-0.5)
            for tc in range(NTC):
                p = ROWP[tc]
                rt = rows if tc < 3 else sf[1]
                mm(pC[:, :], onp[p:p + 1, 0:128], rt[p:p + 1, 0:512],
                   start=True, stop=True, inc=True,
                   waits=[(s_se.h, s_se.n), (s_ve.h, s_ve.n)])
                vop(E_copy, out=Mt[:, tc * 512:(tc + 1) * 512], in_=pC[:, :],
                    waits=[(s_mm.h, s_mm.n)])
                mm(pC[:, :], onp[p:p + 1, 0:128], rt[p:p + 1, 1024:1536],
                   start=True, stop=True, inc=True,
                   waits=[(s_ve.h, s_ve.n)])
                vop(E_copy, out=Rt[:, tc * 512:(tc + 1) * 512], in_=pC[:, :],
                    waits=[(s_mm.h, s_mm.n)])

        ln_rows(1e-5)
        VE_MR1 = s_ve.n

        ld2 = []
        ve_norm = []
        SE_P1A = s_se.n
        for dc in range(ND):
            sem = s_x[dc % 2]
            w = ([(s_ve.h, ve_norm[dc - 2])] if dc >= 2
                 else [(s_se.h, SE_P1A), (s_mm.h, s_mm.n)])
            ld2.append(dma("sync", sem, sf[dc % 2][:],
                           x_d[dc * 128:(dc + 1) * 128, :], w))
            w = [(sem.h, ld2[dc])]
            if dc == 0:
                w.append((s_ve.h, VE_MR1))
            vop(E_tt, out=sf[2][:], a=sf[dc % 2][:], b=Mt[:], op=ALU.subtract,
                waits=w, inc=False)
            ve_norm.append(vop(E_tt, out=pool_a[:, dc * T:(dc + 1) * T],
                               a=sf[2][:], b=Rt[:], op=ALU.mult))
        VE_XN = s_ve.n
        if "xn" in dbg_d:
            dma("sync", s_y[0], dbg_d["xn"][:, :], pool_a[:], [(s_ve.h, VE_XN)])

        def proj_phase(wname, bias_s, kind, tm_dest=None, first_mm_waits=(),
                       xn_ready=None):
            MM_ENTRY = s_mm.n
            ld_w = {}
            se_done = {}
            stt_done = {}
            ve_stage = {}
            ve_tm = {}
            mm_grp = {}
            tpos_mm = {}
            tpos_ve = {}
            kv_mm = {}
            kv_free = {}
            ks_done = {}

            def emit_wload(fc):
                w = []
                if fc >= 2:
                    w.append((s_mm.h, mm_grp[(fc - 2) * NTC + NTC - 1]))
                else:
                    w.append((s_mm.h, MM_ENTRY))
                ld_w[fc] = dma("gpsimd", s_w[fc % 2], wblk[fc % 2][:, 0:D],
                               w_d[wname][fc, :, :], w)

            def emit_groups_and_evacs(fc):
                stg = stage[fc % 2] if kind != "q" else None
                stage_ready = None
                for tc in range(NTC):
                    g = fc * NTC + tc
                    w = list(first_mm_waits) if g == 0 else []
                    w.append((s_w[fc % 2].h, ld_w[fc]))
                    if g >= 3:
                        w.append((s_se.h, se_done[g - 3]))
                    for dc in range(ND):
                        wd = list(w) if dc == 0 else []
                        if xn_ready is not None and g == 0:
                            wd.append((s_ve.h, xn_ready[dc]))
                        mm(pbank[g % 3][:, :],
                           wblk[fc % 2][:, dc * 128:(dc + 1) * 128],
                           pool_a[:, dc * T + tc * 512: dc * T + tc * 512 + 512],
                           start=(dc == 0), stop=(dc == ND - 1),
                           inc=(dc == ND - 1), waits=wd)
                    mm_grp[g] = s_mm.n
                    bank = pbank[g % 3]
                    sl = slice(tc * 512, (tc + 1) * 512)
                    w = [(s_mm.h, mm_grp[g])]
                    if tc == 0 and kind in ("v", "k") and fc >= 2:
                        w.append((s_mm.h, tpos_mm[fc - 2]))
                        if kind == "k":
                            w.append((s_ve.h, ks_done[fc - 2]))
                    if kind == "v":
                        se_done[g] = act(stg[:, sl], bank[:, :], AF.Identity,
                                         bias=bias_s[:, fc:fc + 1], waits=w)
                        stt_done[g] = s_ve.n
                        stage_ready = (s_se.h, se_done[g])
                    else:
                        if g >= NTC:
                            w.append((s_ve.h, stt_done[g - NTC]))
                        act(sf[2][:, sl], bank[:, :], AF.Exp,
                            bias=bias_s[:, fc:fc + 1], waits=w, inc=False)
                        se_done[g] = act(sf[3][:, sl], bank[:, :], AF.Relu,
                                         bias=bias_s[:, fc:fc + 1])
                        dst = (pool_b[:, fc * T + tc * 512: fc * T + tc * 512 + 512]
                               if kind == "q" else stg[:, sl])
                        stt_done[g] = vop(
                            E_stt, out=dst, in0=sf[2][:, sl], scalar=1.0,
                            in1=sf[3][:, sl], op0=ALU.min, op1=ALU.add,
                            waits=[(s_se.h, se_done[g])])
                        stage_ready = (s_ve.h, stt_done[g])
                ve_stage[fc] = stage_ready

            def emit_transposes(fc):
                stg = stage[fc % 2]
                for q4 in range(NQ4):
                    half = q4 % 2
                    gi = fc * NQ4 + q4
                    for j in range(4):
                        tt = q4 * 4 + j
                        w = []
                        if j == 0:
                            w.append(ve_stage[fc])
                            if gi >= 2:
                                w.append((s_ve.h, tpos_ve[gi - 2]))
                        tr(ptrb[:, half * 1024 + j * 128: half * 1024 + (j + 1) * 128],
                           stg[:, tt * 128:(tt + 1) * 128], idb[:],
                           inc=(j == 3), waits=w)
                    if kind == "v":
                        d3 = tm_dest[:].rearrange("p (tt d) -> p tt d", d=D)[
                            :, q4 * 4:q4 * 4 + 4, fc * 128:(fc + 1) * 128]
                        w = [(s_mm.h, s_mm.n)]
                    else:
                        d3 = kstp[fc % 2][:].rearrange("p (tt d) -> p tt d",
                                                       d=128)[:, q4 * 4:q4 * 4 + 4, :]
                        w = [(s_mm.h, s_mm.n)]
                        if fc >= 2 and q4 == 0:
                            w.append((s_mm.h, kv_mm[fc - 2]))
                    s3 = ptrb[:, half * 1024:half * 1024 + 512].rearrange(
                        "p (a b) -> p a b", b=128)
                    tpos_ve[gi] = vop(E_copy, out=d3, in_=s3, waits=w)
                tpos_mm[fc] = s_mm.n

            def emit_kv(fc):
                for tt in range(NTT):
                    w = []
                    if tt == 0:
                        w.append((s_ve.h, tpos_ve[fc * NQ4 + NQ4 - 1]))
                        if fc >= 1:
                            w.append((s_ve.h, kv_free[fc - 1]))
                        elif kind == "k":
                            w.append((s_ve.h, VE_RXV))
                    mm(pG[:, 0:128],
                       kstp[fc % 2][:, tt * 128:(tt + 1) * 128],
                       tm_dest[:].rearrange("p (tt d) -> p tt d", d=D)[
                           :, tt, fc * 128:(fc + 1) * 128],
                       start=(tt == 0), stop=(tt == NTT - 1),
                       inc=(tt == NTT - 1), waits=w)
                kv_mm[fc] = s_mm.n
                kv_free[fc] = vop(E_copy, out=kvf[:, fc * 128:(fc + 1) * 128],
                                  in_=pG[:, 0:128],
                                  waits=[(s_mm.h, kv_mm[fc])])
                for tc in range(NTC):
                    rv = vop(E_red, out=kst[:, tc:tc + 1],
                             in_=stage[fc % 2][:, tc * 512:(tc + 1) * 512],
                             op=ALU.add)
                    vop(E_tt, out=ksacc[:, fc:fc + 1], a=ksacc[:, fc:fc + 1],
                        b=kst[:, tc:tc + 1], op=ALU.add,
                        waits=[(s_ve.h, rv)], inc=(tc == NTC - 1))
                ks_done[fc] = s_ve.n

            def emit_tail(fc):
                if kind in ("v", "k"):
                    emit_transposes(fc)
                if kind == "k":
                    emit_kv(fc)

            for fc in range(ND):
                emit_wload(fc)
                emit_groups_and_evacs(fc)
                if fc > 0:
                    emit_tail(fc - 1)
            emit_tail(ND - 1)

        d1 = dma("gpsimd", s_cd,
                 kvin_d.ap()[0:128].rearrange("(p f) -> p f", f=1),
                 wmx[:, 0:1], [(s_ve.h, VE_WMX)])
        d2 = dma("gpsimd", s_cd,
                 rows[0:1, 0:128],
                 kvin_d.ap()[0:128].rearrange("(f p) -> f p", f=1),
                 [(s_cd.h, d1), (s_ve.h, VE_MR1), (s_mm.h, s_mm.n)])
        rv = vop(E_red, out=sc[0:1, 0:1], in_=rows[0:1, 0:128], op=ALU.max,
                 waits=[(s_cd.h, d2)])
        vop(E_ts, out=sc[0:1, 1:2], in_=sc[0:1, 0:1], s1=INV448,
            op0=ALU.mult, waits=[(s_ve.h, rv)])
        vop(E_ts, out=sc[0:1, 1:2], in_=sc[0:1, 1:2], s1=1e-12, op0=ALU.max)
        rv = vop(E_recip, out=sc[0:1, 2:3], in_=sc[0:1, 1:2])
        VE_WS = vop(E_ts, out=sc[0:1, 2:3], in_=sc[0:1, 2:3], s1=0.5,
                    op0=ALU.mult, waits=[(s_ve.h, rv)])
        mm(pG[:, 0:1], onp[0:1, 0:128], sc[0:1, 2:3], start=True, stop=True,
           inc=True, waits=[(s_ve.h, VE_WS)])
        VE_RXV = vop(E_copy, out=rxv[:], in_=pG[:, 0:1],
                     waits=[(s_mm.h, s_mm.n)])

        proj_phase("wv", vbs, "v", tm_dest=pool_b,
                   first_mm_waits=[(s_lc.h, LD_CONSTS), (s_gp.h, GP_WMX)],
                   xn_ready=ve_norm)
        if "vtm" in dbg_d:
            dma("sync", s_y[0], dbg_d["vtm"][:, :], pool_b[:], [(s_ve.h, s_ve.n)])

        proj_phase("wk", kbs, "k", tm_dest=pool_b)
        VE_KV = s_ve.n
        MM_KV = s_mm.n
        if "kvf" in dbg_d:
            dma("sync", s_y[0], dbg_d["kvf"][:, :], kvf, [(s_ve.h, VE_KV)])
        if "ks" in dbg_d:
            dma("sync", s_y[0], dbg_d["ks"][:, :], ksacc[:], [(s_ve.h, VE_KV)])

        dma("gpsimd", s_cd,
            kvin_d.ap()[0:128 * D].rearrange("(p f) -> p f", p=128),
            kvf, [(s_ve.h, VE_KV)])
        dma("gpsimd", s_cd,
            kvin_d.ap()[128 * D:128 * (D + ND)].rearrange("(p f) -> p f", p=128),
            ksacc[:, 0:ND], [])
        KV_IN = s_cd.n
        A("gpsimd", partial(E_cc, kind="AllReduce", op=ALU.add, groups=pairs,
                            ins=[kvin_d.ap().opt()], outs=[kvout_d.ap().opt()],
                            sem=s_cc.h, waits=((s_cd.h, KV_IN),)))
        s_cc.n += 1
        CC_KV = s_cc.n

        proj_phase("wq", qbs, "q")
        VE_QP = s_ve.n
        SE_Q = s_se.n
        if "qp" in dbg_d:
            dma("sync", s_y[0], dbg_d["qp"][:, :], pool_b[:], [(s_ve.h, VE_QP)])

        dma("gpsimd", s_cd, kvb,
            kvout_d.ap()[0:128 * D].rearrange("(p f) -> p f", p=128),
            [(s_cc.h, CC_KV)])
        dma("gpsimd", s_cd, ksb[:, 0:ND],
            kvout_d.ap()[128 * D:128 * (D + ND)].rearrange("(p f) -> p f", p=128),
            [])
        WD_KVB = s_cd.n

        L = 3
        NIT = ND * NTC
        wo_ld = {}
        se_ln = {}
        se_zr = {}
        ve_apply = {}
        zmm = {}
        zrmm = {}
        amm = {}

        def qslice(it):
            h, tc = divmod(it, NTC)
            return pool_b[:, h * T + tc * 512: h * T + tc * 512 + 512]

        def zrrow(it):
            j = it % 3
            return rows[0:1, j * 512:j * 512 + 512]

        for step in range(NIT + L):
            jt = step - L
            if 0 <= jt < NIT:
                h = jt // NTC
                w = [(s_se.h, se_ln[jt])]
                if jt >= 1:
                    w.append((s_se.h, se_zr[jt - 1]))
                zrmm[jt] = mm(pC[:, :], onp[0:1, 0:128], zrrow(jt),
                              start=True, stop=True, inc=True, waits=w)
                w = [(s_mm.h, zrmm[jt])]
                if jt >= 2:
                    w.append((s_ve.h, ve_apply[jt - 2]))
                se_zr[jt] = act(ZR[jt % 2], pC[:, :], AF.Exp, scale=-1.0,
                                waits=w)
                ab = (pA, pB)[jt % 2]
                w = []
                if jt >= 2:
                    w.append((s_ve.h, ve_apply[jt - 2]))
                amm[jt] = mm(ab[:, :], kvb[:, h * 128:(h + 1) * 128],
                             qslice(jt), start=True, stop=True, inc=True,
                             waits=w)
                ve_apply[jt] = vop(
                    E_tt,
                    out=pool_a[:, h * T + (jt % NTC) * 512:
                               h * T + (jt % NTC) * 512 + 512],
                    a=ab[:, :], b=ZR[jt % 2], op=ALU.mult,
                    waits=[(s_mm.h, amm[jt]), (s_se.h, se_zr[jt])])
                if jt % NTC == NTC - 1:
                    hh = jt // NTC
                    wo_ld[hh] = dma(
                        "gpsimd", s_w[hh % 2],
                        pool_b[:, hh * D:(hh + 1) * D].rearrange(
                            "p (g f) -> p g f", g=ND),
                        w_d["wo"][:, :, hh * 128:(hh + 1) * 128].rearrange(
                            "g p f -> p g f"),
                        [(s_mm.h, amm[jt]),
                         (s_w[hh % 2].h, s_w[hh % 2].n)])
            it = step
            if it < NIT:
                zb = (pD, pE)[it % 2]
                w = []
                if it == 0:
                    w = [(s_cd.h, WD_KVB), (s_ve.h, VE_QP), (s_se.h, SE_Q)]
                if it >= 2:
                    w.append((s_se.h, se_ln[it - 2]))
                zmm[it] = mm(zb[0:1, :], ksb[:, it // NTC:it // NTC + 1],
                             qslice(it), start=True, stop=True, inc=True,
                             waits=w)
                w = [(s_mm.h, zmm[it])]
                if it >= 3:
                    w.append((s_mm.h, zrmm[it - 3]))
                se_ln[it] = act(zrrow(it), zb[0:1, :], AF.Ln, waits=w)
        VE_ATTN = s_ve.n
        SE_P6 = s_se.n
        MM_P6 = s_mm.n
        if "attn" in dbg_d:
            dma("sync", s_y[0], dbg_d["attn"][:, :], pool_a[:], [(s_ve.h, VE_ATTN)])

        st_x1 = {}
        st_u = {}
        ld_xr = {}
        ld9 = {}
        se_sq = {}
        stt7 = {}
        statv = {}
        qfree9 = {}
        mr2v = {}

        VE_ZT2 = vop(E_memset, ap=zt, val=0.0,
                     waits=[(s_mm.h, MM_P6), (s_ve.h, VE_ATTN)])
        stat_zero_pass([(s_ve.h, VE_ZT2), (s_se.h, SE_P6)])

        def x1q(g):
            q = g % 8
            return sf[q // 4][:, (q % 4) * 512:(q % 4) * 512 + 512]

        def xrq(g):
            q = g % 4
            return wblkF[q // 2][:, (q % 2) * 512:(q % 2) * 512 + 512]

        def buf9(idx):
            q = idx % 8
            return sf[2 + q // 4][:, (q % 4) * 512:(q % 4) * 512 + 512]

        def stq(i):
            return stage[i // 4][:, (i % 4) * 512:(i % 4) * 512 + 512]

        def ln2_rows_tc(tc):
            p = ROWP[tc]
            if tc < 3:
                r_m = rows[p:p + 1, 0:512]
                r_1 = rows[p:p + 1, 512:1024]
                r_r = rows[p:p + 1, 1024:1536]
            else:
                r_m = kstpF[0][0:1, 0:512]
                r_1 = kstpF[0][0:1, 512:1024]
                r_r = kstpF[1][0:1, 0:512]
            bs, ps = SITES[0][tc]
            bq, pq = SITES[1][tc]
            vop(E_ts, out=r_m, in_=bs[ps:ps + 1, :], s1=LN_INV,
                op0=ALU.mult, waits=[(s_mm.h, statv[tc * ND + ND - 1])])
            vop(E_ts, out=r_1, in_=bq[pq:pq + 1, :], s1=LN_INV,
                op0=ALU.mult)
            vop(E_tt, out=r_r, a=r_m, b=r_m, op=ALU.mult)
            vop(E_tt, out=r_1, a=r_1, b=r_r, op=ALU.subtract)
            sv = vop(E_ts, out=r_r, in_=r_1, s1=1e-5, op0=ALU.add)
            act(r_1, r_r, AF.Ln, waits=[(s_ve.h, sv)], inc=False)
            act(r_r, r_1, AF.Exp, scale=-0.5)
            mm(ptrbF2, onp[p:p + 1, 0:128], r_m,
               start=True, stop=True, inc=True,
               waits=[(s_se.h, s_se.n), (s_ve.h, s_ve.n)])
            vop(E_copy, out=Mt[:, tc * 512:(tc + 1) * 512], in_=ptrbF2,
                waits=[(s_mm.h, s_mm.n)])
            mm(ptrbF2, onp[p:p + 1, 0:128], r_r,
               start=True, stop=True, inc=True,
               waits=[(s_ve.h, s_ve.n)])
            return vop(E_copy, out=Rt[:, tc * 512:(tc + 1) * 512],
                       in_=ptrbF2, waits=[(s_mm.h, s_mm.n)])

        def emit_p9_load(idx):
            tc9, gc9 = divmod(idx, ND)
            w = [(s_y[idx % 2].h, st_x1[idx])]
            if idx < 8:
                w += [(s_ve.h, VE_ATTN), (s_se.h, SE_P6)]
            if idx >= 2:
                w.append((s_f[idx % 2].h, ld9[idx - 2]))
            if idx >= 8:
                w += [(s_ve.h, qfree9[idx - 8]),
                      (s_u[(idx - 8) % 2].h, st_u[idx - 8])]
            ld9[idx] = dma("sync", s_f[idx % 2], buf9(idx),
                           x1_d.ap()[gc9 * 128:(gc9 + 1) * 128,
                                     tc9 * 512:(tc9 + 1) * 512], w)

        def emit_p9_chunk(idx, on_gp=False):
            tc9, gc9 = divmod(idx, ND)
            b = buf9(idx)
            tw = slice(tc9 * 512, (tc9 + 1) * 512)
            w = [(s_f[idx % 2].h, ld9[idx])]
            if gc9 == 0:
                w.append((s_ve.h, mr2v[tc9]))
            if on_gp:
                gop(E_tt, out=b, a=b, b=Mt[:, tw], op=ALU.subtract,
                    waits=w + [(s_ve.h, mr2v[tc9])])
                du = gop(E_tt, out=b, a=b, b=Rt[:, tw], op=ALU.mult)
                datw = [(s_gp.h, du)]
            else:
                vop(E_tt, out=b, a=b, b=Mt[:, tw], op=ALU.subtract,
                    waits=w, inc=False)
                du = vop(E_tt, out=b, a=b, b=Rt[:, tw], op=ALU.mult)
                datw = [(s_ve.h, du)]
            rv = vop(E_red, out=tmx[:, 0:1], in_=b, op=ALU.max, absv=True,
                     waits=datw)
            qfree9[idx] = vop(E_tt, out=amx[:], a=amx[:], b=tmx[:, 0:1],
                              op=ALU.max, waits=[(s_ve.h, rv)])
            st_u[idx] = dma("scalar", s_u[idx % 2],
                            xn2_d.ap()[gc9 * 128:(gc9 + 1) * 128, tw],
                            b, list(datw) + [(s_u[idx % 2].h,
                                              s_u[idx % 2].n)])

        wo_last = (wo_ld[ND - 2], wo_ld[ND - 1])
        for tc in range(NTC):
            tw = slice(tc * 512, (tc + 1) * 512)
            for gc in range(ND):
                g = tc * ND + gc
                w = [(s_x[g % 2].h, ld_xr[g - 2])] if g >= 2 else []
                if g >= 4:
                    w.append((s_ve.h, stt7[g - 4]))
                else:
                    w.append((s_mm.h, MM_P6))
                ld_xr[g] = dma("sync", s_x[g % 2], xrq(g),
                               x_d[gc * 128:(gc + 1) * 128, tw], w)
                w = []
                if g == 0:
                    w = [(s_ve.h, VE_ATTN), (s_se.h, SE_P6),
                         (s_w[0].h, wo_last[0]), (s_w[1].h, wo_last[1])]
                if g >= 3:
                    w.append((s_ve.h, stt7[g - 3]))
                for ec in range(ND):
                    mm(pbank[g % 3][:, :],
                       pool_b[:, ec * D + gc * 128: ec * D + gc * 128 + 128],
                       pool_a[:, ec * T + tc * 512: ec * T + tc * 512 + 512],
                       start=(ec == 0), stop=(ec == ND - 1),
                       inc=(ec == ND - 1), waits=(w if ec == 0 else []))
                mmg = s_mm.n
                w = [(s_mm.h, mmg), (s_x[g % 2].h, ld_xr[g])]
                if g >= 8:
                    w += [(s_y[g % 2].h, st_x1[g - 8]),
                          (s_se.h, se_sq[g - 8])]
                stt7[g] = vop(E_stt, out=x1q(g), in0=pbank[g % 3][:, :],
                              scalar=obs[:, gc:gc + 1], in1=xrq(g),
                              op0=ALU.add, op1=ALU.add, waits=w)
                st_x1[g] = dma("scalar", s_y[g % 2],
                               x1_d.ap()[gc * 128:(gc + 1) * 128, tw],
                               x1q(g), [(s_ve.h, stt7[g]),
                                        (s_y[g % 2].h, s_y[g % 2].n)])
                pr = g % 3
                w = [(s_ve.h, stt7[g])]
                if g >= 3:
                    w.append((s_mm.h, statv[g - 3]))
                else:
                    w.append((s_mm.h, MM_P6))
                act(stq(2 * pr), x1q(g), AF.Copy, waits=w, inc=False)
                se_sq[g] = act(stq(2 * pr + 1), x1q(g), AF.Square)
                for bi in range(2):
                    bank, p = SITES[bi][tc]
                    mm(bank[p:p + 1, :], onesb[:], stq(2 * pr + bi),
                       start=False, stop=(gc == ND - 1), tp=(0, p),
                       inc=(bi == 1),
                       waits=([(s_se.h, se_sq[g])] if bi == 0 else []))
                statv[g] = s_mm.n
                if tc >= 1:
                    idx = (tc - 1) * ND + gc
                    if gc == 0:
                        emit_p9_load(idx)
                        emit_p9_load(idx + 1)
                    if gc < ND - 2:
                        emit_p9_load(idx + 2)
                    emit_p9_chunk(idx)
            mr2v[tc] = ln2_rows_tc(tc)
        VE_P7 = s_ve.n
        SE_P7 = s_se.n
        MM_P7END = s_mm.n
        Y0_P7 = s_y[0].n
        Y1_P7 = s_y[1].n

        for gc in range(ND):
            idx = (NTC - 1) * ND + gc
            if gc == 0:
                emit_p9_load(idx)
                emit_p9_load(idx + 1)
            if gc < ND - 2:
                emit_p9_load(idx + 2)
            emit_p9_chunk(idx, on_gp=(gc % 2 == 1))
        VE_AMX = s_ve.n
        GP_AMX = s_gp.n
        U0_P9 = s_u[0].n
        U1_P9 = s_u[1].n

        ld_wf = {}
        se_wq = {}
        for gcw in range(ND):
            w = [(s_f[gcw % 2].h, s_f[gcw % 2].n)]
            if gcw < 2:
                w.append((s_ve.h, VE_P7))
            else:
                w.append((s_se.h, se_wq[gcw - 2]))
            ld_wf[gcw] = dma("sync", s_f[gcw % 2], wblk[gcw % 2][:, 0:D],
                             wf_d[gcw, :, :], w)
            w = [(s_f[gcw % 2].h, ld_wf[gcw])]
            if gcw == 0:
                w += [(s_mm.h, MM_P7END), (s_ve.h, VE_RXV)]
            se_wq[gcw] = act(wq8[:, gcw * D:(gcw + 1) * D],
                             wblk[gcw % 2][:, 0:D], AF.Identity,
                             scale=rxv[:, 0:1], waits=w)
        SE_WQ = s_se.n

        d3 = dma("gpsimd", s_cd,
                 kvin_d.ap()[0:128].rearrange("(p f) -> p f", f=1),
                 amx[:, 0:1], [(s_ve.h, VE_AMX), (s_gp.h, GP_AMX)])
        d4 = dma("gpsimd", s_cd,
                 rows[0:1, 0:128],
                 kvin_d.ap()[0:128].rearrange("(f p) -> f p", f=1),
                 [(s_cd.h, d3), (s_ve.h, s_ve.n)])
        VE_AM1 = vop(E_red, out=sc[0:1, 16:17], in_=rows[0:1, 0:128],
                     op=ALU.max, waits=[(s_cd.h, d4)])
        dma("gpsimd", s_cd, amin_d.ap()[0:8].rearrange("(a b) -> a b", a=1),
            sc[0:1, 16:24], [(s_ve.h, VE_AM1)])
        A("gpsimd", partial(E_cc, kind="AllReduce", op=ALU.max, groups=allg,
                            ins=[amin_d.ap().opt()], outs=[amout_d.ap().opt()],
                            sem=s_cc.h, waits=((s_cd.h, s_cd.n),)))
        s_cc.n += 1
        dma("gpsimd", s_cd, sc[0:1, 24:32],
            amout_d.ap()[0:8].rearrange("(a b) -> a b", a=1),
            [(s_cc.h, s_cc.n)])
        WD_AM = s_cd.n
        vop(E_ts, out=sc[0:1, 17:18], in_=sc[0:1, 24:25], s1=INV448,
            op0=ALU.mult, waits=[(s_cd.h, WD_AM)])
        vop(E_ts, out=sc[0:1, 17:18], in_=sc[0:1, 17:18], s1=1e-12,
            op0=ALU.max)
        rv = vop(E_recip, out=sc[0:1, 18:19], in_=sc[0:1, 17:18])
        vop(E_ts, out=sc[0:1, 18:19], in_=sc[0:1, 18:19], s1=0.5,
            op0=ALU.mult, waits=[(s_ve.h, rv)])
        vop(E_tt, out=sc[0:1, 19:20], a=sc[0:1, 17:18], b=sc[0:1, 1:2],
            op=ALU.mult)
        VE_SC = vop(E_ts, out=sc[0:1, 19:20], in_=sc[0:1, 19:20], s1=4.0,
                    op0=ALU.mult)
        if "sc" in dbg_d:
            dma("sync", s_y[0], dbg_d["sc"][:, :], sc[:], [(s_ve.h, VE_SC)])
        mm(pG[:, 0:1], onp[0:1, 0:128], sc[0:1, 18:19], start=True, stop=True,
           inc=True, waits=[(s_ve.h, VE_SC)])
        VE_RXQ = vop(E_copy, out=rxq[:], in_=pG[:, 0:1],
                     waits=[(s_mm.h, s_mm.n)])
        mm(pG[:, 0:1], onp[0:1, 0:128], sc[0:1, 19:20], start=True, stop=True,
           inc=True, waits=[(s_ve.h, s_ve.n)])
        VE_S4 = vop(E_copy, out=s4v[:], in_=pG[:, 0:1],
                    waits=[(s_mm.h, s_mm.n)])

        ld_p11 = {}
        ld_q = {}
        ve_q = {}
        npre = ND // 2
        for j in range(npre):
            w = [(s_u[j % 2].h, st_u[(NTC - 1) * ND + j]),
                 (s_p[j % 2].h, s_p[j % 2].n), (s_mm.h, MM_P7END)]
            ld_p11[j] = dma("sync", s_p[j % 2], pbf[:, j * T:(j + 1) * T],
                            xn2_d.ap()[j * 128:(j + 1) * 128, :], w)
        for j in range(npre):
            w = [(s_p[j % 2].h, ld_p11[j])]
            if j == 0:
                w.append((s_ve.h, VE_RXQ))
            ve_q[j] = vop(E_ts, out=xq8[:, j * T:(j + 1) * T],
                          in_=pbf[:, j * T:(j + 1) * T], s1=rxq[:, 0:1],
                          op0=ALU.mult, waits=w)
        for i in range(ND - npre + 2):
            if i >= 2:
                dcv = npre + i - 2
                ve_q[dcv] = vop(E_ts, out=xq8[:, dcv * T:(dcv + 1) * T],
                                in_=sf[dcv % 2][:], s1=rxq[:, 0:1],
                                op0=ALU.mult,
                                waits=[(s_x[dcv % 2].h, ld_q[dcv])])
            dcq = npre + i
            if dcq <= ND - 1:
                w = [(s_u[dcq % 2].h, st_u[(NTC - 1) * ND + dcq]),
                     (s_x[dcq % 2].h, s_x[dcq % 2].n)]
                if dcq < npre + 2:
                    w += [(s_se.h, SE_P7), (s_y[0].h, Y0_P7),
                          (s_y[1].h, Y1_P7)]
                else:
                    w.append((s_ve.h, ve_q[dcq - 2]))
                ld_q[dcq] = dma("sync", s_x[dcq % 2], sf[dcq % 2][:],
                                xn2_d.ap()[dcq * 128:(dcq + 1) * 128, :], w)
        VE_XQ = s_ve.n
        if "xq" in dbg_d:
            dma("sync", s_y[0], dbg_d["xq"][:, :], xq8[:], [(s_ve.h, VE_XQ)])

        ld_r2 = {}
        ve_out = {}
        st_y = {}
        v12 = {}
        nrr = min(2, ND // 2)
        for gc in range(ND):
            w = [(s_x[gc % 2].h, s_x[gc % 2].n)]
            if gc >= nrr:
                w.append((s_ve.h, ve_out[gc - nrr]))
            else:
                w.append((s_ve.h, ve_q[gc]))
            ld_r2[gc] = dma("sync", s_x[gc % 2],
                            pbf[:, (gc % nrr) * T:(gc % nrr) * T + T],
                            x1_d.ap()[gc * 128:(gc + 1) * 128, :], w)
            for tc in range(NTC):
                g = gc * NTC + tc
                w = []
                if g == 0:
                    w = [(s_ve.h, VE_XQ), (s_se.h, SE_WQ)]
                if g >= 3:
                    w.append((s_ve.h, v12[g - 3]))
                for dc in range(ND // 2):
                    lhsT = wq8[:, gc * D + dc * 256:
                               gc * D + dc * 256 + 256].rearrange(
                        "p (j c) -> p j c", j=2)
                    rhs = xq8[:, dc * 2 * T:(dc * 2 + 2) * T].rearrange(
                        "p (j t) -> p j t", j=2)[:, :, tc * 512:(tc + 1) * 512]
                    mm(pbank[g % 3][:, :], lhsT, rhs,
                       start=(dc == 0), stop=(dc == ND // 2 - 1),
                       inc=(dc == ND // 2 - 1), pm=PM_DR,
                       waits=(w if dc == 0 else []))
                mg = s_mm.n
                sl = slice(tc * 512, (tc + 1) * 512)
                vop(E_ts, out=fb, in_=pbank[g % 3][:, :], s1=s4v[:, 0:1],
                    op0=ALU.mult, waits=[(s_mm.h, mg)], inc=False)
                w = [(s_x[gc % 2].h, ld_r2[gc])]
                if gc >= 2 and tc == 0:
                    w.append((s_y[gc % 2].h, st_y[gc - 2]))
                if gc < 2 and tc == 0:
                    w += [(s_u[0].h, U0_P9), (s_u[1].h, U1_P9)]
                v12[g] = vop(E_tt, out=sf[2 + gc % 2][:, sl],
                             a=pbf[:, (gc % nrr) * T + tc * 512:
                                   (gc % nrr) * T + (tc + 1) * 512],
                             b=fb, op=ALU.add, waits=w)
            ve_out[gc] = s_ve.n
            st_y[gc] = dma("scalar", s_y[gc % 2],
                           y_d[gc * 128:(gc + 1) * 128, :],
                           sf[2 + gc % 2][:],
                           [(s_ve.h, ve_out[gc]),
                            (s_y[gc % 2].h, s_y[gc % 2].n)])

        A("sync", partial(E_wait, waits=((s_y[0].h, s_y[0].n),
                                         (s_y[1].h, s_y[1].n))))

        @block.sync
        def _(eng):
            for f in prog["sync"]:
                f(eng)

        @block.gpsimd
        def _(eng):
            for f in prog["gpsimd"]:
                f(eng)

        @block.tensor
        def _(eng):
            for f in prog["tensor"]:
                f(eng)

        @block.vector
        def _(eng):
            for f in prog["vector"]:
                f(eng)

        @block.scalar
        def _(eng):
            for f in prog["scalar"]:
                f(eng)

    return nc


def _wblocks(wt, dt):
    d_in, d_out = wt.shape
    a = np.ascontiguousarray(
        wt.reshape(d_in // 128, 128, d_out // 128, 128).transpose(2, 1, 0, 3)
    ).reshape(d_out // 128, 128, d_in)
    return np.ascontiguousarray(a.astype(dt, copy=False))


def kernel(L0_attn_hidden_states, L0_attn_q_w, L0_attn_q_b, L0_attn_k_w,
           L0_attn_k_b, L0_attn_v_w, L0_attn_v_b, L0_attn_o_w, L0_attn_o_b,
           L0_ffn_weight):
    x = np.asarray(L0_attn_hidden_states)
    B, S, D = x.shape
    ncores = 8
    Tc = (B * S) // ncores
    ND = D // 128

    nc = build_nc(D=D, T=Tc, ncores=ncores)

    wq = _wblocks(np.ascontiguousarray(np.asarray(L0_attn_q_w).T), np.float32)
    wk = _wblocks(np.ascontiguousarray(np.asarray(L0_attn_k_w).T), np.float32)
    wv = _wblocks(np.ascontiguousarray(np.asarray(L0_attn_v_w).T), np.float32)
    wo = _wblocks(np.ascontiguousarray(np.asarray(L0_attn_o_w).T), np.float32)
    wf = _wblocks(np.ascontiguousarray(np.asarray(L0_ffn_weight).T),
                  ml_dtypes.bfloat16)
    qb = np.ascontiguousarray(np.asarray(L0_attn_q_b, np.float32).reshape(ND, 128).T)
    kb = np.ascontiguousarray(np.asarray(L0_attn_k_b, np.float32).reshape(ND, 128).T)
    vb = np.ascontiguousarray(np.asarray(L0_attn_v_b, np.float32).reshape(ND, 128).T)
    ob = np.ascontiguousarray(np.asarray(L0_attn_o_b, np.float32).reshape(ND, 128).T)
    onesb = np.ones((128, 1), ml_dtypes.bfloat16)
    onp_ = np.ones((128, 128), np.float32)
    idb = np.eye(128, dtype=ml_dtypes.bfloat16)

    halves = S // 2
    in_maps = []
    for c in range(ncores):
        b, h = c >> 1, c & 1
        slab = np.asarray(x[b, h * halves:(h + 1) * halves, :], np.float32)
        x_fm = np.ascontiguousarray(slab.T)
        in_maps.append({
            "x": x_fm, "wq": wq, "wk": wk, "wv": wv, "wo": wo, "wf": wf,
            "qb": qb, "kb": kb, "vb": vb, "ob": ob, "onesb": onesb,
            "onp": onp_, "idb": idb,
        })

    res = run_bass_kernel_spmd(nc, in_maps, core_ids=list(range(ncores)))

    out = np.empty((B, S, D), np.float32)
    for c in range(ncores):
        b, h = c >> 1, c & 1
        out[b, h * halves:(h + 1) * halves, :] = res.results[c]["y"].T
    return out


# version 17
# speedup vs baseline: 1.1778x; 1.1778x over previous
import contextlib
from functools import partial

import numpy as np
import ml_dtypes

import concourse.bass as bass
import concourse.mybir as mybir
from concourse.bass_utils import run_bass_kernel_spmd

F32 = mybir.dt.float32
BF16 = mybir.dt.bfloat16
FP8 = mybir.dt.float8e4
AF = mybir.ActivationFunctionType
PM_DR = mybir.MatmulPerfMode.DoubleRow
ALU = mybir.AluOpType
AX = mybir.AxisListType

ENGINES = ("sync", "gpsimd", "tensor", "vector", "scalar")

DBG = []
STRICT = [False]


class Sem:
    __slots__ = ("h", "n")

    def __init__(self, h):
        self.h = h
        self.n = 0


def _w(e, waits):
    for (h, v) in waits:
        e.wait_ge(h, v)


def E_dma(e, out, in_, sem, waits=()):
    _w(e, waits)
    e.dma_start(out=out, in_=in_).then_inc(sem, 16)


def E_mm(e, out, l, r, start, stop, sem=None, tp=None, pm=None, waits=()):
    _w(e, waits)
    kw = {}
    if tp is not None:
        kw["tile_position"] = tp
    if pm is not None:
        kw["perf_mode"] = pm
    i = e.matmul(out, l, r, start=start, stop=stop, **kw)
    if sem is not None:
        i.then_inc(sem, 1)


def E_tr(e, out, in_, ident, sem=None, waits=()):
    _w(e, waits)
    i = e.transpose(out, in_, ident)
    if sem is not None:
        i.then_inc(sem, 1)


def E_act(e, out, in_, func, bias=None, scale=None, sem=None, waits=()):
    _w(e, waits)
    kw = {}
    if bias is not None:
        kw["bias"] = bias
    if scale is not None:
        kw["scale"] = scale
    i = e.activation(out, in_, func, **kw)
    if sem is not None:
        i.then_inc(sem, 1)


def E_copy(e, out, in_, sem=None, waits=()):
    _w(e, waits)
    i = e.tensor_copy(out=out, in_=in_)
    if sem is not None:
        i.then_inc(sem, 1)


def E_tt(e, out, a, b, op, sem=None, waits=()):
    _w(e, waits)
    i = e.tensor_tensor(out=out, in0=a, in1=b, op=op)
    if sem is not None:
        i.then_inc(sem, 1)


def E_ts(e, out, in_, s1, op0, sem=None, waits=()):
    _w(e, waits)
    i = e.tensor_scalar(out=out, in0=in_, scalar1=s1, scalar2=None, op0=op0)
    if sem is not None:
        i.then_inc(sem, 1)


def E_stt(e, out, in0, scalar, in1, op0, op1, sem=None, waits=()):
    _w(e, waits)
    i = e.scalar_tensor_tensor(out=out, in0=in0, scalar=scalar, in1=in1,
                               op0=op0, op1=op1)
    if sem is not None:
        i.then_inc(sem, 1)


def E_red(e, out, in_, op, absv=False, sem=None, waits=()):
    _w(e, waits)
    i = e.tensor_reduce(out=out, in_=in_, axis=AX.X, op=op,
                        apply_absolute_value=absv)
    if sem is not None:
        i.then_inc(sem, 1)


def E_recip(e, out, in_, sem=None, waits=()):
    _w(e, waits)
    i = e.reciprocal(out=out, in_=in_)
    if sem is not None:
        i.then_inc(sem, 1)


def E_memset(e, ap, val, sem=None, waits=()):
    _w(e, waits)
    i = e.memset(ap, val)
    if sem is not None and i is not None:
        i.then_inc(sem, 1)


def E_cc(e, kind, op, groups, ins, outs, sem, waits=()):
    _w(e, waits)
    e.collective_compute(kind, op, replica_groups=groups, ins=ins,
                         outs=outs).then_inc(sem, 1)


def E_wait(e, waits):
    _w(e, waits)


def build_nc(D=2048, T=2048, ncores=8):
    ND = D // 128
    NTC = T // 512
    NTT = T // 128
    NQ4 = NTT // 4
    assert NTC <= 4
    LN_INV = 1.0 / D
    INV448 = float(np.float32(1.0) / np.float32(448.0))

    nc = bass.Bass("TRN2", target_bir_lowering=False)

    x_d = nc.dram_tensor("x", [D, T], F32, kind="ExternalInput").ap()
    w_d = {}
    for w in ("wq", "wk", "wv", "wo"):
        w_d[w] = nc.dram_tensor(w, [ND, 128, D], F32, kind="ExternalInput").ap()
    wf_d = nc.dram_tensor("wf", [ND, 128, D], BF16, kind="ExternalInput").ap()
    qb_d = nc.dram_tensor("qb", [128, ND], F32, kind="ExternalInput").ap()
    kb_d = nc.dram_tensor("kb", [128, ND], F32, kind="ExternalInput").ap()
    vb_d = nc.dram_tensor("vb", [128, ND], F32, kind="ExternalInput").ap()
    ob_d = nc.dram_tensor("ob", [128, ND], F32, kind="ExternalInput").ap()
    ones_d = nc.dram_tensor("onesb", [128, 1], BF16, kind="ExternalInput").ap()
    onp_d = nc.dram_tensor("onp", [128, 128], F32, kind="ExternalInput").ap()
    idb_d = nc.dram_tensor("idb", [128, 128], BF16, kind="ExternalInput").ap()

    y_d = nc.dram_tensor("y", [D, T], F32, kind="ExternalOutput").ap()

    x1_d = nc.dram_tensor("x1s", [D, T], F32)
    xn2_d = nc.dram_tensor("xn2s", [D, T], F32)
    kvin_d = nc.dram_tensor("kvin", [128 * (D + ND)], F32)
    kvout_d = nc.dram_tensor("kvout", [128 * (D + ND)], F32)
    amin_d = nc.dram_tensor("amin", [8], F32)
    amout_d = nc.dram_tensor("amout", [8], F32)

    dbg_d = {}
    for name, shape, dt in (
        ("xn", [128, ND * T], BF16), ("vtm", [128, ND * T], BF16),
        ("kvf", [128, D], F32), ("ks", [128, ND], F32),
        ("qp", [128, ND * T], BF16), ("attn", [128, ND * T], BF16),
        ("rows", [128, 1536], F32), ("sc", [1, 32], F32),
        ("xq", [128, ND * T], FP8),
    ):
        if name in DBG:
            dbg_d[name] = nc.dram_tensor("dbg_" + name, shape, dt,
                                         kind="ExternalOutput").ap()

    pairs = [[2 * i, 2 * i + 1] for i in range(max(1, ncores // 2))]
    allg = [list(range(ncores))]

    prog = {k: [] for k in ENGINES}

    with contextlib.ExitStack() as ctx:
        ent = ctx.enter_context
        pool_a = ent(nc.sbuf_tensor("pool_a", [128, ND * T], BF16))
        pool_b = ent(nc.sbuf_tensor("pool_b", [128, ND * T], BF16))
        Mt = ent(nc.sbuf_tensor("Mt", [128, T], F32))
        Rt = ent(nc.sbuf_tensor("Rt", [128, T], F32))
        sf = [ent(nc.sbuf_tensor(f"sf{i}", [128, T], F32)) for i in range(4)]
        wblk = [ent(nc.sbuf_tensor(f"wblk{i}", [128, max(D, 2048)], BF16))
                for i in range(2)]
        kstp = [ent(nc.sbuf_tensor(f"kstp{i}", [128, T], BF16)) for i in range(2)]
        stage = [ent(nc.sbuf_tensor(f"stage{i}", [128, T], BF16)) for i in range(2)]
        rows = ent(nc.sbuf_tensor("rows", [128, 1536], F32))
        sc = ent(nc.sbuf_tensor("sc", [1, 32], F32))
        ksacc = ent(nc.sbuf_tensor("ksacc", [128, ND], F32))
        kst = ent(nc.sbuf_tensor("kst", [128, 4], F32))
        ksb = ent(nc.sbuf_tensor("ksb", [128, ND], BF16))

        amx = ent(nc.sbuf_tensor("amx", [128, 1], F32))
        wmx = ent(nc.sbuf_tensor("wmx", [128, 1], F32))
        tmx = ent(nc.sbuf_tensor("tmx", [128, 1], F32))
        rxv = ent(nc.sbuf_tensor("rxv", [128, 1], F32))
        s4v = ent(nc.sbuf_tensor("s4v", [128, 1], F32))
        rxq = ent(nc.sbuf_tensor("rxq", [128, 1], F32))
        qbs = ent(nc.sbuf_tensor("qbs", [128, ND], F32))
        kbs = ent(nc.sbuf_tensor("kbs", [128, ND], F32))
        vbs = ent(nc.sbuf_tensor("vbs", [128, ND], F32))
        obs = ent(nc.sbuf_tensor("obs", [128, ND], F32))
        onesb = ent(nc.sbuf_tensor("onesbs", [128, 1], BF16))
        onp = ent(nc.sbuf_tensor("onps", [128, 128], F32))
        idb = ent(nc.sbuf_tensor("idbs", [128, 128], BF16))

        xq8 = pool_a[:, 0:ND * T // 2].bitcast(FP8)
        kvf = sf[0][:, 0:D]
        paf = pool_a[:].bitcast(F32)
        pbf = pool_b[:].bitcast(F32)
        fb = kstp[0][:, 0:512]
        ztA = pool_a[:, 0:512]
        zt = stage[1][:, 1536:2048]
        kvb = stage[0][:, 0:D]
        ZR = [kstp[i][:, 0:1024].bitcast(F32) for i in range(2)]
        wq8 = pool_a[:, ND * T // 2:ND * T].bitcast(FP8)
        wblkF = [wblk[i][:].bitcast(F32) for i in range(2)]
        kstpF = [kstp[i][:].bitcast(F32) for i in range(2)]

        pA = ent(nc.psum_tensor("pA", [128, 512], F32))
        pB = ent(nc.psum_tensor("pB", [128, 512], F32))
        pC = ent(nc.psum_tensor("pC", [128, 512], F32))
        pD = ent(nc.psum_tensor("pD", [128, 512], F32))
        pE = ent(nc.psum_tensor("pE", [128, 512], F32))
        ptrb = ent(nc.psum_tensor("ptrb", [128, 2048], BF16))
        pG = ent(nc.psum_tensor("pG", [128, 512], F32))
        pbank = [pA, pB, pC]

        s_lc = Sem(ent(nc.semaphore("s_lc")))
        s_x = [Sem(ent(nc.semaphore(f"s_x{i}"))) for i in range(2)]
        s_f = [Sem(ent(nc.semaphore(f"s_f{i}"))) for i in range(2)]
        s_y = [Sem(ent(nc.semaphore(f"s_y{i}"))) for i in range(2)]
        s_w = [Sem(ent(nc.semaphore(f"s_w{i}"))) for i in range(2)]
        s_u = [Sem(ent(nc.semaphore(f"s_u{i}"))) for i in range(2)]
        s_p = [Sem(ent(nc.semaphore(f"s_p{i}"))) for i in range(2)]
        s_cd = Sem(ent(nc.semaphore("s_cd")))
        s_gp = Sem(ent(nc.semaphore("s_gp")))
        s_cc = Sem(ent(nc.semaphore("s_cc")))
        s_mm = Sem(ent(nc.semaphore("s_mm")))
        s_se = Sem(ent(nc.semaphore("s_se")))
        s_ve = Sem(ent(nc.semaphore("s_ve")))

        block = ent(nc.Block())

        def A(eng, fn):
            prog[eng].append(fn)

        def dma(qname, sem, out, in_, waits=()):
            A(qname, partial(E_dma, out=out, in_=in_, sem=sem.h,
                             waits=tuple(waits)))
            sem.n += 16
            return sem.n

        def mm(out, l, r, start, stop, inc=False, tp=None, pm=None, waits=()):
            if STRICT[0]:
                waits = list(waits) + [(s_mm.h, s_mm.n)]
                inc = True
            A("tensor", partial(E_mm, out=out, l=l, r=r, start=start, stop=stop,
                                sem=(s_mm.h if inc else None), tp=tp, pm=pm,
                                waits=tuple(waits)))
            if inc:
                s_mm.n += 1
            return s_mm.n

        def tr(out, in_, ident, inc=True, waits=()):
            if STRICT[0]:
                waits = list(waits) + [(s_mm.h, s_mm.n)]
                inc = True
            A("tensor", partial(E_tr, out=out, in_=in_, ident=ident,
                                sem=(s_mm.h if inc else None),
                                waits=tuple(waits)))
            if inc:
                s_mm.n += 1
            return s_mm.n

        def act(out, in_, func, bias=None, scale=None, inc=True, waits=()):
            waits = list(waits) + [(s_se.h, s_se.n)]
            inc = True
            A("scalar", partial(E_act, out=out, in_=in_, func=func, bias=bias,
                                scale=scale, sem=(s_se.h if inc else None),
                                waits=tuple(waits)))
            if inc:
                s_se.n += 1
            return s_se.n

        def gop(fn, waits=(), **kw):
            A("gpsimd", partial(fn, sem=s_gp.h,
                                waits=tuple(list(waits) + [(s_gp.h, s_gp.n)]),
                                **kw))
            s_gp.n += 1
            return s_gp.n

        def vop(fn, inc=True, waits=(), **kw):
            waits = list(waits) + [(s_ve.h, s_ve.n)]
            inc = True
            A("vector", partial(fn, sem=(s_ve.h if inc else None),
                                waits=tuple(waits), **kw))
            if inc:
                s_ve.n += 1
            return s_ve.n

        for (dst, src) in ((qbs, qb_d), (kbs, kb_d), (vbs, vb_d), (obs, ob_d),
                           (onesb, ones_d), (onp, onp_d), (idb, idb_d)):
            dma("sync", s_lc, dst[:], src[:, :])
        LD_CONSTS = s_lc.n
        vop(E_memset, ap=ztA, val=0.0)
        vop(E_memset, ap=sc[:], val=0.0)
        vop(E_memset, ap=ksacc[:], val=0.0)
        vop(E_memset, ap=amx[:], val=0.0)
        vop(E_memset, ap=wmx[:], val=0.0)

        ptrbF = ptrb[:, 0:1024].bitcast(F32)
        ptrbF2 = ptrb[:, 1024:2048].bitcast(F32)
        SITES = {
            0: [(pD[:, :], 0), (pD[:, :], 32), (pD[:, :], 64), (pG[:, :], 0)],
            1: [(pE[:, :], 0), (pE[:, :], 32), (pE[:, :], 64), (ptrbF, 0)],
        }
        ROWP = [0, 32, 64, 0]

        def stat_zero_pass(waits, z=None):
            first = True
            for bi in (0, 1):
                for tc in range(NTC):
                    bank, p = SITES[bi][tc]
                    mm(bank[p:p + 1, :], onesb[:],
                       (z if z is not None else zt),
                       start=True, stop=False, tp=(0, p),
                       waits=(waits if first else []))
                    first = False

        ld_wf0 = []
        ld_x = []
        se_cast = []
        stat_done = []
        stat_zero_pass([(s_lc.h, LD_CONSTS), (s_ve.h, s_ve.n)], z=ztA)
        for dc in range(ND):
            sem = s_x[dc % 2]
            w = [(s_se.h, se_cast[dc - 2])] if dc >= 2 else []
            ld_x.append(dma("sync", sem, sf[dc % 2][:],
                            x_d[dc * 128:(dc + 1) * 128, :], w))
            w = [(s_lc.h, LD_CONSTS if dc == 0 else ld_wf0[dc - 1])]
            ld_wf0.append(dma("sync", s_lc,
                              pool_b[:, dc * D:(dc + 1) * D],
                              wf_d[dc, :, :], w))
            rv0 = vop(E_red, out=kst[:, 2:3],
                      in_=pool_b[:, dc * D:(dc + 1) * D],
                      op=ALU.max, absv=True, waits=[(s_lc.h, ld_wf0[dc])])
            vop(E_tt, out=wmx[:], a=wmx[:], b=kst[:, 2:3], op=ALU.max,
                waits=[(s_ve.h, rv0)], inc=False)
            w = [(sem.h, ld_x[dc])]
            if dc >= 2:
                w.append((s_mm.h, stat_done[dc - 2]))
            se_cast.append(act(stage[dc % 2][:], sf[dc % 2][:], AF.Copy,
                               waits=w))
            wv = [(s_se.h, se_cast[dc])]
            if dc >= 2:
                wv.append((s_mm.h, stat_done[dc - 2]))
            ve_sq = vop(E_tt, out=kstp[dc % 2][:], a=stage[dc % 2][:],
                        b=stage[dc % 2][:], op=ALU.mult, waits=wv)
            for tc in range(NTC):
                for bi, src in enumerate((stage[dc % 2], kstp[dc % 2])):
                    bank, p = SITES[bi][tc]
                    last = (tc == NTC - 1 and bi == 1)
                    w = []
                    if tc == 0 and bi == 0:
                        w = [(s_se.h, se_cast[dc]), (s_ve.h, ve_sq)]
                    mm(bank[p:p + 1, :], onesb[:],
                       src[:, tc * 512:(tc + 1) * 512],
                       start=False, stop=(dc == ND - 1),
                       tp=(0, p), inc=last, waits=w)
            stat_done.append(s_mm.n)
        GP_WMX = s_gp.n
        VE_WMX = s_ve.n

        def ln_rows(eps):
            mmv = s_mm.n
            for tc in range(NTC):
                p = ROWP[tc]
                rt = rows if tc < 3 else sf[1]
                bs, ps = SITES[0][tc]
                bq, pq = SITES[1][tc]
                r_m = rt[p:p + 1, 0:512]
                r_1 = rt[p:p + 1, 512:1024]
                r_r = rt[p:p + 1, 1024:1536]
                vop(E_ts, out=r_m, in_=bs[ps:ps + 1, :], s1=LN_INV,
                    op0=ALU.mult, waits=[(s_mm.h, mmv)])
                vop(E_ts, out=r_1, in_=bq[pq:pq + 1, :], s1=LN_INV,
                    op0=ALU.mult)
                vop(E_tt, out=r_r, a=r_m, b=r_m, op=ALU.mult)
                vop(E_tt, out=r_1, a=r_1, b=r_r, op=ALU.subtract)
                sv = vop(E_ts, out=r_r, in_=r_1, s1=eps, op0=ALU.add)
                act(r_1, r_r, AF.Ln, waits=[(s_ve.h, sv)], inc=False)
                act(r_r, r_1, AF.Exp, scale=-0.5)
            for tc in range(NTC):
                p = ROWP[tc]
                rt = rows if tc < 3 else sf[1]
                mm(pC[:, :], onp[p:p + 1, 0:128], rt[p:p + 1, 0:512],
                   start=True, stop=True, inc=True,
                   waits=[(s_se.h, s_se.n), (s_ve.h, s_ve.n)])
                vop(E_copy, out=Mt[:, tc * 512:(tc + 1) * 512], in_=pC[:, :],
                    waits=[(s_mm.h, s_mm.n)])
                mm(pC[:, :], onp[p:p + 1, 0:128], rt[p:p + 1, 1024:1536],
                   start=True, stop=True, inc=True,
                   waits=[(s_ve.h, s_ve.n)])
                vop(E_copy, out=Rt[:, tc * 512:(tc + 1) * 512], in_=pC[:, :],
                    waits=[(s_mm.h, s_mm.n)])

        ln_rows(1e-5)
        VE_MR1 = s_ve.n

        ld2 = []
        ve_norm = []
        SE_P1A = s_se.n
        for dc in range(ND):
            sem = s_x[dc % 2]
            w = ([(s_ve.h, ve_norm[dc - 2])] if dc >= 2
                 else [(s_se.h, SE_P1A), (s_mm.h, s_mm.n)])
            ld2.append(dma("sync", sem, sf[dc % 2][:],
                           x_d[dc * 128:(dc + 1) * 128, :], w))
            w = [(sem.h, ld2[dc])]
            if dc == 0:
                w.append((s_ve.h, VE_MR1))
            vop(E_tt, out=sf[2][:], a=sf[dc % 2][:], b=Mt[:], op=ALU.subtract,
                waits=w, inc=False)
            ve_norm.append(vop(E_tt, out=pool_a[:, dc * T:(dc + 1) * T],
                               a=sf[2][:], b=Rt[:], op=ALU.mult))
        VE_XN = s_ve.n
        if "xn" in dbg_d:
            dma("sync", s_y[0], dbg_d["xn"][:, :], pool_a[:], [(s_ve.h, VE_XN)])

        def proj_phase(wname, bias_s, kind, tm_dest=None, first_mm_waits=(),
                       xn_ready=None):
            MM_ENTRY = s_mm.n
            ld_w = {}
            se_done = {}
            stt_done = {}
            ve_stage = {}
            ve_tm = {}
            mm_grp = {}
            tpos_mm = {}
            tpos_ve = {}
            kv_mm = {}
            kv_free = {}
            ks_done = {}

            def emit_wload(fc):
                w = []
                if fc >= 2:
                    w.append((s_mm.h, mm_grp[(fc - 2) * NTC + NTC - 1]))
                else:
                    w.append((s_mm.h, MM_ENTRY))
                ld_w[fc] = dma("gpsimd", s_w[fc % 2], wblk[fc % 2][:, 0:D],
                               w_d[wname][fc, :, :], w)

            def emit_groups_and_evacs(fc):
                stg = stage[fc % 2] if kind != "q" else None
                stage_ready = None
                for tc in range(NTC):
                    g = fc * NTC + tc
                    w = list(first_mm_waits) if g == 0 else []
                    w.append((s_w[fc % 2].h, ld_w[fc]))
                    if g >= 3:
                        w.append((s_se.h, se_done[g - 3]))
                    for dc in range(ND):
                        wd = list(w) if dc == 0 else []
                        if xn_ready is not None and g == 0:
                            wd.append((s_ve.h, xn_ready[dc]))
                        mm(pbank[g % 3][:, :],
                           wblk[fc % 2][:, dc * 128:(dc + 1) * 128],
                           pool_a[:, dc * T + tc * 512: dc * T + tc * 512 + 512],
                           start=(dc == 0), stop=(dc == ND - 1),
                           inc=(dc == ND - 1), waits=wd)
                    mm_grp[g] = s_mm.n
                    bank = pbank[g % 3]
                    sl = slice(tc * 512, (tc + 1) * 512)
                    w = [(s_mm.h, mm_grp[g])]
                    if tc == 0 and kind in ("v", "k") and fc >= 2:
                        w.append((s_mm.h, tpos_mm[fc - 2]))
                        if kind == "k":
                            w.append((s_ve.h, ks_done[fc - 2]))
                    if kind == "v":
                        se_done[g] = act(stg[:, sl], bank[:, :], AF.Identity,
                                         bias=bias_s[:, fc:fc + 1], waits=w)
                        stt_done[g] = s_ve.n
                        stage_ready = (s_se.h, se_done[g])
                    else:
                        if g >= NTC:
                            w.append((s_ve.h, stt_done[g - NTC]))
                        act(sf[2][:, sl], bank[:, :], AF.Exp,
                            bias=bias_s[:, fc:fc + 1], waits=w, inc=False)
                        se_done[g] = act(sf[3][:, sl], bank[:, :], AF.Relu,
                                         bias=bias_s[:, fc:fc + 1])
                        dst = (pool_b[:, fc * T + tc * 512: fc * T + tc * 512 + 512]
                               if kind == "q" else stg[:, sl])
                        stt_done[g] = vop(
                            E_stt, out=dst, in0=sf[2][:, sl], scalar=1.0,
                            in1=sf[3][:, sl], op0=ALU.min, op1=ALU.add,
                            waits=[(s_se.h, se_done[g])])
                        stage_ready = (s_ve.h, stt_done[g])
                ve_stage[fc] = stage_ready

            def emit_transposes(fc):
                stg = stage[fc % 2]
                for q4 in range(NQ4):
                    half = q4 % 2
                    gi = fc * NQ4 + q4
                    for j in range(4):
                        tt = q4 * 4 + j
                        w = []
                        if j == 0:
                            w.append(ve_stage[fc])
                            if gi >= 2:
                                w.append((s_ve.h, tpos_ve[gi - 2]))
                        tr(ptrb[:, half * 1024 + j * 128: half * 1024 + (j + 1) * 128],
                           stg[:, tt * 128:(tt + 1) * 128], idb[:],
                           inc=(j == 3), waits=w)
                    if kind == "v":
                        d3 = tm_dest[:].rearrange("p (tt d) -> p tt d", d=D)[
                            :, q4 * 4:q4 * 4 + 4, fc * 128:(fc + 1) * 128]
                        w = [(s_mm.h, s_mm.n)]
                    else:
                        d3 = kstp[fc % 2][:].rearrange("p (tt d) -> p tt d",
                                                       d=128)[:, q4 * 4:q4 * 4 + 4, :]
                        w = [(s_mm.h, s_mm.n)]
                        if fc >= 2 and q4 == 0:
                            w.append((s_mm.h, kv_mm[fc - 2]))
                    s3 = ptrb[:, half * 1024:half * 1024 + 512].rearrange(
                        "p (a b) -> p a b", b=128)
                    tpos_ve[gi] = vop(E_copy, out=d3, in_=s3, waits=w)
                tpos_mm[fc] = s_mm.n

            def emit_kv(fc):
                for tt in range(NTT):
                    w = []
                    if tt == 0:
                        w.append((s_ve.h, tpos_ve[fc * NQ4 + NQ4 - 1]))
                        if fc >= 1:
                            w.append((s_ve.h, kv_free[fc - 1]))
                        elif kind == "k":
                            w.append((s_ve.h, VE_RXV))
                    mm(pG[:, 0:128],
                       kstp[fc % 2][:, tt * 128:(tt + 1) * 128],
                       tm_dest[:].rearrange("p (tt d) -> p tt d", d=D)[
                           :, tt, fc * 128:(fc + 1) * 128],
                       start=(tt == 0), stop=(tt == NTT - 1),
                       inc=(tt == NTT - 1), waits=w)
                kv_mm[fc] = s_mm.n
                kv_free[fc] = vop(E_copy, out=kvf[:, fc * 128:(fc + 1) * 128],
                                  in_=pG[:, 0:128],
                                  waits=[(s_mm.h, kv_mm[fc])])
                for tc in range(NTC):
                    rv = vop(E_red, out=kst[:, tc:tc + 1],
                             in_=stage[fc % 2][:, tc * 512:(tc + 1) * 512],
                             op=ALU.add)
                    vop(E_tt, out=ksacc[:, fc:fc + 1], a=ksacc[:, fc:fc + 1],
                        b=kst[:, tc:tc + 1], op=ALU.add,
                        waits=[(s_ve.h, rv)], inc=(tc == NTC - 1))
                ks_done[fc] = s_ve.n

            def emit_tail(fc):
                if kind in ("v", "k"):
                    emit_transposes(fc)
                if kind == "k":
                    emit_kv(fc)

            for fc in range(ND):
                emit_wload(fc)
                emit_groups_and_evacs(fc)
                if fc > 0:
                    emit_tail(fc - 1)
            emit_tail(ND - 1)

        d1 = dma("gpsimd", s_cd,
                 kvin_d.ap()[0:128].rearrange("(p f) -> p f", f=1),
                 wmx[:, 0:1], [(s_ve.h, VE_WMX)])
        d2 = dma("gpsimd", s_cd,
                 rows[0:1, 0:128],
                 kvin_d.ap()[0:128].rearrange("(f p) -> f p", f=1),
                 [(s_cd.h, d1), (s_ve.h, VE_MR1), (s_mm.h, s_mm.n)])
        rv = vop(E_red, out=sc[0:1, 0:1], in_=rows[0:1, 0:128], op=ALU.max,
                 waits=[(s_cd.h, d2)])
        vop(E_ts, out=sc[0:1, 1:2], in_=sc[0:1, 0:1], s1=INV448,
            op0=ALU.mult, waits=[(s_ve.h, rv)])
        vop(E_ts, out=sc[0:1, 1:2], in_=sc[0:1, 1:2], s1=1e-12, op0=ALU.max)
        rv = vop(E_recip, out=sc[0:1, 2:3], in_=sc[0:1, 1:2])
        VE_WS = vop(E_ts, out=sc[0:1, 2:3], in_=sc[0:1, 2:3], s1=0.5,
                    op0=ALU.mult, waits=[(s_ve.h, rv)])
        mm(pG[:, 0:1], onp[0:1, 0:128], sc[0:1, 2:3], start=True, stop=True,
           inc=True, waits=[(s_ve.h, VE_WS)])
        VE_RXV = vop(E_copy, out=rxv[:], in_=pG[:, 0:1],
                     waits=[(s_mm.h, s_mm.n)])

        proj_phase("wv", vbs, "v", tm_dest=pool_b,
                   first_mm_waits=[(s_lc.h, LD_CONSTS), (s_gp.h, GP_WMX)],
                   xn_ready=ve_norm)
        if "vtm" in dbg_d:
            dma("sync", s_y[0], dbg_d["vtm"][:, :], pool_b[:], [(s_ve.h, s_ve.n)])

        proj_phase("wk", kbs, "k", tm_dest=pool_b)
        VE_KV = s_ve.n
        MM_KV = s_mm.n
        if "kvf" in dbg_d:
            dma("sync", s_y[0], dbg_d["kvf"][:, :], kvf, [(s_ve.h, VE_KV)])
        if "ks" in dbg_d:
            dma("sync", s_y[0], dbg_d["ks"][:, :], ksacc[:], [(s_ve.h, VE_KV)])

        dma("gpsimd", s_cd,
            kvin_d.ap()[0:128 * D].rearrange("(p f) -> p f", p=128),
            kvf, [(s_ve.h, VE_KV)])
        dma("gpsimd", s_cd,
            kvin_d.ap()[128 * D:128 * (D + ND)].rearrange("(p f) -> p f", p=128),
            ksacc[:, 0:ND], [])
        KV_IN = s_cd.n
        A("gpsimd", partial(E_cc, kind="AllReduce", op=ALU.add, groups=pairs,
                            ins=[kvin_d.ap().opt()], outs=[kvout_d.ap().opt()],
                            sem=s_cc.h, waits=((s_cd.h, KV_IN),)))
        s_cc.n += 1
        CC_KV = s_cc.n

        proj_phase("wq", qbs, "q")
        VE_QP = s_ve.n
        SE_Q = s_se.n
        if "qp" in dbg_d:
            dma("sync", s_y[0], dbg_d["qp"][:, :], pool_b[:], [(s_ve.h, VE_QP)])

        dma("gpsimd", s_cd, kvb,
            kvout_d.ap()[0:128 * D].rearrange("(p f) -> p f", p=128),
            [(s_cc.h, CC_KV)])
        dma("gpsimd", s_cd, ksb[:, 0:ND],
            kvout_d.ap()[128 * D:128 * (D + ND)].rearrange("(p f) -> p f", p=128),
            [])
        WD_KVB = s_cd.n

        L = 3
        NIT = ND * NTC
        wo_ld = {}
        se_ln = {}
        se_zr = {}
        ve_apply = {}
        zmm = {}
        zrmm = {}
        amm = {}

        def qslice(it):
            h, tc = divmod(it, NTC)
            return pool_b[:, h * T + tc * 512: h * T + tc * 512 + 512]

        def zrrow(it):
            j = it % 3
            return rows[0:1, j * 512:j * 512 + 512]

        for step in range(NIT + L):
            jt = step - L
            if 0 <= jt < NIT:
                h = jt // NTC
                w = [(s_se.h, se_ln[jt])]
                if jt >= 1:
                    w.append((s_se.h, se_zr[jt - 1]))
                zrmm[jt] = mm(pC[:, :], onp[0:1, 0:128], zrrow(jt),
                              start=True, stop=True, inc=True, waits=w)
                w = [(s_mm.h, zrmm[jt])]
                if jt >= 2:
                    w.append((s_ve.h, ve_apply[jt - 2]))
                se_zr[jt] = act(ZR[jt % 2], pC[:, :], AF.Exp, scale=-1.0,
                                waits=w)
                ab = (pA, pB)[jt % 2]
                w = []
                if jt >= 2:
                    w.append((s_ve.h, ve_apply[jt - 2]))
                amm[jt] = mm(ab[:, :], kvb[:, h * 128:(h + 1) * 128],
                             qslice(jt), start=True, stop=True, inc=True,
                             waits=w)
                ve_apply[jt] = vop(
                    E_tt,
                    out=pool_a[:, h * T + (jt % NTC) * 512:
                               h * T + (jt % NTC) * 512 + 512],
                    a=ab[:, :], b=ZR[jt % 2], op=ALU.mult,
                    waits=[(s_mm.h, amm[jt]), (s_se.h, se_zr[jt])])
                if jt % NTC == NTC - 1:
                    hh = jt // NTC
                    wo_ld[hh] = dma(
                        "gpsimd", s_w[hh % 2],
                        pool_b[:, hh * D:(hh + 1) * D].rearrange(
                            "p (g f) -> p g f", g=ND),
                        w_d["wo"][:, :, hh * 128:(hh + 1) * 128].rearrange(
                            "g p f -> p g f"),
                        [(s_mm.h, amm[jt]),
                         (s_w[hh % 2].h, s_w[hh % 2].n)])
            it = step
            if it < NIT:
                zb = (pD, pE)[it % 2]
                w = []
                if it == 0:
                    w = [(s_cd.h, WD_KVB), (s_ve.h, VE_QP), (s_se.h, SE_Q)]
                if it >= 2:
                    w.append((s_se.h, se_ln[it - 2]))
                zmm[it] = mm(zb[0:1, :], ksb[:, it // NTC:it // NTC + 1],
                             qslice(it), start=True, stop=True, inc=True,
                             waits=w)
                w = [(s_mm.h, zmm[it])]
                if it >= 3:
                    w.append((s_mm.h, zrmm[it - 3]))
                se_ln[it] = act(zrrow(it), zb[0:1, :], AF.Ln, waits=w)
        VE_ATTN = s_ve.n
        SE_P6 = s_se.n
        MM_P6 = s_mm.n
        if "attn" in dbg_d:
            dma("sync", s_y[0], dbg_d["attn"][:, :], pool_a[:], [(s_ve.h, VE_ATTN)])

        st_x1 = {}
        st_u = {}
        ld_xr = {}
        ld9 = {}
        se_sq = {}
        stt7 = {}
        statv = {}
        qfree9 = {}
        mr2v = {}

        VE_ZT2 = vop(E_memset, ap=zt, val=0.0,
                     waits=[(s_mm.h, MM_P6), (s_ve.h, VE_ATTN)])
        stat_zero_pass([(s_ve.h, VE_ZT2), (s_se.h, SE_P6)])

        def x1q(g):
            q = g % 8
            return sf[q // 4][:, (q % 4) * 512:(q % 4) * 512 + 512]

        def xrq(g):
            q = g % 4
            return wblkF[q // 2][:, (q % 2) * 512:(q % 2) * 512 + 512]

        def buf9(idx):
            q = idx % 8
            return sf[2 + q // 4][:, (q % 4) * 512:(q % 4) * 512 + 512]

        def stq(i):
            return stage[i // 4][:, (i % 4) * 512:(i % 4) * 512 + 512]

        def ln2_rows_tc(tc):
            p = ROWP[tc]
            if tc < 3:
                r_m = rows[p:p + 1, 0:512]
                r_1 = rows[p:p + 1, 512:1024]
                r_r = rows[p:p + 1, 1024:1536]
            else:
                r_m = kstpF[0][0:1, 0:512]
                r_1 = kstpF[0][0:1, 512:1024]
                r_r = kstpF[1][0:1, 0:512]
            bs, ps = SITES[0][tc]
            bq, pq = SITES[1][tc]
            vop(E_ts, out=r_m, in_=bs[ps:ps + 1, :], s1=LN_INV,
                op0=ALU.mult, waits=[(s_mm.h, statv[tc * ND + ND - 1])])
            vop(E_ts, out=r_1, in_=bq[pq:pq + 1, :], s1=LN_INV,
                op0=ALU.mult)
            vop(E_tt, out=r_r, a=r_m, b=r_m, op=ALU.mult)
            vop(E_tt, out=r_1, a=r_1, b=r_r, op=ALU.subtract)
            sv = vop(E_ts, out=r_r, in_=r_1, s1=1e-5, op0=ALU.add)
            act(r_1, r_r, AF.Ln, waits=[(s_ve.h, sv)], inc=False)
            act(r_r, r_1, AF.Exp, scale=-0.5)
            mm(ptrbF2, onp[p:p + 1, 0:128], r_m,
               start=True, stop=True, inc=True,
               waits=[(s_se.h, s_se.n), (s_ve.h, s_ve.n)])
            vop(E_copy, out=Mt[:, tc * 512:(tc + 1) * 512], in_=ptrbF2,
                waits=[(s_mm.h, s_mm.n)])
            mm(ptrbF2, onp[p:p + 1, 0:128], r_r,
               start=True, stop=True, inc=True,
               waits=[(s_ve.h, s_ve.n)])
            return vop(E_copy, out=Rt[:, tc * 512:(tc + 1) * 512],
                       in_=ptrbF2, waits=[(s_mm.h, s_mm.n)])

        def emit_p9_load(idx):
            tc9, gc9 = divmod(idx, ND)
            w = [(s_y[idx % 2].h, st_x1[idx])]
            if idx < 8:
                w += [(s_ve.h, VE_ATTN), (s_se.h, SE_P6)]
            if idx >= 2:
                w.append((s_f[idx % 2].h, ld9[idx - 2]))
            if idx >= 8:
                w += [(s_ve.h, qfree9[idx - 8]),
                      (s_u[(idx - 8) % 2].h, st_u[idx - 8])]
            ld9[idx] = dma("sync", s_f[idx % 2], buf9(idx),
                           x1_d.ap()[gc9 * 128:(gc9 + 1) * 128,
                                     tc9 * 512:(tc9 + 1) * 512], w)

        def emit_p9_chunk(idx, on_gp=False):
            tc9, gc9 = divmod(idx, ND)
            b = buf9(idx)
            tw = slice(tc9 * 512, (tc9 + 1) * 512)
            w = [(s_f[idx % 2].h, ld9[idx])]
            if gc9 == 0:
                w.append((s_ve.h, mr2v[tc9]))
            if on_gp:
                gop(E_tt, out=b, a=b, b=Mt[:, tw], op=ALU.subtract,
                    waits=w + [(s_ve.h, mr2v[tc9])])
                du = gop(E_tt, out=b, a=b, b=Rt[:, tw], op=ALU.mult)
                datw = [(s_gp.h, du)]
            else:
                vop(E_tt, out=b, a=b, b=Mt[:, tw], op=ALU.subtract,
                    waits=w, inc=False)
                du = vop(E_tt, out=b, a=b, b=Rt[:, tw], op=ALU.mult)
                datw = [(s_ve.h, du)]
            rv = vop(E_red, out=tmx[:, 0:1], in_=b, op=ALU.max, absv=True,
                     waits=datw)
            qfree9[idx] = vop(E_tt, out=amx[:], a=amx[:], b=tmx[:, 0:1],
                              op=ALU.max, waits=[(s_ve.h, rv)])
            st_u[idx] = dma("scalar", s_u[idx % 2],
                            xn2_d.ap()[gc9 * 128:(gc9 + 1) * 128, tw],
                            b, list(datw) + [(s_u[idx % 2].h,
                                              s_u[idx % 2].n)])

        def emit_stat(g):
            tcs, gcs = divmod(g, ND)
            pr = g % 3
            for bi in range(2):
                bank, p = SITES[bi][tcs]
                mm(bank[p:p + 1, :], onesb[:], stq(2 * pr + bi),
                   start=False, stop=(gcs == ND - 1), tp=(0, p),
                   inc=(bi == 1),
                   waits=([(s_se.h, se_sq[g])] if bi == 0 else []))
            statv[g] = s_mm.n
            if gcs == ND - 1:
                mr2v[tcs] = ln2_rows_tc(tcs)

        wo_last = (wo_ld[ND - 2], wo_ld[ND - 1])
        NG = NTC * ND
        for tc in range(NTC):
            tw = slice(tc * 512, (tc + 1) * 512)
            for gc in range(ND):
                g = tc * ND + gc
                w = [(s_x[g % 2].h, ld_xr[g - 2])] if g >= 2 else []
                if g >= 4:
                    w.append((s_ve.h, stt7[g - 4]))
                else:
                    w.append((s_mm.h, MM_P6))
                ld_xr[g] = dma("sync", s_x[g % 2], xrq(g),
                               x_d[gc * 128:(gc + 1) * 128, tw], w)
                w = []
                if g == 0:
                    w = [(s_ve.h, VE_ATTN), (s_se.h, SE_P6),
                         (s_w[0].h, wo_last[0]), (s_w[1].h, wo_last[1])]
                if g >= 3:
                    w.append((s_ve.h, stt7[g - 3]))
                for ec in range(ND):
                    mm(pbank[g % 3][:, :],
                       pool_b[:, ec * D + gc * 128: ec * D + gc * 128 + 128],
                       pool_a[:, ec * T + tc * 512: ec * T + tc * 512 + 512],
                       start=(ec == 0), stop=(ec == ND - 1),
                       inc=(ec == ND - 1), waits=(w if ec == 0 else []))
                mmg = s_mm.n
                if g >= 2:
                    emit_stat(g - 2)
                w = [(s_mm.h, mmg), (s_x[g % 2].h, ld_xr[g])]
                if g >= 8:
                    w += [(s_y[g % 2].h, st_x1[g - 8]),
                          (s_se.h, se_sq[g - 8])]
                stt7[g] = vop(E_stt, out=x1q(g), in0=pbank[g % 3][:, :],
                              scalar=obs[:, gc:gc + 1], in1=xrq(g),
                              op0=ALU.add, op1=ALU.add, waits=w)
                st_x1[g] = dma("scalar", s_y[g % 2],
                               x1_d.ap()[gc * 128:(gc + 1) * 128, tw],
                               x1q(g), [(s_ve.h, stt7[g]),
                                        (s_y[g % 2].h, s_y[g % 2].n)])
                pr = g % 3
                w = [(s_ve.h, stt7[g])]
                if g >= 3:
                    w.append((s_mm.h, statv[g - 3]))
                else:
                    w.append((s_mm.h, MM_P6))
                act(stq(2 * pr), x1q(g), AF.Copy, waits=w, inc=False)
                se_sq[g] = act(stq(2 * pr + 1), x1q(g), AF.Square)
                if g >= ND:
                    emit_p9_load(g - ND)
                if g >= ND + 2:
                    emit_p9_chunk(g - ND - 2)
        for g in (NG - 2, NG - 1):
            emit_stat(g)
        VE_P7 = s_ve.n
        SE_P7 = s_se.n
        MM_P7END = s_mm.n
        Y0_P7 = s_y[0].n
        Y1_P7 = s_y[1].n

        for idx in range(NG - ND - 2, NG):
            nxt = idx + 2
            if (NTC - 1) * ND <= nxt < NG:
                emit_p9_load(nxt)
            emit_p9_chunk(idx, on_gp=(idx >= (NTC - 1) * ND
                                      and idx % 2 == 1))
        VE_AMX = s_ve.n
        GP_AMX = s_gp.n
        U0_P9 = s_u[0].n
        U1_P9 = s_u[1].n

        ld_wf = {}
        se_wq = {}
        for gcw in range(ND):
            w = [(s_f[gcw % 2].h, s_f[gcw % 2].n)]
            if gcw < 2:
                w.append((s_ve.h, VE_P7))
            else:
                w.append((s_se.h, se_wq[gcw - 2]))
            ld_wf[gcw] = dma("sync", s_f[gcw % 2], wblk[gcw % 2][:, 0:D],
                             wf_d[gcw, :, :], w)
            w = [(s_f[gcw % 2].h, ld_wf[gcw])]
            if gcw == 0:
                w += [(s_mm.h, MM_P7END), (s_ve.h, VE_RXV)]
            se_wq[gcw] = act(wq8[:, gcw * D:(gcw + 1) * D],
                             wblk[gcw % 2][:, 0:D], AF.Identity,
                             scale=rxv[:, 0:1], waits=w)
        SE_WQ = s_se.n

        d3 = dma("gpsimd", s_cd,
                 kvin_d.ap()[0:128].rearrange("(p f) -> p f", f=1),
                 amx[:, 0:1], [(s_ve.h, VE_AMX), (s_gp.h, GP_AMX)])
        d4 = dma("gpsimd", s_cd,
                 rows[0:1, 0:128],
                 kvin_d.ap()[0:128].rearrange("(f p) -> f p", f=1),
                 [(s_cd.h, d3), (s_ve.h, s_ve.n)])
        VE_AM1 = vop(E_red, out=sc[0:1, 16:17], in_=rows[0:1, 0:128],
                     op=ALU.max, waits=[(s_cd.h, d4)])
        dma("gpsimd", s_cd, amin_d.ap()[0:8].rearrange("(a b) -> a b", a=1),
            sc[0:1, 16:24], [(s_ve.h, VE_AM1)])
        A("gpsimd", partial(E_cc, kind="AllReduce", op=ALU.max, groups=allg,
                            ins=[amin_d.ap().opt()], outs=[amout_d.ap().opt()],
                            sem=s_cc.h, waits=((s_cd.h, s_cd.n),)))
        s_cc.n += 1
        dma("gpsimd", s_cd, sc[0:1, 24:32],
            amout_d.ap()[0:8].rearrange("(a b) -> a b", a=1),
            [(s_cc.h, s_cc.n)])
        WD_AM = s_cd.n
        vop(E_ts, out=sc[0:1, 17:18], in_=sc[0:1, 24:25], s1=INV448,
            op0=ALU.mult, waits=[(s_cd.h, WD_AM)])
        vop(E_ts, out=sc[0:1, 17:18], in_=sc[0:1, 17:18], s1=1e-12,
            op0=ALU.max)
        rv = vop(E_recip, out=sc[0:1, 18:19], in_=sc[0:1, 17:18])
        vop(E_ts, out=sc[0:1, 18:19], in_=sc[0:1, 18:19], s1=0.5,
            op0=ALU.mult, waits=[(s_ve.h, rv)])
        vop(E_tt, out=sc[0:1, 19:20], a=sc[0:1, 17:18], b=sc[0:1, 1:2],
            op=ALU.mult)
        VE_SC = vop(E_ts, out=sc[0:1, 19:20], in_=sc[0:1, 19:20], s1=4.0,
                    op0=ALU.mult)
        if "sc" in dbg_d:
            dma("sync", s_y[0], dbg_d["sc"][:, :], sc[:], [(s_ve.h, VE_SC)])
        mm(pG[:, 0:1], onp[0:1, 0:128], sc[0:1, 18:19], start=True, stop=True,
           inc=True, waits=[(s_ve.h, VE_SC)])
        VE_RXQ = vop(E_copy, out=rxq[:], in_=pG[:, 0:1],
                     waits=[(s_mm.h, s_mm.n)])
        mm(pG[:, 0:1], onp[0:1, 0:128], sc[0:1, 19:20], start=True, stop=True,
           inc=True, waits=[(s_ve.h, s_ve.n)])
        VE_S4 = vop(E_copy, out=s4v[:], in_=pG[:, 0:1],
                    waits=[(s_mm.h, s_mm.n)])

        ld_p11 = {}
        ld_q = {}
        ve_q = {}
        npre = ND // 2
        for j in range(npre):
            w = [(s_u[j % 2].h, st_u[(NTC - 1) * ND + j]),
                 (s_p[j % 2].h, s_p[j % 2].n), (s_mm.h, MM_P7END)]
            ld_p11[j] = dma("sync", s_p[j % 2], pbf[:, j * T:(j + 1) * T],
                            xn2_d.ap()[j * 128:(j + 1) * 128, :], w)
        for j in range(npre):
            w = [(s_p[j % 2].h, ld_p11[j])]
            if j == 0:
                w.append((s_ve.h, VE_RXQ))
            ve_q[j] = vop(E_ts, out=xq8[:, j * T:(j + 1) * T],
                          in_=pbf[:, j * T:(j + 1) * T], s1=rxq[:, 0:1],
                          op0=ALU.mult, waits=w)
        for i in range(ND - npre + 2):
            if i >= 2:
                dcv = npre + i - 2
                ve_q[dcv] = vop(E_ts, out=xq8[:, dcv * T:(dcv + 1) * T],
                                in_=sf[dcv % 2][:], s1=rxq[:, 0:1],
                                op0=ALU.mult,
                                waits=[(s_x[dcv % 2].h, ld_q[dcv])])
            dcq = npre + i
            if dcq <= ND - 1:
                w = [(s_u[dcq % 2].h, st_u[(NTC - 1) * ND + dcq]),
                     (s_x[dcq % 2].h, s_x[dcq % 2].n)]
                if dcq < npre + 2:
                    w += [(s_se.h, SE_P7), (s_y[0].h, Y0_P7),
                          (s_y[1].h, Y1_P7)]
                else:
                    w.append((s_ve.h, ve_q[dcq - 2]))
                ld_q[dcq] = dma("sync", s_x[dcq % 2], sf[dcq % 2][:],
                                xn2_d.ap()[dcq * 128:(dcq + 1) * 128, :], w)
        VE_XQ = s_ve.n
        if "xq" in dbg_d:
            dma("sync", s_y[0], dbg_d["xq"][:, :], xq8[:], [(s_ve.h, VE_XQ)])

        ld_r2 = {}
        ve_out = {}
        st_y = {}
        v12 = {}
        nrr = min(2, ND // 2)
        for gc in range(ND):
            w = [(s_x[gc % 2].h, s_x[gc % 2].n)]
            if gc >= nrr:
                w.append((s_ve.h, ve_out[gc - nrr]))
            else:
                w.append((s_ve.h, ve_q[gc]))
            ld_r2[gc] = dma("sync", s_x[gc % 2],
                            pbf[:, (gc % nrr) * T:(gc % nrr) * T + T],
                            x1_d.ap()[gc * 128:(gc + 1) * 128, :], w)
            for tc in range(NTC):
                g = gc * NTC + tc
                w = []
                if g == 0:
                    w = [(s_ve.h, VE_XQ), (s_se.h, SE_WQ)]
                if g >= 3:
                    w.append((s_ve.h, v12[g - 3]))
                for dc in range(ND // 2):
                    lhsT = wq8[:, gc * D + dc * 256:
                               gc * D + dc * 256 + 256].rearrange(
                        "p (j c) -> p j c", j=2)
                    rhs = xq8[:, dc * 2 * T:(dc * 2 + 2) * T].rearrange(
                        "p (j t) -> p j t", j=2)[:, :, tc * 512:(tc + 1) * 512]
                    mm(pbank[g % 3][:, :], lhsT, rhs,
                       start=(dc == 0), stop=(dc == ND // 2 - 1),
                       inc=(dc == ND // 2 - 1), pm=PM_DR,
                       waits=(w if dc == 0 else []))
                mg = s_mm.n
                sl = slice(tc * 512, (tc + 1) * 512)
                vop(E_ts, out=fb, in_=pbank[g % 3][:, :], s1=s4v[:, 0:1],
                    op0=ALU.mult, waits=[(s_mm.h, mg)], inc=False)
                w = [(s_x[gc % 2].h, ld_r2[gc])]
                if gc >= 2 and tc == 0:
                    w.append((s_y[gc % 2].h, st_y[gc - 2]))
                if gc < 2 and tc == 0:
                    w += [(s_u[0].h, U0_P9), (s_u[1].h, U1_P9)]
                v12[g] = vop(E_tt, out=sf[2 + gc % 2][:, sl],
                             a=pbf[:, (gc % nrr) * T + tc * 512:
                                   (gc % nrr) * T + (tc + 1) * 512],
                             b=fb, op=ALU.add, waits=w)
            ve_out[gc] = s_ve.n
            st_y[gc] = dma("scalar", s_y[gc % 2],
                           y_d[gc * 128:(gc + 1) * 128, :],
                           sf[2 + gc % 2][:],
                           [(s_ve.h, ve_out[gc]),
                            (s_y[gc % 2].h, s_y[gc % 2].n)])

        A("sync", partial(E_wait, waits=((s_y[0].h, s_y[0].n),
                                         (s_y[1].h, s_y[1].n))))

        @block.sync
        def _(eng):
            for f in prog["sync"]:
                f(eng)

        @block.gpsimd
        def _(eng):
            for f in prog["gpsimd"]:
                f(eng)

        @block.tensor
        def _(eng):
            for f in prog["tensor"]:
                f(eng)

        @block.vector
        def _(eng):
            for f in prog["vector"]:
                f(eng)

        @block.scalar
        def _(eng):
            for f in prog["scalar"]:
                f(eng)

    return nc


def _wblocks(wt, dt):
    d_in, d_out = wt.shape
    a = np.ascontiguousarray(
        wt.reshape(d_in // 128, 128, d_out // 128, 128).transpose(2, 1, 0, 3)
    ).reshape(d_out // 128, 128, d_in)
    return np.ascontiguousarray(a.astype(dt, copy=False))


def kernel(L0_attn_hidden_states, L0_attn_q_w, L0_attn_q_b, L0_attn_k_w,
           L0_attn_k_b, L0_attn_v_w, L0_attn_v_b, L0_attn_o_w, L0_attn_o_b,
           L0_ffn_weight):
    x = np.asarray(L0_attn_hidden_states)
    B, S, D = x.shape
    ncores = 8
    Tc = (B * S) // ncores
    ND = D // 128

    nc = build_nc(D=D, T=Tc, ncores=ncores)

    wq = _wblocks(np.ascontiguousarray(np.asarray(L0_attn_q_w).T), np.float32)
    wk = _wblocks(np.ascontiguousarray(np.asarray(L0_attn_k_w).T), np.float32)
    wv = _wblocks(np.ascontiguousarray(np.asarray(L0_attn_v_w).T), np.float32)
    wo = _wblocks(np.ascontiguousarray(np.asarray(L0_attn_o_w).T), np.float32)
    wf = _wblocks(np.ascontiguousarray(np.asarray(L0_ffn_weight).T),
                  ml_dtypes.bfloat16)
    qb = np.ascontiguousarray(np.asarray(L0_attn_q_b, np.float32).reshape(ND, 128).T)
    kb = np.ascontiguousarray(np.asarray(L0_attn_k_b, np.float32).reshape(ND, 128).T)
    vb = np.ascontiguousarray(np.asarray(L0_attn_v_b, np.float32).reshape(ND, 128).T)
    ob = np.ascontiguousarray(np.asarray(L0_attn_o_b, np.float32).reshape(ND, 128).T)
    onesb = np.ones((128, 1), ml_dtypes.bfloat16)
    onp_ = np.ones((128, 128), np.float32)
    idb = np.eye(128, dtype=ml_dtypes.bfloat16)

    halves = S // 2
    in_maps = []
    for c in range(ncores):
        b, h = c >> 1, c & 1
        slab = np.asarray(x[b, h * halves:(h + 1) * halves, :], np.float32)
        x_fm = np.ascontiguousarray(slab.T)
        in_maps.append({
            "x": x_fm, "wq": wq, "wk": wk, "wv": wv, "wo": wo, "wf": wf,
            "qb": qb, "kb": kb, "vb": vb, "ob": ob, "onesb": onesb,
            "onp": onp_, "idb": idb,
        })

    res = run_bass_kernel_spmd(nc, in_maps, core_ids=list(range(ncores)))

    out = np.empty((B, S, D), np.float32)
    for c in range(ncores):
        b, h = c >> 1, c & 1
        out[b, h * halves:(h + 1) * halves, :] = res.results[c]["y"].T
    return out


# version 18
# speedup vs baseline: 1.1809x; 1.0026x over previous
import contextlib
from functools import partial

import numpy as np
import ml_dtypes

import concourse.bass as bass
import concourse.mybir as mybir
from concourse.bass_utils import run_bass_kernel_spmd

F32 = mybir.dt.float32
BF16 = mybir.dt.bfloat16
FP8 = mybir.dt.float8e4
AF = mybir.ActivationFunctionType
PM_DR = mybir.MatmulPerfMode.DoubleRow
ALU = mybir.AluOpType
AX = mybir.AxisListType

ENGINES = ("sync", "gpsimd", "tensor", "vector", "scalar")

DBG = []
STRICT = [False]


class Sem:
    __slots__ = ("h", "n")

    def __init__(self, h):
        self.h = h
        self.n = 0


def _w(e, waits):
    for (h, v) in waits:
        e.wait_ge(h, v)


def E_dma(e, out, in_, sem, waits=()):
    _w(e, waits)
    e.dma_start(out=out, in_=in_).then_inc(sem, 16)


def E_mm(e, out, l, r, start, stop, sem=None, tp=None, pm=None, waits=()):
    _w(e, waits)
    kw = {}
    if tp is not None:
        kw["tile_position"] = tp
    if pm is not None:
        kw["perf_mode"] = pm
    i = e.matmul(out, l, r, start=start, stop=stop, **kw)
    if sem is not None:
        i.then_inc(sem, 1)


def E_tr(e, out, in_, ident, sem=None, waits=()):
    _w(e, waits)
    i = e.transpose(out, in_, ident)
    if sem is not None:
        i.then_inc(sem, 1)


def E_act(e, out, in_, func, bias=None, scale=None, sem=None, waits=()):
    _w(e, waits)
    kw = {}
    if bias is not None:
        kw["bias"] = bias
    if scale is not None:
        kw["scale"] = scale
    i = e.activation(out, in_, func, **kw)
    if sem is not None:
        i.then_inc(sem, 1)


def E_copy(e, out, in_, sem=None, waits=()):
    _w(e, waits)
    i = e.tensor_copy(out=out, in_=in_)
    if sem is not None:
        i.then_inc(sem, 1)


def E_tt(e, out, a, b, op, sem=None, waits=()):
    _w(e, waits)
    i = e.tensor_tensor(out=out, in0=a, in1=b, op=op)
    if sem is not None:
        i.then_inc(sem, 1)


def E_ts(e, out, in_, s1, op0, sem=None, waits=()):
    _w(e, waits)
    i = e.tensor_scalar(out=out, in0=in_, scalar1=s1, scalar2=None, op0=op0)
    if sem is not None:
        i.then_inc(sem, 1)


def E_stt(e, out, in0, scalar, in1, op0, op1, sem=None, waits=()):
    _w(e, waits)
    i = e.scalar_tensor_tensor(out=out, in0=in0, scalar=scalar, in1=in1,
                               op0=op0, op1=op1)
    if sem is not None:
        i.then_inc(sem, 1)


def E_red(e, out, in_, op, absv=False, sem=None, waits=()):
    _w(e, waits)
    i = e.tensor_reduce(out=out, in_=in_, axis=AX.X, op=op,
                        apply_absolute_value=absv)
    if sem is not None:
        i.then_inc(sem, 1)


def E_recip(e, out, in_, sem=None, waits=()):
    _w(e, waits)
    i = e.reciprocal(out=out, in_=in_)
    if sem is not None:
        i.then_inc(sem, 1)


def E_memset(e, ap, val, sem=None, waits=()):
    _w(e, waits)
    i = e.memset(ap, val)
    if sem is not None and i is not None:
        i.then_inc(sem, 1)


def E_cc(e, kind, op, groups, ins, outs, sem, waits=()):
    _w(e, waits)
    e.collective_compute(kind, op, replica_groups=groups, ins=ins,
                         outs=outs).then_inc(sem, 1)


def E_wait(e, waits):
    _w(e, waits)


def build_nc(D=2048, T=2048, ncores=8):
    ND = D // 128
    NTC = T // 512
    NTT = T // 128
    NQ4 = NTT // 4
    assert NTC <= 4
    LN_INV = 1.0 / D
    INV448 = float(np.float32(1.0) / np.float32(448.0))

    nc = bass.Bass("TRN2", target_bir_lowering=False)

    x_d = nc.dram_tensor("x", [D, T], F32, kind="ExternalInput").ap()
    w_d = {}
    for w in ("wq", "wk", "wv", "wo"):
        w_d[w] = nc.dram_tensor(w, [ND, 128, D], F32, kind="ExternalInput").ap()
    wf_d = nc.dram_tensor("wf", [ND, 128, D], BF16, kind="ExternalInput").ap()
    qb_d = nc.dram_tensor("qb", [128, ND], F32, kind="ExternalInput").ap()
    kb_d = nc.dram_tensor("kb", [128, ND], F32, kind="ExternalInput").ap()
    vb_d = nc.dram_tensor("vb", [128, ND], F32, kind="ExternalInput").ap()
    ob_d = nc.dram_tensor("ob", [128, ND], F32, kind="ExternalInput").ap()
    ones_d = nc.dram_tensor("onesb", [128, 1], BF16, kind="ExternalInput").ap()
    onp_d = nc.dram_tensor("onp", [128, 128], F32, kind="ExternalInput").ap()
    idb_d = nc.dram_tensor("idb", [128, 128], BF16, kind="ExternalInput").ap()

    y_d = nc.dram_tensor("y", [D, T], F32, kind="ExternalOutput").ap()

    x1_d = nc.dram_tensor("x1s", [D, T], F32)
    xn2_d = nc.dram_tensor("xn2s", [D, T], F32)
    kvin_d = nc.dram_tensor("kvin", [128 * (D + ND)], F32)
    kvout_d = nc.dram_tensor("kvout", [128 * (D + ND)], F32)
    amin_d = nc.dram_tensor("amin", [8], F32)
    amout_d = nc.dram_tensor("amout", [8], F32)

    dbg_d = {}
    for name, shape, dt in (
        ("xn", [128, ND * T], BF16), ("vtm", [128, ND * T], BF16),
        ("kvf", [128, D], F32), ("ks", [128, ND], F32),
        ("qp", [128, ND * T], BF16), ("attn", [128, ND * T], BF16),
        ("rows", [128, 1536], F32), ("sc", [1, 32], F32),
        ("xq", [128, ND * T], FP8),
    ):
        if name in DBG:
            dbg_d[name] = nc.dram_tensor("dbg_" + name, shape, dt,
                                         kind="ExternalOutput").ap()

    pairs = [[2 * i, 2 * i + 1] for i in range(max(1, ncores // 2))]
    allg = [list(range(ncores))]

    prog = {k: [] for k in ENGINES}

    with contextlib.ExitStack() as ctx:
        ent = ctx.enter_context
        pool_a = ent(nc.sbuf_tensor("pool_a", [128, ND * T], BF16))
        pool_b = ent(nc.sbuf_tensor("pool_b", [128, ND * T], BF16))
        Mt = ent(nc.sbuf_tensor("Mt", [128, T], F32))
        Rt = ent(nc.sbuf_tensor("Rt", [128, T], F32))
        sf = [ent(nc.sbuf_tensor(f"sf{i}", [128, T], F32)) for i in range(4)]
        wblk = [ent(nc.sbuf_tensor(f"wblk{i}", [128, max(D, 2048)], BF16))
                for i in range(2)]
        kstp = [ent(nc.sbuf_tensor(f"kstp{i}", [128, T], BF16)) for i in range(2)]
        stage = [ent(nc.sbuf_tensor(f"stage{i}", [128, T], BF16)) for i in range(2)]
        rows = ent(nc.sbuf_tensor("rows", [128, 1536], F32))
        sc = ent(nc.sbuf_tensor("sc", [1, 32], F32))
        ksacc = ent(nc.sbuf_tensor("ksacc", [128, ND], F32))
        kst = ent(nc.sbuf_tensor("kst", [128, 4], F32))
        ksb = ent(nc.sbuf_tensor("ksb", [128, ND], BF16))

        amx = ent(nc.sbuf_tensor("amx", [128, 1], F32))
        wmx = ent(nc.sbuf_tensor("wmx", [128, 1], F32))
        tmx = ent(nc.sbuf_tensor("tmx", [128, 1], F32))
        rxv = ent(nc.sbuf_tensor("rxv", [128, 1], F32))
        s4v = ent(nc.sbuf_tensor("s4v", [128, 1], F32))
        rxq = ent(nc.sbuf_tensor("rxq", [128, 1], F32))
        qbs = ent(nc.sbuf_tensor("qbs", [128, ND], F32))
        kbs = ent(nc.sbuf_tensor("kbs", [128, ND], F32))
        vbs = ent(nc.sbuf_tensor("vbs", [128, ND], F32))
        obs = ent(nc.sbuf_tensor("obs", [128, ND], F32))
        onesb = ent(nc.sbuf_tensor("onesbs", [128, 1], BF16))
        onp = ent(nc.sbuf_tensor("onps", [128, 128], F32))
        idb = ent(nc.sbuf_tensor("idbs", [128, 128], BF16))

        xq8 = pool_a[:, 0:ND * T // 2].bitcast(FP8)
        kvf = sf[0][:, 0:D]
        paf = pool_a[:].bitcast(F32)
        pbf = pool_b[:].bitcast(F32)
        fb = kstp[0][:, 0:512]
        ztA = pool_a[:, 0:512]
        zt = stage[1][:, 1536:2048]
        kvb = stage[0][:, 0:D]
        ZR = [kstp[i][:, 0:1024].bitcast(F32) for i in range(2)]
        wq8 = pool_a[:, ND * T // 2:ND * T].bitcast(FP8)
        wblkF = [wblk[i][:].bitcast(F32) for i in range(2)]
        kstpF = [kstp[i][:].bitcast(F32) for i in range(2)]

        pA = ent(nc.psum_tensor("pA", [128, 512], F32))
        pB = ent(nc.psum_tensor("pB", [128, 512], F32))
        pC = ent(nc.psum_tensor("pC", [128, 512], F32))
        pD = ent(nc.psum_tensor("pD", [128, 512], F32))
        pE = ent(nc.psum_tensor("pE", [128, 512], F32))
        ptrb = ent(nc.psum_tensor("ptrb", [128, 2048], BF16))
        pG = ent(nc.psum_tensor("pG", [128, 512], F32))
        pbank = [pA, pB, pC]

        s_lc = Sem(ent(nc.semaphore("s_lc")))
        s_x = [Sem(ent(nc.semaphore(f"s_x{i}"))) for i in range(2)]
        s_f = [Sem(ent(nc.semaphore(f"s_f{i}"))) for i in range(2)]
        s_y = [Sem(ent(nc.semaphore(f"s_y{i}"))) for i in range(2)]
        s_w = [Sem(ent(nc.semaphore(f"s_w{i}"))) for i in range(2)]
        s_u = [Sem(ent(nc.semaphore(f"s_u{i}"))) for i in range(2)]
        s_p = [Sem(ent(nc.semaphore(f"s_p{i}"))) for i in range(2)]
        s_cd = Sem(ent(nc.semaphore("s_cd")))
        s_gp = Sem(ent(nc.semaphore("s_gp")))
        s_cc = Sem(ent(nc.semaphore("s_cc")))
        s_mm = Sem(ent(nc.semaphore("s_mm")))
        s_se = Sem(ent(nc.semaphore("s_se")))
        s_ve = Sem(ent(nc.semaphore("s_ve")))

        block = ent(nc.Block())

        def A(eng, fn):
            prog[eng].append(fn)

        def dma(qname, sem, out, in_, waits=()):
            A(qname, partial(E_dma, out=out, in_=in_, sem=sem.h,
                             waits=tuple(waits)))
            sem.n += 16
            return sem.n

        def mm(out, l, r, start, stop, inc=False, tp=None, pm=None, waits=()):
            if STRICT[0]:
                waits = list(waits) + [(s_mm.h, s_mm.n)]
                inc = True
            A("tensor", partial(E_mm, out=out, l=l, r=r, start=start, stop=stop,
                                sem=(s_mm.h if inc else None), tp=tp, pm=pm,
                                waits=tuple(waits)))
            if inc:
                s_mm.n += 1
            return s_mm.n

        def tr(out, in_, ident, inc=True, waits=()):
            if STRICT[0]:
                waits = list(waits) + [(s_mm.h, s_mm.n)]
                inc = True
            A("tensor", partial(E_tr, out=out, in_=in_, ident=ident,
                                sem=(s_mm.h if inc else None),
                                waits=tuple(waits)))
            if inc:
                s_mm.n += 1
            return s_mm.n

        def act(out, in_, func, bias=None, scale=None, inc=True, waits=()):
            waits = list(waits) + [(s_se.h, s_se.n)]
            inc = True
            A("scalar", partial(E_act, out=out, in_=in_, func=func, bias=bias,
                                scale=scale, sem=(s_se.h if inc else None),
                                waits=tuple(waits)))
            if inc:
                s_se.n += 1
            return s_se.n

        def gop(fn, waits=(), **kw):
            A("gpsimd", partial(fn, sem=s_gp.h,
                                waits=tuple(list(waits) + [(s_gp.h, s_gp.n)]),
                                **kw))
            s_gp.n += 1
            return s_gp.n

        def vop(fn, inc=True, waits=(), **kw):
            waits = list(waits) + [(s_ve.h, s_ve.n)]
            inc = True
            A("vector", partial(fn, sem=(s_ve.h if inc else None),
                                waits=tuple(waits), **kw))
            if inc:
                s_ve.n += 1
            return s_ve.n

        for (dst, src) in ((qbs, qb_d), (kbs, kb_d), (vbs, vb_d), (obs, ob_d),
                           (onesb, ones_d), (onp, onp_d), (idb, idb_d)):
            dma("sync", s_lc, dst[:], src[:, :])
        LD_CONSTS = s_lc.n
        vop(E_memset, ap=ztA, val=0.0)
        vop(E_memset, ap=sc[:], val=0.0)
        vop(E_memset, ap=ksacc[:], val=0.0)
        vop(E_memset, ap=amx[:], val=0.0)
        vop(E_memset, ap=wmx[:], val=0.0)

        ptrbF = ptrb[:, 0:1024].bitcast(F32)
        ptrbF2 = ptrb[:, 1024:2048].bitcast(F32)
        SITES = {
            0: [(pD[:, :], 0), (pD[:, :], 32), (pD[:, :], 64), (pG[:, :], 0)],
            1: [(pE[:, :], 0), (pE[:, :], 32), (pE[:, :], 64), (ptrbF, 0)],
        }
        ROWP = [0, 32, 64, 0]

        def stat_zero_pass(waits, z=None):
            first = True
            for bi in (0, 1):
                for tc in range(NTC):
                    bank, p = SITES[bi][tc]
                    mm(bank[p:p + 1, :], onesb[:],
                       (z if z is not None else zt),
                       start=True, stop=False, tp=(0, p),
                       waits=(waits if first else []))
                    first = False

        ld_x = []
        se_cast = []
        stat_done = []
        stat_zero_pass([(s_lc.h, LD_CONSTS), (s_ve.h, s_ve.n)], z=ztA)
        for dc in range(ND):
            sem = s_x[dc % 2]
            w = [(s_se.h, se_cast[dc - 2])] if dc >= 2 else []
            ld_x.append(dma("sync", sem, sf[dc % 2][:],
                            x_d[dc * 128:(dc + 1) * 128, :], w))
            w = [(sem.h, ld_x[dc])]
            if dc >= 2:
                w.append((s_mm.h, stat_done[dc - 2]))
            se_cast.append(act(stage[dc % 2][:], sf[dc % 2][:], AF.Copy,
                               waits=w))
            wv = [(s_se.h, se_cast[dc])]
            if dc >= 2:
                wv.append((s_mm.h, stat_done[dc - 2]))
            ve_sq = vop(E_tt, out=kstp[dc % 2][:], a=stage[dc % 2][:],
                        b=stage[dc % 2][:], op=ALU.mult, waits=wv)
            for tc in range(NTC):
                for bi, src in enumerate((stage[dc % 2], kstp[dc % 2])):
                    bank, p = SITES[bi][tc]
                    last = (tc == NTC - 1 and bi == 1)
                    w = []
                    if tc == 0 and bi == 0:
                        w = [(s_se.h, se_cast[dc]), (s_ve.h, ve_sq)]
                    mm(bank[p:p + 1, :], onesb[:],
                       src[:, tc * 512:(tc + 1) * 512],
                       start=False, stop=(dc == ND - 1),
                       tp=(0, p), inc=last, waits=w)
            stat_done.append(s_mm.n)
        GP_WMX = s_gp.n

        def ln_rows(eps):
            mmv = s_mm.n
            for tc in range(NTC):
                p = ROWP[tc]
                rt = rows if tc < 3 else sf[1]
                bs, ps = SITES[0][tc]
                bq, pq = SITES[1][tc]
                r_m = rt[p:p + 1, 0:512]
                r_1 = rt[p:p + 1, 512:1024]
                r_r = rt[p:p + 1, 1024:1536]
                vop(E_ts, out=r_m, in_=bs[ps:ps + 1, :], s1=LN_INV,
                    op0=ALU.mult, waits=[(s_mm.h, mmv)])
                vop(E_ts, out=r_1, in_=bq[pq:pq + 1, :], s1=LN_INV,
                    op0=ALU.mult)
                vop(E_tt, out=r_r, a=r_m, b=r_m, op=ALU.mult)
                vop(E_tt, out=r_1, a=r_1, b=r_r, op=ALU.subtract)
                sv = vop(E_ts, out=r_r, in_=r_1, s1=eps, op0=ALU.add)
                act(r_1, r_r, AF.Ln, waits=[(s_ve.h, sv)], inc=False)
                act(r_r, r_1, AF.Exp, scale=-0.5)
            for tc in range(NTC):
                p = ROWP[tc]
                rt = rows if tc < 3 else sf[1]
                mm(pC[:, :], onp[p:p + 1, 0:128], rt[p:p + 1, 0:512],
                   start=True, stop=True, inc=True,
                   waits=[(s_se.h, s_se.n), (s_ve.h, s_ve.n)])
                vop(E_copy, out=Mt[:, tc * 512:(tc + 1) * 512], in_=pC[:, :],
                    waits=[(s_mm.h, s_mm.n)])
                mm(pC[:, :], onp[p:p + 1, 0:128], rt[p:p + 1, 1024:1536],
                   start=True, stop=True, inc=True,
                   waits=[(s_ve.h, s_ve.n)])
                vop(E_copy, out=Rt[:, tc * 512:(tc + 1) * 512], in_=pC[:, :],
                    waits=[(s_mm.h, s_mm.n)])

        ln_rows(1e-5)
        VE_MR1 = s_ve.n
        MM_R1 = s_mm.n

        ld2 = []
        ve_norm = []
        SE_P1A = s_se.n
        GPC = set(range(2, ND, 3))
        for dc in range(ND):
            sem = s_x[dc % 2]
            w = ([ve_norm[dc - 2]] if dc >= 2
                 else [(s_se.h, SE_P1A), (s_mm.h, s_mm.n)])
            ld2.append(dma("sync", sem, sf[dc % 2][:],
                           x_d[dc * 128:(dc + 1) * 128, :], w))
            w = [(sem.h, ld2[dc])]
            if dc in GPC:
                gop(E_tt, out=sf[3][:], a=sf[dc % 2][:], b=Mt[:],
                    op=ALU.subtract, waits=w + [(s_ve.h, VE_MR1)])
                gv = gop(E_tt, out=pool_a[:, dc * T:(dc + 1) * T],
                         a=sf[3][:], b=Rt[:], op=ALU.mult)
                ve_norm.append((s_gp.h, gv))
            else:
                if dc == 0:
                    w.append((s_ve.h, VE_MR1))
                vop(E_tt, out=sf[2][:], a=sf[dc % 2][:], b=Mt[:],
                    op=ALU.subtract, waits=w, inc=False)
                ve_norm.append((s_ve.h,
                                vop(E_tt,
                                    out=pool_a[:, dc * T:(dc + 1) * T],
                                    a=sf[2][:], b=Rt[:], op=ALU.mult)))
        VE_XN = s_ve.n
        GP_XN = s_gp.n
        if "xn" in dbg_d:
            dma("sync", s_y[0], dbg_d["xn"][:, :], pool_a[:], [(s_ve.h, VE_XN)])

        def proj_phase(wname, bias_s, kind, tm_dest=None, first_mm_waits=(),
                       xn_ready=None, extra_fc=None):
            MM_ENTRY = s_mm.n
            ld_w = {}
            se_done = {}
            stt_done = {}
            ve_stage = {}
            ve_tm = {}
            mm_grp = {}
            tpos_mm = {}
            tpos_ve = {}
            kv_mm = {}
            kv_free = {}
            ks_done = {}

            def emit_wload(fc):
                w = []
                if fc >= 2:
                    w.append((s_mm.h, mm_grp[(fc - 2) * NTC + NTC - 1]))
                else:
                    w.append((s_mm.h, MM_ENTRY))
                ld_w[fc] = dma("gpsimd", s_w[fc % 2], wblk[fc % 2][:, 0:D],
                               w_d[wname][fc, :, :], w)

            def emit_groups_and_evacs(fc):
                stg = stage[fc % 2] if kind != "q" else None
                stage_ready = None
                for tc in range(NTC):
                    g = fc * NTC + tc
                    w = list(first_mm_waits) if g == 0 else []
                    w.append((s_w[fc % 2].h, ld_w[fc]))
                    if g >= 3:
                        w.append((s_se.h, se_done[g - 3]))
                    for dc in range(ND):
                        wd = list(w) if dc == 0 else []
                        if xn_ready is not None and g == 0:
                            wd.append(xn_ready[dc])
                        mm(pbank[g % 3][:, :],
                           wblk[fc % 2][:, dc * 128:(dc + 1) * 128],
                           pool_a[:, dc * T + tc * 512: dc * T + tc * 512 + 512],
                           start=(dc == 0), stop=(dc == ND - 1),
                           inc=(dc == ND - 1), waits=wd)
                    mm_grp[g] = s_mm.n
                    bank = pbank[g % 3]
                    sl = slice(tc * 512, (tc + 1) * 512)
                    w = [(s_mm.h, mm_grp[g])]
                    if tc == 0 and kind in ("v", "k") and fc >= 2:
                        w.append((s_mm.h, tpos_mm[fc - 2]))
                        if kind == "k":
                            w.append((s_ve.h, ks_done[fc - 2]))
                    if kind == "v":
                        se_done[g] = act(stg[:, sl], bank[:, :], AF.Identity,
                                         bias=bias_s[:, fc:fc + 1], waits=w)
                        stt_done[g] = s_ve.n
                        stage_ready = (s_se.h, se_done[g])
                    else:
                        if g >= NTC:
                            w.append((s_ve.h, stt_done[g - NTC]))
                        act(sf[2][:, sl], bank[:, :], AF.Exp,
                            bias=bias_s[:, fc:fc + 1], waits=w, inc=False)
                        se_done[g] = act(sf[3][:, sl], bank[:, :], AF.Relu,
                                         bias=bias_s[:, fc:fc + 1])
                        dst = (pool_b[:, fc * T + tc * 512: fc * T + tc * 512 + 512]
                               if kind == "q" else stg[:, sl])
                        stt_done[g] = vop(
                            E_stt, out=dst, in0=sf[2][:, sl], scalar=1.0,
                            in1=sf[3][:, sl], op0=ALU.min, op1=ALU.add,
                            waits=[(s_se.h, se_done[g])])
                        stage_ready = (s_ve.h, stt_done[g])
                ve_stage[fc] = stage_ready

            def emit_transposes(fc):
                stg = stage[fc % 2]
                for q4 in range(NQ4):
                    half = q4 % 2
                    gi = fc * NQ4 + q4
                    for j in range(4):
                        tt = q4 * 4 + j
                        w = []
                        if j == 0:
                            w.append(ve_stage[fc])
                            if gi >= 2:
                                w.append((s_ve.h, tpos_ve[gi - 2]))
                        tr(ptrb[:, half * 1024 + j * 128: half * 1024 + (j + 1) * 128],
                           stg[:, tt * 128:(tt + 1) * 128], idb[:],
                           inc=(j == 3), waits=w)
                    if kind == "v":
                        d3 = tm_dest[:].rearrange("p (tt d) -> p tt d", d=D)[
                            :, q4 * 4:q4 * 4 + 4, fc * 128:(fc + 1) * 128]
                        w = [(s_mm.h, s_mm.n)]
                    else:
                        d3 = kstp[fc % 2][:].rearrange("p (tt d) -> p tt d",
                                                       d=128)[:, q4 * 4:q4 * 4 + 4, :]
                        w = [(s_mm.h, s_mm.n)]
                        if fc >= 2 and q4 == 0:
                            w.append((s_mm.h, kv_mm[fc - 2]))
                    s3 = ptrb[:, half * 1024:half * 1024 + 512].rearrange(
                        "p (a b) -> p a b", b=128)
                    tpos_ve[gi] = vop(E_copy, out=d3, in_=s3, waits=w)
                tpos_mm[fc] = s_mm.n

            def emit_kv(fc):
                for tt in range(NTT):
                    w = []
                    if tt == 0:
                        w.append((s_ve.h, tpos_ve[fc * NQ4 + NQ4 - 1]))
                        if fc >= 1:
                            w.append((s_ve.h, kv_free[fc - 1]))
                        elif kind == "k":
                            w.append((s_ve.h, VE_RXV))
                    mm(pG[:, 0:128],
                       kstp[fc % 2][:, tt * 128:(tt + 1) * 128],
                       tm_dest[:].rearrange("p (tt d) -> p tt d", d=D)[
                           :, tt, fc * 128:(fc + 1) * 128],
                       start=(tt == 0), stop=(tt == NTT - 1),
                       inc=(tt == NTT - 1), waits=w)
                kv_mm[fc] = s_mm.n
                kv_free[fc] = vop(E_copy, out=kvf[:, fc * 128:(fc + 1) * 128],
                                  in_=pG[:, 0:128],
                                  waits=[(s_mm.h, kv_mm[fc])])
                for tc in range(NTC):
                    rv = vop(E_red, out=kst[:, tc:tc + 1],
                             in_=stage[fc % 2][:, tc * 512:(tc + 1) * 512],
                             op=ALU.add)
                    vop(E_tt, out=ksacc[:, fc:fc + 1], a=ksacc[:, fc:fc + 1],
                        b=kst[:, tc:tc + 1], op=ALU.add,
                        waits=[(s_ve.h, rv)], inc=(tc == NTC - 1))
                ks_done[fc] = s_ve.n

            def emit_tail(fc):
                if kind in ("v", "k"):
                    emit_transposes(fc)
                if kind == "k":
                    emit_kv(fc)

            for fc in range(ND):
                emit_wload(fc)
                emit_groups_and_evacs(fc)
                if fc > 0:
                    emit_tail(fc - 1)
                if extra_fc is not None:
                    extra_fc(fc)
            emit_tail(ND - 1)

        STAT_END = s_mm.n
        wf0_prev = [LD_CONSTS]
        wf0_red = {}

        def wf0_fc(fc):
            w = [(s_lc.h, wf0_prev[0])]
            if fc >= 2:
                w.append((s_ve.h, wf0_red[fc - 2]))
            else:
                w.append((s_mm.h, STAT_END))
            ld = dma("sync", s_lc, kstp[fc % 2][:, 0:D], wf_d[fc, :, :], w)
            wf0_prev[0] = ld
            rv0 = vop(E_red, out=kst[:, 2:3], in_=kstp[fc % 2][:, 0:D],
                      op=ALU.max, absv=True, waits=[(s_lc.h, ld)])
            wf0_red[fc] = vop(E_tt, out=wmx[:], a=wmx[:], b=kst[:, 2:3],
                              op=ALU.max, waits=[(s_ve.h, rv0)])

        proj_phase("wv", vbs, "v", tm_dest=pool_b,
                   first_mm_waits=[(s_lc.h, LD_CONSTS), (s_gp.h, GP_WMX)],
                   xn_ready=ve_norm, extra_fc=wf0_fc)
        VE_WMX = wf0_red[ND - 1]

        d1 = dma("gpsimd", s_cd,
                 kvin_d.ap()[0:128].rearrange("(p f) -> p f", f=1),
                 wmx[:, 0:1], [(s_ve.h, VE_WMX)])
        d2 = dma("gpsimd", s_cd,
                 rows[0:1, 0:128],
                 kvin_d.ap()[0:128].rearrange("(f p) -> f p", f=1),
                 [(s_cd.h, d1), (s_ve.h, VE_MR1), (s_mm.h, MM_R1)])
        rv = vop(E_red, out=sc[0:1, 0:1], in_=rows[0:1, 0:128], op=ALU.max,
                 waits=[(s_cd.h, d2)])
        vop(E_ts, out=sc[0:1, 1:2], in_=sc[0:1, 0:1], s1=INV448,
            op0=ALU.mult, waits=[(s_ve.h, rv)])
        vop(E_ts, out=sc[0:1, 1:2], in_=sc[0:1, 1:2], s1=1e-12, op0=ALU.max)
        rv = vop(E_recip, out=sc[0:1, 2:3], in_=sc[0:1, 1:2])
        VE_WS = vop(E_ts, out=sc[0:1, 2:3], in_=sc[0:1, 2:3], s1=0.5,
                    op0=ALU.mult, waits=[(s_ve.h, rv)])
        mm(pG[:, 0:1], onp[0:1, 0:128], sc[0:1, 2:3], start=True, stop=True,
           inc=True, waits=[(s_ve.h, VE_WS)])
        VE_RXV = vop(E_copy, out=rxv[:], in_=pG[:, 0:1],
                     waits=[(s_mm.h, s_mm.n)])
        if "vtm" in dbg_d:
            dma("sync", s_y[0], dbg_d["vtm"][:, :], pool_b[:], [(s_ve.h, s_ve.n)])

        proj_phase("wk", kbs, "k", tm_dest=pool_b)
        VE_KV = s_ve.n
        MM_KV = s_mm.n
        if "kvf" in dbg_d:
            dma("sync", s_y[0], dbg_d["kvf"][:, :], kvf, [(s_ve.h, VE_KV)])
        if "ks" in dbg_d:
            dma("sync", s_y[0], dbg_d["ks"][:, :], ksacc[:], [(s_ve.h, VE_KV)])

        dma("gpsimd", s_cd,
            kvin_d.ap()[0:128 * D].rearrange("(p f) -> p f", p=128),
            kvf, [(s_ve.h, VE_KV)])
        dma("gpsimd", s_cd,
            kvin_d.ap()[128 * D:128 * (D + ND)].rearrange("(p f) -> p f", p=128),
            ksacc[:, 0:ND], [])
        KV_IN = s_cd.n
        A("gpsimd", partial(E_cc, kind="AllReduce", op=ALU.add, groups=pairs,
                            ins=[kvin_d.ap().opt()], outs=[kvout_d.ap().opt()],
                            sem=s_cc.h, waits=((s_cd.h, KV_IN),)))
        s_cc.n += 1
        CC_KV = s_cc.n

        proj_phase("wq", qbs, "q")
        VE_QP = s_ve.n
        SE_Q = s_se.n
        if "qp" in dbg_d:
            dma("sync", s_y[0], dbg_d["qp"][:, :], pool_b[:], [(s_ve.h, VE_QP)])

        dma("gpsimd", s_cd, kvb,
            kvout_d.ap()[0:128 * D].rearrange("(p f) -> p f", p=128),
            [(s_cc.h, CC_KV)])
        dma("gpsimd", s_cd, ksb[:, 0:ND],
            kvout_d.ap()[128 * D:128 * (D + ND)].rearrange("(p f) -> p f", p=128),
            [])
        WD_KVB = s_cd.n

        L = 3
        NIT = ND * NTC
        wo_ld = {}
        se_ln = {}
        se_zr = {}
        ve_apply = {}
        zmm = {}
        zrmm = {}
        amm = {}

        def qslice(it):
            h, tc = divmod(it, NTC)
            return pool_b[:, h * T + tc * 512: h * T + tc * 512 + 512]

        def zrrow(it):
            j = it % 3
            return rows[0:1, j * 512:j * 512 + 512]

        for step in range(NIT + L):
            jt = step - L
            if 0 <= jt < NIT:
                h = jt // NTC
                w = [(s_se.h, se_ln[jt])]
                if jt >= 1:
                    w.append((s_se.h, se_zr[jt - 1]))
                zrmm[jt] = mm(pC[:, :], onp[0:1, 0:128], zrrow(jt),
                              start=True, stop=True, inc=True, waits=w)
                w = [(s_mm.h, zrmm[jt])]
                if jt >= 2:
                    w.append((s_ve.h, ve_apply[jt - 2]))
                se_zr[jt] = act(ZR[jt % 2], pC[:, :], AF.Exp, scale=-1.0,
                                waits=w)
                ab = (pA, pB)[jt % 2]
                w = []
                if jt >= 2:
                    w.append((s_ve.h, ve_apply[jt - 2]))
                amm[jt] = mm(ab[:, :], kvb[:, h * 128:(h + 1) * 128],
                             qslice(jt), start=True, stop=True, inc=True,
                             waits=w)
                ve_apply[jt] = vop(
                    E_tt,
                    out=pool_a[:, h * T + (jt % NTC) * 512:
                               h * T + (jt % NTC) * 512 + 512],
                    a=ab[:, :], b=ZR[jt % 2], op=ALU.mult,
                    waits=[(s_mm.h, amm[jt]), (s_se.h, se_zr[jt])])
                if jt % NTC == NTC - 1:
                    hh = jt // NTC
                    wo_ld[hh] = dma(
                        "gpsimd", s_w[hh % 2],
                        pool_b[:, hh * D:(hh + 1) * D].rearrange(
                            "p (g f) -> p g f", g=ND),
                        w_d["wo"][:, :, hh * 128:(hh + 1) * 128].rearrange(
                            "g p f -> p g f"),
                        [(s_mm.h, amm[jt]),
                         (s_w[hh % 2].h, s_w[hh % 2].n)])
            it = step
            if it < NIT:
                zb = (pD, pE)[it % 2]
                w = []
                if it == 0:
                    w = [(s_cd.h, WD_KVB), (s_ve.h, VE_QP), (s_se.h, SE_Q)]
                if it >= 2:
                    w.append((s_se.h, se_ln[it - 2]))
                zmm[it] = mm(zb[0:1, :], ksb[:, it // NTC:it // NTC + 1],
                             qslice(it), start=True, stop=True, inc=True,
                             waits=w)
                w = [(s_mm.h, zmm[it])]
                if it >= 3:
                    w.append((s_mm.h, zrmm[it - 3]))
                se_ln[it] = act(zrrow(it), zb[0:1, :], AF.Ln, waits=w)
        VE_ATTN = s_ve.n
        SE_P6 = s_se.n
        MM_P6 = s_mm.n
        if "attn" in dbg_d:
            dma("sync", s_y[0], dbg_d["attn"][:, :], pool_a[:], [(s_ve.h, VE_ATTN)])

        st_x1 = {}
        st_u = {}
        ld_xr = {}
        ld9 = {}
        se_sq = {}
        stt7 = {}
        statv = {}
        qfree9 = {}
        mr2v = {}
        ln2_marks = {}

        VE_ZT2 = vop(E_memset, ap=zt, val=0.0,
                     waits=[(s_mm.h, MM_P6), (s_ve.h, VE_ATTN)])
        stat_zero_pass([(s_ve.h, VE_ZT2), (s_se.h, SE_P6)])

        def x1q(g):
            q = g % 8
            return sf[q // 4][:, (q % 4) * 512:(q % 4) * 512 + 512]

        def xrq(g):
            q = g % 4
            return wblkF[q // 2][:, (q % 2) * 512:(q % 2) * 512 + 512]

        def buf9(idx):
            q = idx % 8
            return sf[2 + q // 4][:, (q % 4) * 512:(q % 4) * 512 + 512]

        def stq(i):
            return stage[i // 4][:, (i % 4) * 512:(i % 4) * 512 + 512]

        def ln2_slots(tc):
            p = ROWP[tc]
            if tc < 3:
                return (rows[p:p + 1, 0:512], rows[p:p + 1, 512:1024],
                        rows[p:p + 1, 1024:1536])
            return (kstpF[0][0:1, 0:512], kstpF[0][0:1, 512:1024],
                    kstpF[1][0:1, 0:512])

        def ln2_rows_stats(tc):
            r_m, r_1, r_r = ln2_slots(tc)
            bs, ps = SITES[0][tc]
            bq, pq = SITES[1][tc]
            vop(E_ts, out=r_m, in_=bs[ps:ps + 1, :], s1=LN_INV,
                op0=ALU.mult, waits=[(s_mm.h, statv[tc * ND + ND - 1])])
            vop(E_ts, out=r_1, in_=bq[pq:pq + 1, :], s1=LN_INV,
                op0=ALU.mult)
            vop(E_tt, out=r_r, a=r_m, b=r_m, op=ALU.mult)
            vop(E_tt, out=r_1, a=r_1, b=r_r, op=ALU.subtract)
            sv = vop(E_ts, out=r_r, in_=r_1, s1=1e-5, op0=ALU.add)
            act(r_1, r_r, AF.Ln, waits=[(s_ve.h, sv)], inc=False)
            act(r_r, r_1, AF.Exp, scale=-0.5)
            return (s_se.n, s_ve.n)

        last_rep_v = [0]

        def ln2_rows_rep(tc):
            se_mark, ve_mark = ln2_marks[tc]
            r_m, r_1, r_r = ln2_slots(tc)
            p = ROWP[tc]
            mm(ptrbF2, onp[p:p + 1, 0:128], r_m,
               start=True, stop=True, inc=True,
               waits=[(s_se.h, se_mark),
                      (s_ve.h, max(ve_mark, last_rep_v[0]))])
            mv = vop(E_copy, out=Mt[:, tc * 512:(tc + 1) * 512], in_=ptrbF2,
                     waits=[(s_mm.h, s_mm.n)])
            mm(ptrbF2, onp[p:p + 1, 0:128], r_r,
               start=True, stop=True, inc=True,
               waits=[(s_ve.h, mv)])
            rv2 = vop(E_copy, out=Rt[:, tc * 512:(tc + 1) * 512],
                      in_=ptrbF2, waits=[(s_mm.h, s_mm.n)])
            last_rep_v[0] = rv2
            return rv2

        def emit_p9_load(idx):
            tc9, gc9 = divmod(idx, ND)
            w = [(s_y[idx % 2].h, st_x1[idx])]
            if idx < 8:
                w += [(s_ve.h, VE_ATTN), (s_se.h, SE_P6)]
            if idx >= 2:
                w.append((s_f[idx % 2].h, ld9[idx - 2]))
            if idx >= 8:
                w += [(s_ve.h, qfree9[idx - 8]),
                      (s_u[(idx - 8) % 2].h, st_u[idx - 8])]
            ld9[idx] = dma("sync", s_f[idx % 2], buf9(idx),
                           x1_d.ap()[gc9 * 128:(gc9 + 1) * 128,
                                     tc9 * 512:(tc9 + 1) * 512], w)

        def emit_p9_chunk(idx, on_gp=False, inplace=False):
            tc9, gc9 = divmod(idx, ND)
            tw = slice(tc9 * 512, (tc9 + 1) * 512)
            if inplace:
                b = x1q(idx)
                w = [(s_y[idx % 2].h, st_x1[idx]), (s_se.h, se_sq[idx])]
            else:
                b = buf9(idx)
                w = [(s_f[idx % 2].h, ld9[idx])]
            if gc9 == 0:
                w.append((s_ve.h, mr2v[tc9]))
            if on_gp:
                gop(E_tt, out=b, a=b, b=Mt[:, tw], op=ALU.subtract,
                    waits=w + [(s_ve.h, mr2v[tc9])])
                du = gop(E_tt, out=b, a=b, b=Rt[:, tw], op=ALU.mult)
                datw = [(s_gp.h, du)]
            else:
                vop(E_tt, out=b, a=b, b=Mt[:, tw], op=ALU.subtract,
                    waits=w, inc=False)
                du = vop(E_tt, out=b, a=b, b=Rt[:, tw], op=ALU.mult)
                datw = [(s_ve.h, du)]
            rv = vop(E_red, out=tmx[:, 0:1], in_=b, op=ALU.max, absv=True,
                     waits=datw)
            qfree9[idx] = vop(E_tt, out=amx[:], a=amx[:], b=tmx[:, 0:1],
                              op=ALU.max, waits=[(s_ve.h, rv)])
            st_u[idx] = dma("scalar", s_u[idx % 2],
                            xn2_d.ap()[gc9 * 128:(gc9 + 1) * 128, tw],
                            b, list(datw) + [(s_u[idx % 2].h,
                                              s_u[idx % 2].n)])

        def emit_stat(g):
            tcs, gcs = divmod(g, ND)
            pr = g % 3
            for bi in range(2):
                bank, p = SITES[bi][tcs]
                mm(bank[p:p + 1, :], onesb[:], stq(2 * pr + bi),
                   start=False, stop=(gcs == ND - 1), tp=(0, p),
                   inc=(bi == 1),
                   waits=([(s_se.h, se_sq[g])] if bi == 0 else []))
            statv[g] = s_mm.n
            if gcs == ND - 1:
                ln2_marks[tcs] = ln2_rows_stats(tcs)

        wo_last = (wo_ld[ND - 2], wo_ld[ND - 1])
        NG = NTC * ND
        pending_rep = [None]
        for tc in range(NTC):
            tw = slice(tc * 512, (tc + 1) * 512)
            for gc in range(ND):
                g = tc * ND + gc
                w = [(s_x[g % 2].h, ld_xr[g - 2])] if g >= 2 else []
                if g >= 4:
                    w.append((s_ve.h, stt7[g - 4]))
                else:
                    w.append((s_mm.h, MM_P6))
                ld_xr[g] = dma("sync", s_x[g % 2], xrq(g),
                               x_d[gc * 128:(gc + 1) * 128, tw], w)
                w = []
                if g == 0:
                    w = [(s_ve.h, VE_ATTN), (s_se.h, SE_P6),
                         (s_w[0].h, wo_last[0]), (s_w[1].h, wo_last[1])]
                if g >= 3:
                    w.append((s_ve.h, stt7[g - 3]))
                for ec in range(ND):
                    mm(pbank[g % 3][:, :],
                       pool_b[:, ec * D + gc * 128: ec * D + gc * 128 + 128],
                       pool_a[:, ec * T + tc * 512: ec * T + tc * 512 + 512],
                       start=(ec == 0), stop=(ec == ND - 1),
                       inc=(ec == ND - 1), waits=(w if ec == 0 else []))
                mmg = s_mm.n
                if g >= 2:
                    emit_stat(g - 2)
                if pending_rep[0] is not None:
                    mr2v[pending_rep[0]] = ln2_rows_rep(pending_rep[0])
                    pending_rep[0] = None
                if g >= 2 and (g - 2) % ND == ND - 1:
                    pending_rep[0] = (g - 2) // ND
                w = [(s_mm.h, mmg), (s_x[g % 2].h, ld_xr[g])]
                if g >= 8:
                    w += [(s_y[g % 2].h, st_x1[g - 8]),
                          (s_se.h, se_sq[g - 8])]
                stt7[g] = vop(E_stt, out=x1q(g), in0=pbank[g % 3][:, :],
                              scalar=obs[:, gc:gc + 1], in1=xrq(g),
                              op0=ALU.add, op1=ALU.add, waits=w)
                st_x1[g] = dma("scalar", s_y[g % 2],
                               x1_d.ap()[gc * 128:(gc + 1) * 128, tw],
                               x1q(g), [(s_ve.h, stt7[g]),
                                        (s_y[g % 2].h, s_y[g % 2].n)])
                pr = g % 3
                w = [(s_ve.h, stt7[g])]
                if g >= 3:
                    w.append((s_mm.h, statv[g - 3]))
                else:
                    w.append((s_mm.h, MM_P6))
                act(stq(2 * pr), x1q(g), AF.Copy, waits=w, inc=False)
                se_sq[g] = act(stq(2 * pr + 1), x1q(g), AF.Square)
                if g >= ND:
                    emit_p9_load(g - ND)
                if g >= ND + 2:
                    emit_p9_chunk(g - ND - 2)
        for g in (NG - 2, NG - 1):
            emit_stat(g)
        if pending_rep[0] is not None:
            mr2v[pending_rep[0]] = ln2_rows_rep(pending_rep[0])
            pending_rep[0] = None
        mr2v[NTC - 1] = ln2_rows_rep(NTC - 1)
        VE_P7 = s_ve.n
        SE_P7 = s_se.n
        MM_P7END = s_mm.n
        Y0_P7 = s_y[0].n
        Y1_P7 = s_y[1].n

        nres = min(8, ND)
        for idx in range(NG - ND - 2, NG):
            nxt = idx + 2
            if (NTC - 1) * ND <= nxt < NG - nres:
                emit_p9_load(nxt)
            emit_p9_chunk(idx, on_gp=(idx >= (NTC - 1) * ND
                                      and idx % 2 == 1),
                          inplace=(idx >= NG - nres))
        VE_AMX = s_ve.n
        GP_AMX = s_gp.n
        U0_P9 = s_u[0].n
        U1_P9 = s_u[1].n

        ld_wf = {}
        se_wq = {}
        for gcw in range(ND):
            w = [(s_f[gcw % 2].h, s_f[gcw % 2].n)]
            if gcw < 2:
                w.append((s_ve.h, VE_P7))
            else:
                w.append((s_se.h, se_wq[gcw - 2]))
            ld_wf[gcw] = dma("sync", s_f[gcw % 2], wblk[gcw % 2][:, 0:D],
                             wf_d[gcw, :, :], w)
            w = [(s_f[gcw % 2].h, ld_wf[gcw])]
            if gcw == 0:
                w += [(s_mm.h, MM_P7END), (s_ve.h, VE_RXV)]
            se_wq[gcw] = act(wq8[:, gcw * D:(gcw + 1) * D],
                             wblk[gcw % 2][:, 0:D], AF.Identity,
                             scale=rxv[:, 0:1], waits=w)
        SE_WQ = s_se.n

        d3 = dma("gpsimd", s_cd,
                 kvin_d.ap()[0:128].rearrange("(p f) -> p f", f=1),
                 amx[:, 0:1], [(s_ve.h, VE_AMX), (s_gp.h, GP_AMX)])
        d4 = dma("gpsimd", s_cd,
                 rows[0:1, 0:128],
                 kvin_d.ap()[0:128].rearrange("(f p) -> f p", f=1),
                 [(s_cd.h, d3), (s_ve.h, s_ve.n)])
        VE_AM1 = vop(E_red, out=sc[0:1, 16:17], in_=rows[0:1, 0:128],
                     op=ALU.max, waits=[(s_cd.h, d4)])
        dma("gpsimd", s_cd, amin_d.ap()[0:8].rearrange("(a b) -> a b", a=1),
            sc[0:1, 16:24], [(s_ve.h, VE_AM1)])
        A("gpsimd", partial(E_cc, kind="AllReduce", op=ALU.max, groups=allg,
                            ins=[amin_d.ap().opt()], outs=[amout_d.ap().opt()],
                            sem=s_cc.h, waits=((s_cd.h, s_cd.n),)))
        s_cc.n += 1
        dma("gpsimd", s_cd, sc[0:1, 24:32],
            amout_d.ap()[0:8].rearrange("(a b) -> a b", a=1),
            [(s_cc.h, s_cc.n)])
        WD_AM = s_cd.n
        vop(E_ts, out=sc[0:1, 17:18], in_=sc[0:1, 24:25], s1=INV448,
            op0=ALU.mult, waits=[(s_cd.h, WD_AM)])
        vop(E_ts, out=sc[0:1, 17:18], in_=sc[0:1, 17:18], s1=1e-12,
            op0=ALU.max)
        rv = vop(E_recip, out=sc[0:1, 18:19], in_=sc[0:1, 17:18])
        vop(E_ts, out=sc[0:1, 18:19], in_=sc[0:1, 18:19], s1=0.5,
            op0=ALU.mult, waits=[(s_ve.h, rv)])
        vop(E_tt, out=sc[0:1, 19:20], a=sc[0:1, 17:18], b=sc[0:1, 1:2],
            op=ALU.mult)
        VE_SC = vop(E_ts, out=sc[0:1, 19:20], in_=sc[0:1, 19:20], s1=4.0,
                    op0=ALU.mult)
        if "sc" in dbg_d:
            dma("sync", s_y[0], dbg_d["sc"][:, :], sc[:], [(s_ve.h, VE_SC)])
        mm(pG[:, 0:1], onp[0:1, 0:128], sc[0:1, 18:19], start=True, stop=True,
           inc=True, waits=[(s_ve.h, VE_SC)])
        VE_RXQ = vop(E_copy, out=rxq[:], in_=pG[:, 0:1],
                     waits=[(s_mm.h, s_mm.n)])
        mm(pG[:, 0:1], onp[0:1, 0:128], sc[0:1, 19:20], start=True, stop=True,
           inc=True, waits=[(s_ve.h, s_ve.n)])
        VE_S4 = vop(E_copy, out=s4v[:], in_=pG[:, 0:1],
                    waits=[(s_mm.h, s_mm.n)])

        ld_p11 = {}
        ld_q = {}
        ve_q = {}
        npre = ND // 2
        for j in range(npre):
            w = [(s_u[j % 2].h, st_u[(NTC - 1) * ND + j]),
                 (s_p[j % 2].h, s_p[j % 2].n), (s_mm.h, MM_P7END)]
            ld_p11[j] = dma("sync", s_p[j % 2], pbf[:, j * T:(j + 1) * T],
                            xn2_d.ap()[j * 128:(j + 1) * 128, :], w)
        for j in range(npre):
            w = [(s_p[j % 2].h, ld_p11[j])]
            if j == 0:
                w.append((s_ve.h, VE_RXQ))
            ve_q[j] = vop(E_ts, out=xq8[:, j * T:(j + 1) * T],
                          in_=pbf[:, j * T:(j + 1) * T], s1=rxq[:, 0:1],
                          op0=ALU.mult, waits=w)
        nring = min(4, ND - npre)
        for i in range(ND - npre + nring):
            if i >= nring:
                dcv = npre + i - nring
                ve_q[dcv] = vop(E_ts, out=xq8[:, dcv * T:(dcv + 1) * T],
                                in_=sf[dcv % nring][:], s1=rxq[:, 0:1],
                                op0=ALU.mult,
                                waits=[(s_x[dcv % 2].h, ld_q[dcv])])
            dcq = npre + i
            if dcq <= ND - 1:
                w = [(s_u[dcq % 2].h, st_u[(NTC - 1) * ND + dcq]),
                     (s_x[dcq % 2].h, s_x[dcq % 2].n),
                     (s_ve.h, VE_AMX), (s_gp.h, GP_AMX),
                     (s_u[0].h, U0_P9), (s_u[1].h, U1_P9)]
                if dcq >= npre + nring:
                    w.append((s_ve.h, ve_q[dcq - nring]))
                ld_q[dcq] = dma("sync", s_x[dcq % 2], sf[dcq % nring][:],
                                xn2_d.ap()[dcq * 128:(dcq + 1) * 128, :], w)
        VE_XQ = s_ve.n
        if "xq" in dbg_d:
            dma("sync", s_y[0], dbg_d["xq"][:, :], xq8[:], [(s_ve.h, VE_XQ)])

        ld_r2 = {}
        ve_out = {}
        st_y = {}
        v12 = {}
        nrr = min(2, ND // 2)
        for gc in range(ND):
            w = [(s_x[gc % 2].h, s_x[gc % 2].n)]
            if gc >= nrr:
                w.append((s_ve.h, ve_out[gc - nrr]))
            else:
                w.append((s_ve.h, ve_q[gc]))
            ld_r2[gc] = dma("sync", s_x[gc % 2],
                            pbf[:, (gc % nrr) * T:(gc % nrr) * T + T],
                            x1_d.ap()[gc * 128:(gc + 1) * 128, :], w)
            for tc in range(NTC):
                g = gc * NTC + tc
                w = []
                if g == 0:
                    w = [(s_ve.h, VE_XQ), (s_se.h, SE_WQ)]
                if g >= 3:
                    w.append((s_ve.h, v12[g - 3]))
                for dc in range(ND // 2):
                    lhsT = wq8[:, gc * D + dc * 256:
                               gc * D + dc * 256 + 256].rearrange(
                        "p (j c) -> p j c", j=2)
                    rhs = xq8[:, dc * 2 * T:(dc * 2 + 2) * T].rearrange(
                        "p (j t) -> p j t", j=2)[:, :, tc * 512:(tc + 1) * 512]
                    mm(pbank[g % 3][:, :], lhsT, rhs,
                       start=(dc == 0), stop=(dc == ND // 2 - 1),
                       inc=(dc == ND // 2 - 1), pm=PM_DR,
                       waits=(w if dc == 0 else []))
                mg = s_mm.n
                sl = slice(tc * 512, (tc + 1) * 512)
                vop(E_ts, out=fb, in_=pbank[g % 3][:, :], s1=s4v[:, 0:1],
                    op0=ALU.mult, waits=[(s_mm.h, mg)], inc=False)
                w = [(s_x[gc % 2].h, ld_r2[gc])]
                if gc >= 2 and tc == 0:
                    w.append((s_y[gc % 2].h, st_y[gc - 2]))
                if gc < 2 and tc == 0:
                    w += [(s_u[0].h, U0_P9), (s_u[1].h, U1_P9),
                          (s_gp.h, GP_AMX)]
                v12[g] = vop(E_tt, out=sf[2 + gc % 2][:, sl],
                             a=pbf[:, (gc % nrr) * T + tc * 512:
                                   (gc % nrr) * T + (tc + 1) * 512],
                             b=fb, op=ALU.add, waits=w)
            ve_out[gc] = s_ve.n
            st_y[gc] = dma("scalar", s_y[gc % 2],
                           y_d[gc * 128:(gc + 1) * 128, :],
                           sf[2 + gc % 2][:],
                           [(s_ve.h, ve_out[gc]),
                            (s_y[gc % 2].h, s_y[gc % 2].n)])

        A("sync", partial(E_wait, waits=((s_y[0].h, s_y[0].n),
                                         (s_y[1].h, s_y[1].n))))

        @block.sync
        def _(eng):
            for f in prog["sync"]:
                f(eng)

        @block.gpsimd
        def _(eng):
            for f in prog["gpsimd"]:
                f(eng)

        @block.tensor
        def _(eng):
            for f in prog["tensor"]:
                f(eng)

        @block.vector
        def _(eng):
            for f in prog["vector"]:
                f(eng)

        @block.scalar
        def _(eng):
            for f in prog["scalar"]:
                f(eng)

    return nc


def _wblocks(wt, dt):
    d_in, d_out = wt.shape
    a = np.ascontiguousarray(
        wt.reshape(d_in // 128, 128, d_out // 128, 128).transpose(2, 1, 0, 3)
    ).reshape(d_out // 128, 128, d_in)
    return np.ascontiguousarray(a.astype(dt, copy=False))


def kernel(L0_attn_hidden_states, L0_attn_q_w, L0_attn_q_b, L0_attn_k_w,
           L0_attn_k_b, L0_attn_v_w, L0_attn_v_b, L0_attn_o_w, L0_attn_o_b,
           L0_ffn_weight):
    x = np.asarray(L0_attn_hidden_states)
    B, S, D = x.shape
    ncores = 8
    Tc = (B * S) // ncores
    ND = D // 128

    nc = build_nc(D=D, T=Tc, ncores=ncores)

    wq = _wblocks(np.ascontiguousarray(np.asarray(L0_attn_q_w).T), np.float32)
    wk = _wblocks(np.ascontiguousarray(np.asarray(L0_attn_k_w).T), np.float32)
    wv = _wblocks(np.ascontiguousarray(np.asarray(L0_attn_v_w).T), np.float32)
    wo = _wblocks(np.ascontiguousarray(np.asarray(L0_attn_o_w).T), np.float32)
    wf = _wblocks(np.ascontiguousarray(np.asarray(L0_ffn_weight).T),
                  ml_dtypes.bfloat16)
    qb = np.ascontiguousarray(np.asarray(L0_attn_q_b, np.float32).reshape(ND, 128).T)
    kb = np.ascontiguousarray(np.asarray(L0_attn_k_b, np.float32).reshape(ND, 128).T)
    vb = np.ascontiguousarray(np.asarray(L0_attn_v_b, np.float32).reshape(ND, 128).T)
    ob = np.ascontiguousarray(np.asarray(L0_attn_o_b, np.float32).reshape(ND, 128).T)
    onesb = np.ones((128, 1), ml_dtypes.bfloat16)
    onp_ = np.ones((128, 128), np.float32)
    idb = np.eye(128, dtype=ml_dtypes.bfloat16)

    halves = S // 2
    in_maps = []
    for c in range(ncores):
        b, h = c >> 1, c & 1
        slab = np.asarray(x[b, h * halves:(h + 1) * halves, :], np.float32)
        x_fm = np.ascontiguousarray(slab.T)
        in_maps.append({
            "x": x_fm, "wq": wq, "wk": wk, "wv": wv, "wo": wo, "wf": wf,
            "qb": qb, "kb": kb, "vb": vb, "ob": ob, "onesb": onesb,
            "onp": onp_, "idb": idb,
        })

    res = run_bass_kernel_spmd(nc, in_maps, core_ids=list(range(ncores)))

    out = np.empty((B, S, D), np.float32)
    for c in range(ncores):
        b, h = c >> 1, c & 1
        out[b, h * halves:(h + 1) * halves, :] = res.results[c]["y"].T
    return out


# version 19
# speedup vs baseline: 1.2207x; 1.0338x over previous
import contextlib
from functools import partial

import numpy as np
import ml_dtypes

import concourse.bass as bass
import concourse.mybir as mybir
from concourse.bass_utils import run_bass_kernel_spmd

F32 = mybir.dt.float32
BF16 = mybir.dt.bfloat16
FP8 = mybir.dt.float8e4
AF = mybir.ActivationFunctionType
PM_DR = mybir.MatmulPerfMode.DoubleRow
ALU = mybir.AluOpType
AX = mybir.AxisListType

ENGINES = ("sync", "gpsimd", "tensor", "vector", "scalar")

DBG = []
STRICT = [False]


class Sem:
    __slots__ = ("h", "n")

    def __init__(self, h):
        self.h = h
        self.n = 0


def _w(e, waits):
    for (h, v) in waits:
        e.wait_ge(h, v)


def E_dma(e, out, in_, sem, waits=()):
    _w(e, waits)
    e.dma_start(out=out, in_=in_).then_inc(sem, 16)


def E_mm(e, out, l, r, start, stop, sem=None, tp=None, pm=None, waits=()):
    _w(e, waits)
    kw = {}
    if tp is not None:
        kw["tile_position"] = tp
    if pm is not None:
        kw["perf_mode"] = pm
    i = e.matmul(out, l, r, start=start, stop=stop, **kw)
    if sem is not None:
        i.then_inc(sem, 1)


def E_tr(e, out, in_, ident, sem=None, waits=()):
    _w(e, waits)
    i = e.transpose(out, in_, ident)
    if sem is not None:
        i.then_inc(sem, 1)


def E_act(e, out, in_, func, bias=None, scale=None, sem=None, waits=()):
    _w(e, waits)
    kw = {}
    if bias is not None:
        kw["bias"] = bias
    if scale is not None:
        kw["scale"] = scale
    i = e.activation(out, in_, func, **kw)
    if sem is not None:
        i.then_inc(sem, 1)


def E_copy(e, out, in_, sem=None, waits=()):
    _w(e, waits)
    i = e.tensor_copy(out=out, in_=in_)
    if sem is not None:
        i.then_inc(sem, 1)


def E_tt(e, out, a, b, op, sem=None, waits=()):
    _w(e, waits)
    i = e.tensor_tensor(out=out, in0=a, in1=b, op=op)
    if sem is not None:
        i.then_inc(sem, 1)


def E_ts(e, out, in_, s1, op0, sem=None, waits=()):
    _w(e, waits)
    i = e.tensor_scalar(out=out, in0=in_, scalar1=s1, scalar2=None, op0=op0)
    if sem is not None:
        i.then_inc(sem, 1)


def E_stt(e, out, in0, scalar, in1, op0, op1, sem=None, waits=()):
    _w(e, waits)
    i = e.scalar_tensor_tensor(out=out, in0=in0, scalar=scalar, in1=in1,
                               op0=op0, op1=op1)
    if sem is not None:
        i.then_inc(sem, 1)


def E_red(e, out, in_, op, absv=False, sem=None, waits=()):
    _w(e, waits)
    i = e.tensor_reduce(out=out, in_=in_, axis=AX.X, op=op,
                        apply_absolute_value=absv)
    if sem is not None:
        i.then_inc(sem, 1)


def E_recip(e, out, in_, sem=None, waits=()):
    _w(e, waits)
    i = e.reciprocal(out=out, in_=in_)
    if sem is not None:
        i.then_inc(sem, 1)


def E_memset(e, ap, val, sem=None, waits=()):
    _w(e, waits)
    i = e.memset(ap, val)
    if sem is not None and i is not None:
        i.then_inc(sem, 1)


def E_cc(e, kind, op, groups, ins, outs, sem, waits=()):
    _w(e, waits)
    e.collective_compute(kind, op, replica_groups=groups, ins=ins,
                         outs=outs).then_inc(sem, 1)


def E_wait(e, waits):
    _w(e, waits)


def build_nc(D=2048, T=2048, ncores=8):
    ND = D // 128
    NTC = T // 512
    NTT = T // 128
    NQ4 = NTT // 4
    assert NTC <= 4
    LN_INV = 1.0 / D
    INV448 = float(np.float32(1.0) / np.float32(448.0))

    nc = bass.Bass("TRN2", target_bir_lowering=False)

    x_d = nc.dram_tensor("x", [D, T], F32, kind="ExternalInput").ap()
    w_d = {}
    for w in ("wq", "wk", "wv", "wo"):
        w_d[w] = nc.dram_tensor(w, [ND, 128, D], F32, kind="ExternalInput").ap()
    wf_d = nc.dram_tensor("wf", [ND, 128, D], BF16, kind="ExternalInput").ap()
    qb_d = nc.dram_tensor("qb", [128, ND], F32, kind="ExternalInput").ap()
    kb_d = nc.dram_tensor("kb", [128, ND], F32, kind="ExternalInput").ap()
    vb_d = nc.dram_tensor("vb", [128, ND], F32, kind="ExternalInput").ap()
    ob_d = nc.dram_tensor("ob", [128, ND], F32, kind="ExternalInput").ap()
    ones_d = nc.dram_tensor("onesb", [128, 1], BF16, kind="ExternalInput").ap()
    onp_d = nc.dram_tensor("onp", [128, 128], F32, kind="ExternalInput").ap()
    idb_d = nc.dram_tensor("idb", [128, 128], BF16, kind="ExternalInput").ap()

    y_d = nc.dram_tensor("y", [D, T], F32, kind="ExternalOutput").ap()

    x1_d = nc.dram_tensor("x1s", [D, T], F32)
    xn2_d = nc.dram_tensor("xn2s", [D, T], F32)
    kvin_d = nc.dram_tensor("kvin", [128 * (D + ND)], F32)
    kvout_d = nc.dram_tensor("kvout", [128 * (D + ND)], F32)
    amin_d = nc.dram_tensor("amin", [8], F32)
    amout_d = nc.dram_tensor("amout", [8], F32)

    dbg_d = {}
    for name, shape, dt in (
        ("xn", [128, ND * T], BF16), ("vtm", [128, ND * T], BF16),
        ("kvf", [128, D], F32), ("ks", [128, ND], F32),
        ("qp", [128, ND * T], BF16), ("attn", [128, ND * T], BF16),
        ("rows", [128, 1536], F32), ("sc", [1, 32], F32),
        ("xq", [128, ND * T], FP8),
    ):
        if name in DBG:
            dbg_d[name] = nc.dram_tensor("dbg_" + name, shape, dt,
                                         kind="ExternalOutput").ap()

    pairs = [[2 * i, 2 * i + 1] for i in range(max(1, ncores // 2))]
    allg = [list(range(ncores))]

    prog = {k: [] for k in ENGINES}

    with contextlib.ExitStack() as ctx:
        ent = ctx.enter_context
        pool_a = ent(nc.sbuf_tensor("pool_a", [128, ND * T], BF16))
        pool_b = ent(nc.sbuf_tensor("pool_b", [128, ND * T], BF16))
        Mt = ent(nc.sbuf_tensor("Mt", [128, T], F32))
        Rt = ent(nc.sbuf_tensor("Rt", [128, T], F32))
        sf = [ent(nc.sbuf_tensor(f"sf{i}", [128, T], F32)) for i in range(4)]
        wblk = [ent(nc.sbuf_tensor(f"wblk{i}", [128, max(D, 2048)], BF16))
                for i in range(2)]
        kstp = [ent(nc.sbuf_tensor(f"kstp{i}", [128, T], BF16)) for i in range(2)]
        stage = [ent(nc.sbuf_tensor(f"stage{i}", [128, T], BF16)) for i in range(2)]
        rows = ent(nc.sbuf_tensor("rows", [128, 1536], F32))
        sc = ent(nc.sbuf_tensor("sc", [1, 32], F32))
        ksacc = ent(nc.sbuf_tensor("ksacc", [128, ND], F32))
        kst = ent(nc.sbuf_tensor("kst", [128, 4], F32))
        ksb = ent(nc.sbuf_tensor("ksb", [128, ND], BF16))

        amx = ent(nc.sbuf_tensor("amx", [128, 1], F32))
        wmx = ent(nc.sbuf_tensor("wmx", [128, 1], F32))
        tmx = ent(nc.sbuf_tensor("tmx", [128, 1], F32))
        rxv = ent(nc.sbuf_tensor("rxv", [128, 1], F32))
        s4v = ent(nc.sbuf_tensor("s4v", [128, 1], F32))
        rxq = ent(nc.sbuf_tensor("rxq", [128, 1], F32))
        qbs = ent(nc.sbuf_tensor("qbs", [128, ND], F32))
        kbs = ent(nc.sbuf_tensor("kbs", [128, ND], F32))
        vbs = ent(nc.sbuf_tensor("vbs", [128, ND], F32))
        obs = ent(nc.sbuf_tensor("obs", [128, ND], F32))
        onesb = ent(nc.sbuf_tensor("onesbs", [128, 1], BF16))
        onp = ent(nc.sbuf_tensor("onps", [128, 128], F32))
        idb = ent(nc.sbuf_tensor("idbs", [128, 128], BF16))

        xq8 = pool_a[:, 0:ND * T // 2].bitcast(FP8)
        kvf = sf[0][:, 0:D]
        paf = pool_a[:].bitcast(F32)
        pbf = pool_b[:].bitcast(F32)
        fb = kstp[0][:, 0:512]
        ztA = pool_a[:, 0:512]
        zt = stage[1][:, 1536:2048]
        kvb = stage[0][:, 0:D]
        ZR = [kstp[i][:, 0:1024].bitcast(F32) for i in range(2)]
        wq8 = pool_a[:, ND * T // 2:ND * T].bitcast(FP8)
        wblkF = [wblk[i][:].bitcast(F32) for i in range(2)]
        kstpF = [kstp[i][:].bitcast(F32) for i in range(2)]

        pA = ent(nc.psum_tensor("pA", [128, 512], F32))
        pB = ent(nc.psum_tensor("pB", [128, 512], F32))
        pC = ent(nc.psum_tensor("pC", [128, 512], F32))
        pD = ent(nc.psum_tensor("pD", [128, 512], F32))
        pE = ent(nc.psum_tensor("pE", [128, 512], F32))
        ptrb = ent(nc.psum_tensor("ptrb", [128, 2048], BF16))
        pG = ent(nc.psum_tensor("pG", [128, 512], F32))
        pbank = [pA, pB, pC]

        s_lc = Sem(ent(nc.semaphore("s_lc")))
        s_x = [Sem(ent(nc.semaphore(f"s_x{i}"))) for i in range(2)]
        s_f = [Sem(ent(nc.semaphore(f"s_f{i}"))) for i in range(2)]
        s_y = [Sem(ent(nc.semaphore(f"s_y{i}"))) for i in range(2)]
        s_w = [Sem(ent(nc.semaphore(f"s_w{i}"))) for i in range(2)]
        s_u = [Sem(ent(nc.semaphore(f"s_u{i}"))) for i in range(2)]
        s_p = [Sem(ent(nc.semaphore(f"s_p{i}"))) for i in range(2)]
        s_cd = Sem(ent(nc.semaphore("s_cd")))
        s_gp = Sem(ent(nc.semaphore("s_gp")))
        s_cc = Sem(ent(nc.semaphore("s_cc")))
        s_mm = Sem(ent(nc.semaphore("s_mm")))
        s_se = Sem(ent(nc.semaphore("s_se")))
        s_ve = Sem(ent(nc.semaphore("s_ve")))

        block = ent(nc.Block())

        def A(eng, fn):
            prog[eng].append(fn)

        def dma(qname, sem, out, in_, waits=()):
            A(qname, partial(E_dma, out=out, in_=in_, sem=sem.h,
                             waits=tuple(waits)))
            sem.n += 16
            return sem.n

        def mm(out, l, r, start, stop, inc=False, tp=None, pm=None, waits=()):
            if STRICT[0]:
                waits = list(waits) + [(s_mm.h, s_mm.n)]
                inc = True
            A("tensor", partial(E_mm, out=out, l=l, r=r, start=start, stop=stop,
                                sem=(s_mm.h if inc else None), tp=tp, pm=pm,
                                waits=tuple(waits)))
            if inc:
                s_mm.n += 1
            return s_mm.n

        def tr(out, in_, ident, inc=True, waits=()):
            if STRICT[0]:
                waits = list(waits) + [(s_mm.h, s_mm.n)]
                inc = True
            A("tensor", partial(E_tr, out=out, in_=in_, ident=ident,
                                sem=(s_mm.h if inc else None),
                                waits=tuple(waits)))
            if inc:
                s_mm.n += 1
            return s_mm.n

        def act(out, in_, func, bias=None, scale=None, inc=True, waits=()):
            waits = list(waits) + [(s_se.h, s_se.n)]
            inc = True
            A("scalar", partial(E_act, out=out, in_=in_, func=func, bias=bias,
                                scale=scale, sem=(s_se.h if inc else None),
                                waits=tuple(waits)))
            if inc:
                s_se.n += 1
            return s_se.n

        def gop(fn, waits=(), **kw):
            A("gpsimd", partial(fn, sem=s_gp.h,
                                waits=tuple(list(waits) + [(s_gp.h, s_gp.n)]),
                                **kw))
            s_gp.n += 1
            return s_gp.n

        def vop(fn, inc=True, waits=(), **kw):
            waits = list(waits) + [(s_ve.h, s_ve.n)]
            inc = True
            A("vector", partial(fn, sem=(s_ve.h if inc else None),
                                waits=tuple(waits), **kw))
            if inc:
                s_ve.n += 1
            return s_ve.n

        for (dst, src) in ((qbs, qb_d), (kbs, kb_d), (vbs, vb_d), (obs, ob_d),
                           (onesb, ones_d), (onp, onp_d), (idb, idb_d)):
            dma("sync", s_lc, dst[:], src[:, :])
        LD_CONSTS = s_lc.n
        vop(E_memset, ap=ztA, val=0.0)
        vop(E_memset, ap=sc[:], val=0.0)
        vop(E_memset, ap=ksacc[:], val=0.0)
        vop(E_memset, ap=amx[:], val=0.0)
        vop(E_memset, ap=wmx[:], val=0.0)

        ptrbF = ptrb[:, 0:1024].bitcast(F32)
        ptrbF2 = ptrb[:, 1024:2048].bitcast(F32)
        SITES = {
            0: [(pD[:, :], 0), (pD[:, :], 32), (pD[:, :], 64), (pG[:, :], 0)],
            1: [(pE[:, :], 0), (pE[:, :], 32), (pE[:, :], 64), (ptrbF, 0)],
        }
        ROWP = [0, 32, 64, 0]

        def stat_zero_pass(waits, z=None):
            first = True
            for bi in (0, 1):
                for tc in range(NTC):
                    bank, p = SITES[bi][tc]
                    mm(bank[p:p + 1, :], onesb[:],
                       (z if z is not None else zt),
                       start=True, stop=False, tp=(0, p),
                       waits=(waits if first else []))
                    first = False

        ld_x = []
        se_cast = []
        stat_done = []
        SEM4 = [s_x[0], s_x[1], s_f[0], s_f[1]]
        stat_zero_pass([(s_lc.h, LD_CONSTS), (s_ve.h, s_ve.n)], z=ztA)
        for dc in range(ND):
            sem = SEM4[dc % 4]
            w = [(s_se.h, se_cast[dc - 4])] if dc >= 4 else []
            ld_x.append(dma("sync", sem, sf[dc % 4][:],
                            x_d[dc * 128:(dc + 1) * 128, :], w))
            w = [(sem.h, ld_x[dc])]
            if dc >= 2:
                w.append((s_mm.h, stat_done[dc - 2]))
            se_cast.append(act(stage[dc % 2][:], sf[dc % 4][:], AF.Copy,
                               waits=w))
            wv = [(s_se.h, se_cast[dc])]
            if dc >= 2:
                wv.append((s_mm.h, stat_done[dc - 2]))
            ve_sq = vop(E_tt, out=kstp[dc % 2][:], a=stage[dc % 2][:],
                        b=stage[dc % 2][:], op=ALU.mult, waits=wv)
            for tc in range(NTC):
                for bi, src in enumerate((stage[dc % 2], kstp[dc % 2])):
                    bank, p = SITES[bi][tc]
                    last = (tc == NTC - 1 and bi == 1)
                    w = []
                    if tc == 0 and bi == 0:
                        w = [(s_se.h, se_cast[dc]), (s_ve.h, ve_sq)]
                    mm(bank[p:p + 1, :], onesb[:],
                       src[:, tc * 512:(tc + 1) * 512],
                       start=False, stop=(dc == ND - 1),
                       tp=(0, p), inc=last, waits=w)
            stat_done.append(s_mm.n)
        GP_WMX = s_gp.n

        def ln_rows(eps):
            mmv = s_mm.n
            marks = []
            for tc in range(NTC):
                p = ROWP[tc]
                if tc < 3:
                    r_m = rows[p:p + 1, 0:512]
                    r_1 = rows[p:p + 1, 512:1024]
                    r_r = rows[p:p + 1, 1024:1536]
                else:
                    r_m = kstpF[0][0:1, 0:512]
                    r_1 = kstpF[0][0:1, 512:1024]
                    r_r = kstpF[1][0:1, 0:512]
                bs, ps = SITES[0][tc]
                bq, pq = SITES[1][tc]
                vop(E_ts, out=r_m, in_=bs[ps:ps + 1, :], s1=LN_INV,
                    op0=ALU.mult,
                    waits=([(s_mm.h, mmv)] if tc == 0 else []))
                vop(E_ts, out=r_1, in_=bq[pq:pq + 1, :], s1=LN_INV,
                    op0=ALU.mult)
                vop(E_tt, out=r_r, a=r_m, b=r_m, op=ALU.mult)
                vop(E_tt, out=r_1, a=r_1, b=r_r, op=ALU.subtract)
                sv = vop(E_ts, out=r_r, in_=r_1, s1=eps, op0=ALU.add)
                act(r_1, r_r, AF.Ln, waits=[(s_ve.h, sv)], inc=False)
                act(r_r, r_1, AF.Exp, scale=-0.5)
                marks.append((s_se.n, s_ve.n, r_m, r_r, p))
            cpv = {}
            for tc in range(NTC):
                se_m, ve_m, r_m, r_r, p = marks[tc]
                for j, (src, dstt) in enumerate(((r_m, Mt), (r_r, Rt))):
                    bank = (pA, pB)[j]
                    w = [(s_se.h, se_m), (s_ve.h, ve_m)]
                    if tc >= 1:
                        w.append((s_ve.h, cpv[(tc - 1, j)]))
                    mm(bank[:, :], onp[p:p + 1, 0:128], src,
                       start=True, stop=True, inc=True, waits=w)
                    cpv[(tc, j)] = vop(
                        E_copy, out=dstt[:, tc * 512:(tc + 1) * 512],
                        in_=bank[:, :], waits=[(s_mm.h, s_mm.n)])

        ln_rows(1e-5)
        VE_MR1 = s_ve.n
        MM_R1 = s_mm.n

        ld2 = []
        ve_norm = []
        SE_P1A = s_se.n
        GPC = {2, 4, 7, 9, 12, 14} & set(range(ND))
        for dc in range(ND):
            sem = SEM4[dc % 4]
            if dc >= 4:
                w = [ve_norm[dc - 4]]
            else:
                w = [(s_se.h, se_cast[min(ND - 4 + dc, ND - 1)])]
            ld2.append(dma("sync", sem, sf[dc % 4][:],
                           x_d[dc * 128:(dc + 1) * 128, :], w))
            w = [(sem.h, ld2[dc])]
            if dc in GPC:
                gop(E_tt, out=sf[dc % 4][:], a=sf[dc % 4][:], b=Mt[:],
                    op=ALU.subtract, waits=w + [(s_ve.h, VE_MR1)])
                gv = gop(E_tt, out=pool_a[:, dc * T:(dc + 1) * T],
                         a=sf[dc % 4][:], b=Rt[:], op=ALU.mult)
                ve_norm.append((s_gp.h, gv))
            else:
                if dc == 0:
                    w.append((s_ve.h, VE_MR1))
                vop(E_tt, out=sf[dc % 4][:], a=sf[dc % 4][:], b=Mt[:],
                    op=ALU.subtract, waits=w, inc=False)
                ve_norm.append((s_ve.h,
                                vop(E_tt,
                                    out=pool_a[:, dc * T:(dc + 1) * T],
                                    a=sf[dc % 4][:], b=Rt[:], op=ALU.mult)))
        VE_XN = s_ve.n
        GP_XN = s_gp.n
        if "xn" in dbg_d:
            dma("sync", s_y[0], dbg_d["xn"][:, :], pool_a[:], [(s_ve.h, VE_XN)])

        def proj_phase(wname, bias_s, kind, tm_dest=None, first_mm_waits=(),
                       xn_ready=None, extra_fc=None):
            MM_ENTRY = s_mm.n
            ld_w = {}
            se_done = {}
            stt_done = {}
            ve_stage = {}
            ve_tm = {}
            mm_grp = {}
            tpos_mm = {}
            tpos_ve = {}
            kv_mm = {}
            kv_free = {}
            ks_done = {}

            def emit_wload(fc):
                w = []
                if fc >= 2:
                    w.append((s_mm.h, mm_grp[(fc - 2) * NTC + NTC - 1]))
                else:
                    w.append((s_mm.h, MM_ENTRY))
                ld_w[fc] = dma("gpsimd", s_w[fc % 2], wblk[fc % 2][:, 0:D],
                               w_d[wname][fc, :, :], w)

            def emit_groups_and_evacs(fc):
                stg = stage[fc % 2] if kind != "q" else None
                stage_ready = None
                for tc in range(NTC):
                    g = fc * NTC + tc
                    w = list(first_mm_waits) if g == 0 else []
                    w.append((s_w[fc % 2].h, ld_w[fc]))
                    if g >= 3:
                        w.append((s_se.h, se_done[g - 3]))
                    for dc in range(ND):
                        wd = list(w) if dc == 0 else []
                        if xn_ready is not None and g == 0:
                            wd.append(xn_ready[dc])
                        mm(pbank[g % 3][:, :],
                           wblk[fc % 2][:, dc * 128:(dc + 1) * 128],
                           pool_a[:, dc * T + tc * 512: dc * T + tc * 512 + 512],
                           start=(dc == 0), stop=(dc == ND - 1),
                           inc=(dc == ND - 1), waits=wd)
                    mm_grp[g] = s_mm.n
                    bank = pbank[g % 3]
                    sl = slice(tc * 512, (tc + 1) * 512)
                    w = [(s_mm.h, mm_grp[g])]
                    if tc == 0 and kind in ("v", "k") and fc >= 2:
                        w.append((s_mm.h, tpos_mm[fc - 2]))
                        if kind == "k":
                            w.append((s_ve.h, ks_done[fc - 2]))
                    if kind == "v":
                        se_done[g] = act(stg[:, sl], bank[:, :], AF.Identity,
                                         bias=bias_s[:, fc:fc + 1], waits=w)
                        stt_done[g] = s_ve.n
                        stage_ready = (s_se.h, se_done[g])
                    else:
                        if g >= NTC:
                            w.append((s_ve.h, stt_done[g - NTC]))
                        act(sf[2][:, sl], bank[:, :], AF.Exp,
                            bias=bias_s[:, fc:fc + 1], waits=w, inc=False)
                        se_done[g] = act(sf[3][:, sl], bank[:, :], AF.Relu,
                                         bias=bias_s[:, fc:fc + 1])
                        dst = (pool_b[:, fc * T + tc * 512: fc * T + tc * 512 + 512]
                               if kind == "q" else stg[:, sl])
                        stt_done[g] = vop(
                            E_stt, out=dst, in0=sf[2][:, sl], scalar=1.0,
                            in1=sf[3][:, sl], op0=ALU.min, op1=ALU.add,
                            waits=[(s_se.h, se_done[g])])
                        stage_ready = (s_ve.h, stt_done[g])
                ve_stage[fc] = stage_ready

            def emit_transposes(fc):
                stg = stage[fc % 2]
                for q4 in range(NQ4):
                    half = q4 % 2
                    gi = fc * NQ4 + q4
                    for j in range(4):
                        tt = q4 * 4 + j
                        w = []
                        if j == 0:
                            w.append(ve_stage[fc])
                            if gi >= 2:
                                w.append((s_ve.h, tpos_ve[gi - 2]))
                        tr(ptrb[:, half * 1024 + j * 128: half * 1024 + (j + 1) * 128],
                           stg[:, tt * 128:(tt + 1) * 128], idb[:],
                           inc=(j == 3), waits=w)
                    if kind == "v":
                        d3 = tm_dest[:].rearrange("p (tt d) -> p tt d", d=D)[
                            :, q4 * 4:q4 * 4 + 4, fc * 128:(fc + 1) * 128]
                        w = [(s_mm.h, s_mm.n)]
                    else:
                        d3 = kstp[fc % 2][:].rearrange("p (tt d) -> p tt d",
                                                       d=128)[:, q4 * 4:q4 * 4 + 4, :]
                        w = [(s_mm.h, s_mm.n)]
                        if fc >= 2 and q4 == 0:
                            w.append((s_mm.h, kv_mm[fc - 2]))
                    s3 = ptrb[:, half * 1024:half * 1024 + 512].rearrange(
                        "p (a b) -> p a b", b=128)
                    tpos_ve[gi] = vop(E_copy, out=d3, in_=s3, waits=w)
                tpos_mm[fc] = s_mm.n

            def emit_kv(fc):
                for tt in range(NTT):
                    w = []
                    if tt == 0:
                        w.append((s_ve.h, tpos_ve[fc * NQ4 + NQ4 - 1]))
                        if fc >= 1:
                            w.append((s_ve.h, kv_free[fc - 1]))
                        elif kind == "k":
                            w.append((s_ve.h, VE_RXV))
                    mm(pG[:, 0:128],
                       kstp[fc % 2][:, tt * 128:(tt + 1) * 128],
                       tm_dest[:].rearrange("p (tt d) -> p tt d", d=D)[
                           :, tt, fc * 128:(fc + 1) * 128],
                       start=(tt == 0), stop=(tt == NTT - 1),
                       inc=(tt == NTT - 1), waits=w)
                kv_mm[fc] = s_mm.n
                kv_free[fc] = vop(E_copy, out=kvf[:, fc * 128:(fc + 1) * 128],
                                  in_=pG[:, 0:128],
                                  waits=[(s_mm.h, kv_mm[fc])])
                for tc in range(NTC):
                    rv = vop(E_red, out=kst[:, tc:tc + 1],
                             in_=stage[fc % 2][:, tc * 512:(tc + 1) * 512],
                             op=ALU.add)
                    vop(E_tt, out=ksacc[:, fc:fc + 1], a=ksacc[:, fc:fc + 1],
                        b=kst[:, tc:tc + 1], op=ALU.add,
                        waits=[(s_ve.h, rv)], inc=(tc == NTC - 1))
                ks_done[fc] = s_ve.n

            def emit_tail(fc):
                if kind in ("v", "k"):
                    emit_transposes(fc)
                if kind == "k":
                    emit_kv(fc)

            for fc in range(ND):
                emit_wload(fc)
                emit_groups_and_evacs(fc)
                if fc > 0:
                    emit_tail(fc - 1)
                if extra_fc is not None:
                    extra_fc(fc)
            emit_tail(ND - 1)

        STAT_END = s_mm.n
        wf0_prev = [LD_CONSTS]
        wf0_red = {}

        def wf0_fc(fc):
            w = [(s_lc.h, wf0_prev[0])]
            if fc >= 2:
                w.append((s_ve.h, wf0_red[fc - 2]))
            else:
                w.append((s_mm.h, STAT_END))
            ld = dma("sync", s_lc, kstp[fc % 2][:, 0:D], wf_d[fc, :, :], w)
            wf0_prev[0] = ld
            rv0 = vop(E_red, out=kst[:, 2:3], in_=kstp[fc % 2][:, 0:D],
                      op=ALU.max, absv=True, waits=[(s_lc.h, ld)])
            wf0_red[fc] = vop(E_tt, out=wmx[:], a=wmx[:], b=kst[:, 2:3],
                              op=ALU.max, waits=[(s_ve.h, rv0)])

        proj_phase("wv", vbs, "v", tm_dest=pool_b,
                   first_mm_waits=[(s_lc.h, LD_CONSTS), (s_gp.h, GP_WMX)],
                   xn_ready=ve_norm, extra_fc=wf0_fc)
        VE_WMX = wf0_red[ND - 1]

        d1 = dma("gpsimd", s_cd,
                 kvin_d.ap()[0:128].rearrange("(p f) -> p f", f=1),
                 wmx[:, 0:1], [(s_ve.h, VE_WMX)])
        d2 = dma("gpsimd", s_cd,
                 rows[0:1, 0:128],
                 kvin_d.ap()[0:128].rearrange("(f p) -> f p", f=1),
                 [(s_cd.h, d1), (s_ve.h, VE_MR1), (s_mm.h, MM_R1)])
        rv = vop(E_red, out=sc[0:1, 0:1], in_=rows[0:1, 0:128], op=ALU.max,
                 waits=[(s_cd.h, d2)])
        vop(E_ts, out=sc[0:1, 1:2], in_=sc[0:1, 0:1], s1=INV448,
            op0=ALU.mult, waits=[(s_ve.h, rv)])
        vop(E_ts, out=sc[0:1, 1:2], in_=sc[0:1, 1:2], s1=1e-12, op0=ALU.max)
        rv = vop(E_recip, out=sc[0:1, 2:3], in_=sc[0:1, 1:2])
        VE_WS = vop(E_ts, out=sc[0:1, 2:3], in_=sc[0:1, 2:3], s1=0.5,
                    op0=ALU.mult, waits=[(s_ve.h, rv)])
        mm(pG[:, 0:1], onp[0:1, 0:128], sc[0:1, 2:3], start=True, stop=True,
           inc=True, waits=[(s_ve.h, VE_WS)])
        VE_RXV = vop(E_copy, out=rxv[:], in_=pG[:, 0:1],
                     waits=[(s_mm.h, s_mm.n)])
        if "vtm" in dbg_d:
            dma("sync", s_y[0], dbg_d["vtm"][:, :], pool_b[:], [(s_ve.h, s_ve.n)])

        proj_phase("wk", kbs, "k", tm_dest=pool_b)
        VE_KV = s_ve.n
        MM_KV = s_mm.n
        if "kvf" in dbg_d:
            dma("sync", s_y[0], dbg_d["kvf"][:, :], kvf, [(s_ve.h, VE_KV)])
        if "ks" in dbg_d:
            dma("sync", s_y[0], dbg_d["ks"][:, :], ksacc[:], [(s_ve.h, VE_KV)])

        dma("gpsimd", s_cd,
            kvin_d.ap()[0:128 * D].rearrange("(p f) -> p f", p=128),
            kvf, [(s_ve.h, VE_KV)])
        dma("gpsimd", s_cd,
            kvin_d.ap()[128 * D:128 * (D + ND)].rearrange("(p f) -> p f", p=128),
            ksacc[:, 0:ND], [])
        KV_IN = s_cd.n
        A("gpsimd", partial(E_cc, kind="AllReduce", op=ALU.add, groups=pairs,
                            ins=[kvin_d.ap().opt()], outs=[kvout_d.ap().opt()],
                            sem=s_cc.h, waits=((s_cd.h, KV_IN),)))
        s_cc.n += 1
        CC_KV = s_cc.n

        proj_phase("wq", qbs, "q")
        VE_QP = s_ve.n
        SE_Q = s_se.n
        if "qp" in dbg_d:
            dma("sync", s_y[0], dbg_d["qp"][:, :], pool_b[:], [(s_ve.h, VE_QP)])

        dma("gpsimd", s_cd, kvb,
            kvout_d.ap()[0:128 * D].rearrange("(p f) -> p f", p=128),
            [(s_cc.h, CC_KV)])
        dma("gpsimd", s_cd, ksb[:, 0:ND],
            kvout_d.ap()[128 * D:128 * (D + ND)].rearrange("(p f) -> p f", p=128),
            [])
        WD_KVB = s_cd.n

        L = 3
        NIT = ND * NTC
        wo_ld = {}
        se_ln = {}
        se_zr = {}
        ve_apply = {}
        zmm = {}
        zrmm = {}
        amm = {}

        def qslice(it):
            h, tc = divmod(it, NTC)
            return pool_b[:, h * T + tc * 512: h * T + tc * 512 + 512]

        def zrrow(it):
            j = it % 3
            return rows[0:1, j * 512:j * 512 + 512]

        for step in range(NIT + L):
            jt = step - L
            if 0 <= jt < NIT:
                h = jt // NTC
                w = [(s_se.h, se_ln[jt])]
                if jt >= 1:
                    w.append((s_se.h, se_zr[jt - 1]))
                zrmm[jt] = mm(pC[:, :], onp[0:1, 0:128], zrrow(jt),
                              start=True, stop=True, inc=True, waits=w)
                w = [(s_mm.h, zrmm[jt])]
                if jt >= 2:
                    w.append((s_ve.h, ve_apply[jt - 2]))
                se_zr[jt] = act(ZR[jt % 2], pC[:, :], AF.Exp, scale=-1.0,
                                waits=w)
                ab = (pA, pB)[jt % 2]
                w = []
                if jt >= 2:
                    w.append((s_ve.h, ve_apply[jt - 2]))
                amm[jt] = mm(ab[:, :], kvb[:, h * 128:(h + 1) * 128],
                             qslice(jt), start=True, stop=True, inc=True,
                             waits=w)
                ve_apply[jt] = vop(
                    E_tt,
                    out=pool_a[:, h * T + (jt % NTC) * 512:
                               h * T + (jt % NTC) * 512 + 512],
                    a=ab[:, :], b=ZR[jt % 2], op=ALU.mult,
                    waits=[(s_mm.h, amm[jt]), (s_se.h, se_zr[jt])])
                if jt % NTC == NTC - 1:
                    hh = jt // NTC
                    wo_ld[hh] = dma(
                        "gpsimd", s_w[hh % 2],
                        pool_b[:, hh * D:(hh + 1) * D].rearrange(
                            "p (g f) -> p g f", g=ND),
                        w_d["wo"][:, :, hh * 128:(hh + 1) * 128].rearrange(
                            "g p f -> p g f"),
                        [(s_mm.h, amm[jt]),
                         (s_w[hh % 2].h, s_w[hh % 2].n)])
            it = step
            if it < NIT:
                zb = (pD, pE)[it % 2]
                w = []
                if it == 0:
                    w = [(s_cd.h, WD_KVB), (s_ve.h, VE_QP), (s_se.h, SE_Q)]
                if it >= 2:
                    w.append((s_se.h, se_ln[it - 2]))
                zmm[it] = mm(zb[0:1, :], ksb[:, it // NTC:it // NTC + 1],
                             qslice(it), start=True, stop=True, inc=True,
                             waits=w)
                w = [(s_mm.h, zmm[it])]
                if it >= 3:
                    w.append((s_mm.h, zrmm[it - 3]))
                se_ln[it] = act(zrrow(it), zb[0:1, :], AF.Ln, waits=w)
        VE_ATTN = s_ve.n
        SE_P6 = s_se.n
        MM_P6 = s_mm.n
        if "attn" in dbg_d:
            dma("sync", s_y[0], dbg_d["attn"][:, :], pool_a[:], [(s_ve.h, VE_ATTN)])

        st_x1 = {}
        st_u = {}
        ld_xr = {}
        ld9 = {}
        se_sq = {}
        stt7 = {}
        statv = {}
        qfree9 = {}
        mr2v = {}
        ln2_marks = {}

        VE_ZT2 = vop(E_memset, ap=zt, val=0.0,
                     waits=[(s_mm.h, MM_P6), (s_ve.h, VE_ATTN)])
        stat_zero_pass([(s_ve.h, VE_ZT2), (s_se.h, SE_P6)])

        def x1q(g):
            q = g % 8
            return sf[q // 4][:, (q % 4) * 512:(q % 4) * 512 + 512]

        def xrq(g):
            q = g % 4
            return wblkF[q // 2][:, (q % 2) * 512:(q % 2) * 512 + 512]

        def buf9(idx):
            q = idx % 8
            return sf[2 + q // 4][:, (q % 4) * 512:(q % 4) * 512 + 512]

        def stq(i):
            return stage[i // 4][:, (i % 4) * 512:(i % 4) * 512 + 512]

        def ln2_slots(tc):
            p = ROWP[tc]
            if tc < 3:
                return (rows[p:p + 1, 0:512], rows[p:p + 1, 512:1024],
                        rows[p:p + 1, 1024:1536])
            return (kstpF[0][0:1, 0:512], kstpF[0][0:1, 512:1024],
                    kstpF[1][0:1, 0:512])

        def ln2_rows_stats(tc):
            r_m, r_1, r_r = ln2_slots(tc)
            bs, ps = SITES[0][tc]
            bq, pq = SITES[1][tc]
            vop(E_ts, out=r_m, in_=bs[ps:ps + 1, :], s1=LN_INV,
                op0=ALU.mult, waits=[(s_mm.h, statv[tc * ND + ND - 1])])
            vop(E_ts, out=r_1, in_=bq[pq:pq + 1, :], s1=LN_INV,
                op0=ALU.mult)
            vop(E_tt, out=r_r, a=r_m, b=r_m, op=ALU.mult)
            vop(E_tt, out=r_1, a=r_1, b=r_r, op=ALU.subtract)
            sv = vop(E_ts, out=r_r, in_=r_1, s1=1e-5, op0=ALU.add)
            act(r_1, r_r, AF.Ln, waits=[(s_ve.h, sv)], inc=False)
            act(r_r, r_1, AF.Exp, scale=-0.5)
            return (s_se.n, s_ve.n)

        last_rep_v = [0]

        def ln2_rows_rep(tc):
            se_mark, ve_mark = ln2_marks[tc]
            r_m, r_1, r_r = ln2_slots(tc)
            p = ROWP[tc]
            mm(ptrbF2, onp[p:p + 1, 0:128], r_m,
               start=True, stop=True, inc=True,
               waits=[(s_se.h, se_mark),
                      (s_ve.h, max(ve_mark, last_rep_v[0]))])
            mv = vop(E_copy, out=Mt[:, tc * 512:(tc + 1) * 512], in_=ptrbF2,
                     waits=[(s_mm.h, s_mm.n)])
            mm(ptrbF2, onp[p:p + 1, 0:128], r_r,
               start=True, stop=True, inc=True,
               waits=[(s_ve.h, mv)])
            rv2 = vop(E_copy, out=Rt[:, tc * 512:(tc + 1) * 512],
                      in_=ptrbF2, waits=[(s_mm.h, s_mm.n)])
            last_rep_v[0] = rv2
            return rv2

        def emit_p9_load(idx):
            tc9, gc9 = divmod(idx, ND)
            w = [(s_y[idx % 2].h, st_x1[idx])]
            if idx < 8:
                w += [(s_ve.h, VE_ATTN), (s_se.h, SE_P6)]
            if idx >= 2:
                w.append((s_f[idx % 2].h, ld9[idx - 2]))
            if idx >= 8:
                w += [(s_ve.h, qfree9[idx - 8]),
                      (s_u[(idx - 8) % 2].h, st_u[idx - 8])]
            ld9[idx] = dma("sync", s_f[idx % 2], buf9(idx),
                           x1_d.ap()[gc9 * 128:(gc9 + 1) * 128,
                                     tc9 * 512:(tc9 + 1) * 512], w)

        def emit_p9_chunk(idx, on_gp=False, inplace=False):
            tc9, gc9 = divmod(idx, ND)
            tw = slice(tc9 * 512, (tc9 + 1) * 512)
            if inplace:
                b = x1q(idx)
                w = [(s_y[idx % 2].h, st_x1[idx]), (s_se.h, se_sq[idx])]
            else:
                b = buf9(idx)
                w = [(s_f[idx % 2].h, ld9[idx])]
            if gc9 == 0:
                w.append((s_ve.h, mr2v[tc9]))
            if on_gp:
                gop(E_tt, out=b, a=b, b=Mt[:, tw], op=ALU.subtract,
                    waits=w + [(s_ve.h, mr2v[tc9])])
                du = gop(E_tt, out=b, a=b, b=Rt[:, tw], op=ALU.mult)
                datw = [(s_gp.h, du)]
            else:
                vop(E_tt, out=b, a=b, b=Mt[:, tw], op=ALU.subtract,
                    waits=w, inc=False)
                du = vop(E_tt, out=b, a=b, b=Rt[:, tw], op=ALU.mult)
                datw = [(s_ve.h, du)]
            rv = vop(E_red, out=tmx[:, 0:1], in_=b, op=ALU.max, absv=True,
                     waits=datw)
            qfree9[idx] = vop(E_tt, out=amx[:], a=amx[:], b=tmx[:, 0:1],
                              op=ALU.max, waits=[(s_ve.h, rv)])
            st_u[idx] = dma("scalar", s_u[idx % 2],
                            xn2_d.ap()[gc9 * 128:(gc9 + 1) * 128, tw],
                            b, list(datw) + [(s_u[idx % 2].h,
                                              s_u[idx % 2].n)])

        def emit_stat(g):
            tcs, gcs = divmod(g, ND)
            pr = g % 3
            for bi in range(2):
                bank, p = SITES[bi][tcs]
                mm(bank[p:p + 1, :], onesb[:], stq(2 * pr + bi),
                   start=False, stop=(gcs == ND - 1), tp=(0, p),
                   inc=(bi == 1),
                   waits=([(s_se.h, se_sq[g])] if bi == 0 else []))
            statv[g] = s_mm.n
            if gcs == ND - 1:
                ln2_marks[tcs] = ln2_rows_stats(tcs)

        wo_last = (wo_ld[ND - 2], wo_ld[ND - 1])
        NG = NTC * ND
        pending_rep = [None]
        for tc in range(NTC):
            tw = slice(tc * 512, (tc + 1) * 512)
            for gc in range(ND):
                g = tc * ND + gc
                w = [(s_x[g % 2].h, ld_xr[g - 2])] if g >= 2 else []
                if g >= 4:
                    w.append((s_ve.h, stt7[g - 4]))
                else:
                    w.append((s_mm.h, MM_P6))
                ld_xr[g] = dma("sync", s_x[g % 2], xrq(g),
                               x_d[gc * 128:(gc + 1) * 128, tw], w)
                w = []
                if g == 0:
                    w = [(s_ve.h, VE_ATTN), (s_se.h, SE_P6),
                         (s_w[0].h, wo_last[0]), (s_w[1].h, wo_last[1])]
                if g >= 3:
                    w.append((s_ve.h, stt7[g - 3]))
                for ec in range(ND):
                    mm(pbank[g % 3][:, :],
                       pool_b[:, ec * D + gc * 128: ec * D + gc * 128 + 128],
                       pool_a[:, ec * T + tc * 512: ec * T + tc * 512 + 512],
                       start=(ec == 0), stop=(ec == ND - 1),
                       inc=(ec == ND - 1), waits=(w if ec == 0 else []))
                mmg = s_mm.n
                if g >= 2:
                    emit_stat(g - 2)
                if pending_rep[0] is not None:
                    mr2v[pending_rep[0]] = ln2_rows_rep(pending_rep[0])
                    pending_rep[0] = None
                if g >= 2 and (g - 2) % ND == ND - 1:
                    pending_rep[0] = (g - 2) // ND
                w = [(s_mm.h, mmg), (s_x[g % 2].h, ld_xr[g])]
                if g >= 8:
                    w += [(s_y[g % 2].h, st_x1[g - 8]),
                          (s_se.h, se_sq[g - 8])]
                stt7[g] = vop(E_stt, out=x1q(g), in0=pbank[g % 3][:, :],
                              scalar=obs[:, gc:gc + 1], in1=xrq(g),
                              op0=ALU.add, op1=ALU.add, waits=w)
                st_x1[g] = dma("scalar", s_y[g % 2],
                               x1_d.ap()[gc * 128:(gc + 1) * 128, tw],
                               x1q(g), [(s_ve.h, stt7[g]),
                                        (s_y[g % 2].h, s_y[g % 2].n)])
                pr = g % 3
                w = [(s_ve.h, stt7[g])]
                if g >= 3:
                    w.append((s_mm.h, statv[g - 3]))
                else:
                    w.append((s_mm.h, MM_P6))
                act(stq(2 * pr), x1q(g), AF.Copy, waits=w, inc=False)
                se_sq[g] = act(stq(2 * pr + 1), x1q(g), AF.Square)
                if g >= ND:
                    emit_p9_load(g - ND)
                if g >= ND + 2:
                    emit_p9_chunk(g - ND - 2)
        for g in (NG - 2, NG - 1):
            emit_stat(g)
        if pending_rep[0] is not None:
            mr2v[pending_rep[0]] = ln2_rows_rep(pending_rep[0])
            pending_rep[0] = None
        mr2v[NTC - 1] = ln2_rows_rep(NTC - 1)
        VE_P7 = s_ve.n
        SE_P7 = s_se.n
        MM_P7END = s_mm.n
        Y0_P7 = s_y[0].n
        Y1_P7 = s_y[1].n

        nres = min(8, ND)
        for idx in range(NG - ND - 2, NG):
            nxt = idx + 2
            if (NTC - 1) * ND <= nxt < NG - nres:
                emit_p9_load(nxt)
            emit_p9_chunk(idx, on_gp=(idx >= (NTC - 1) * ND
                                      and idx % 2 == 1),
                          inplace=(idx >= NG - nres))
        VE_AMX = s_ve.n
        GP_AMX = s_gp.n
        U0_P9 = s_u[0].n
        U1_P9 = s_u[1].n

        ld_wf = {}
        se_wq = {}
        for gcw in range(ND):
            w = [(s_f[gcw % 2].h, s_f[gcw % 2].n)]
            if gcw < 2:
                w.append((s_ve.h, VE_P7))
            else:
                w.append((s_se.h, se_wq[gcw - 2]))
            ld_wf[gcw] = dma("sync", s_f[gcw % 2], wblk[gcw % 2][:, 0:D],
                             wf_d[gcw, :, :], w)
            w = [(s_f[gcw % 2].h, ld_wf[gcw])]
            if gcw == 0:
                w += [(s_mm.h, MM_P7END), (s_ve.h, VE_RXV)]
            se_wq[gcw] = act(wq8[:, gcw * D:(gcw + 1) * D],
                             wblk[gcw % 2][:, 0:D], AF.Identity,
                             scale=rxv[:, 0:1], waits=w)
        SE_WQ = s_se.n

        d3 = dma("gpsimd", s_cd,
                 kvin_d.ap()[0:128].rearrange("(p f) -> p f", f=1),
                 amx[:, 0:1], [(s_ve.h, VE_AMX), (s_gp.h, GP_AMX)])
        d4 = dma("gpsimd", s_cd,
                 rows[0:1, 0:128],
                 kvin_d.ap()[0:128].rearrange("(f p) -> f p", f=1),
                 [(s_cd.h, d3), (s_ve.h, s_ve.n)])
        VE_AM1 = vop(E_red, out=sc[0:1, 16:17], in_=rows[0:1, 0:128],
                     op=ALU.max, waits=[(s_cd.h, d4)])
        dma("gpsimd", s_cd, amin_d.ap()[0:8].rearrange("(a b) -> a b", a=1),
            sc[0:1, 16:24], [(s_ve.h, VE_AM1)])
        A("gpsimd", partial(E_cc, kind="AllReduce", op=ALU.max, groups=allg,
                            ins=[amin_d.ap().opt()], outs=[amout_d.ap().opt()],
                            sem=s_cc.h, waits=((s_cd.h, s_cd.n),)))
        s_cc.n += 1
        dma("gpsimd", s_cd, sc[0:1, 24:32],
            amout_d.ap()[0:8].rearrange("(a b) -> a b", a=1),
            [(s_cc.h, s_cc.n)])
        WD_AM = s_cd.n
        vop(E_ts, out=sc[0:1, 17:18], in_=sc[0:1, 24:25], s1=INV448,
            op0=ALU.mult, waits=[(s_cd.h, WD_AM)])
        vop(E_ts, out=sc[0:1, 17:18], in_=sc[0:1, 17:18], s1=1e-12,
            op0=ALU.max)
        rv = vop(E_recip, out=sc[0:1, 18:19], in_=sc[0:1, 17:18])
        vop(E_ts, out=sc[0:1, 18:19], in_=sc[0:1, 18:19], s1=0.5,
            op0=ALU.mult, waits=[(s_ve.h, rv)])
        vop(E_tt, out=sc[0:1, 19:20], a=sc[0:1, 17:18], b=sc[0:1, 1:2],
            op=ALU.mult)
        VE_SC = vop(E_ts, out=sc[0:1, 19:20], in_=sc[0:1, 19:20], s1=4.0,
                    op0=ALU.mult)
        if "sc" in dbg_d:
            dma("sync", s_y[0], dbg_d["sc"][:, :], sc[:], [(s_ve.h, VE_SC)])
        mm(pG[:, 0:1], onp[0:1, 0:128], sc[0:1, 18:19], start=True, stop=True,
           inc=True, waits=[(s_ve.h, VE_SC)])
        VE_RXQ = vop(E_copy, out=rxq[:], in_=pG[:, 0:1],
                     waits=[(s_mm.h, s_mm.n)])
        mm(pG[:, 0:1], onp[0:1, 0:128], sc[0:1, 19:20], start=True, stop=True,
           inc=True, waits=[(s_ve.h, s_ve.n)])
        VE_S4 = vop(E_copy, out=s4v[:], in_=pG[:, 0:1],
                    waits=[(s_mm.h, s_mm.n)])

        ld_p11 = {}
        ld_q = {}
        ve_q = {}
        npre = ND // 2
        for j in range(npre):
            w = [(s_u[j % 2].h, st_u[(NTC - 1) * ND + j]),
                 (s_p[j % 2].h, s_p[j % 2].n), (s_mm.h, MM_P7END)]
            ld_p11[j] = dma("sync", s_p[j % 2], pbf[:, j * T:(j + 1) * T],
                            xn2_d.ap()[j * 128:(j + 1) * 128, :], w)
        for j in range(npre):
            w = [(s_p[j % 2].h, ld_p11[j])]
            if j == 0:
                w.append((s_ve.h, VE_RXQ))
            ve_q[j] = vop(E_ts, out=xq8[:, j * T:(j + 1) * T],
                          in_=pbf[:, j * T:(j + 1) * T], s1=rxq[:, 0:1],
                          op0=ALU.mult, waits=w)
        nring = min(4, ND - npre)
        for i in range(ND - npre + nring):
            if i >= nring:
                dcv = npre + i - nring
                ve_q[dcv] = vop(E_ts, out=xq8[:, dcv * T:(dcv + 1) * T],
                                in_=sf[dcv % nring][:], s1=rxq[:, 0:1],
                                op0=ALU.mult,
                                waits=[(s_x[dcv % 2].h, ld_q[dcv])])
            dcq = npre + i
            if dcq <= ND - 1:
                w = [(s_u[dcq % 2].h, st_u[(NTC - 1) * ND + dcq]),
                     (s_x[dcq % 2].h, s_x[dcq % 2].n),
                     (s_ve.h, VE_AMX), (s_gp.h, GP_AMX),
                     (s_u[0].h, U0_P9), (s_u[1].h, U1_P9)]
                if dcq >= npre + nring:
                    w.append((s_ve.h, ve_q[dcq - nring]))
                ld_q[dcq] = dma("sync", s_x[dcq % 2], sf[dcq % nring][:],
                                xn2_d.ap()[dcq * 128:(dcq + 1) * 128, :], w)
        VE_XQ = s_ve.n
        if "xq" in dbg_d:
            dma("sync", s_y[0], dbg_d["xq"][:, :], xq8[:], [(s_ve.h, VE_XQ)])

        ld_r2 = {}
        ve_out = {}
        st_y = {}
        v12 = {}
        nrr = min(2, ND // 2)
        for gc in range(ND):
            w = [(s_x[gc % 2].h, s_x[gc % 2].n)]
            if gc >= nrr:
                w.append((s_ve.h, ve_out[gc - nrr]))
            else:
                w.append((s_ve.h, ve_q[gc]))
            ld_r2[gc] = dma("sync", s_x[gc % 2],
                            pbf[:, (gc % nrr) * T:(gc % nrr) * T + T],
                            x1_d.ap()[gc * 128:(gc + 1) * 128, :], w)
            for tc in range(NTC):
                g = gc * NTC + tc
                w = []
                if g == 0:
                    w = [(s_ve.h, VE_XQ), (s_se.h, SE_WQ)]
                if g >= 3:
                    w.append((s_ve.h, v12[g - 3]))
                for dc in range(ND // 2):
                    lhsT = wq8[:, gc * D + dc * 256:
                               gc * D + dc * 256 + 256].rearrange(
                        "p (j c) -> p j c", j=2)
                    rhs = xq8[:, dc * 2 * T:(dc * 2 + 2) * T].rearrange(
                        "p (j t) -> p j t", j=2)[:, :, tc * 512:(tc + 1) * 512]
                    mm(pbank[g % 3][:, :], lhsT, rhs,
                       start=(dc == 0), stop=(dc == ND // 2 - 1),
                       inc=(dc == ND // 2 - 1), pm=PM_DR,
                       waits=(w if dc == 0 else []))
                mg = s_mm.n
                sl = slice(tc * 512, (tc + 1) * 512)
                vop(E_ts, out=fb, in_=pbank[g % 3][:, :], s1=s4v[:, 0:1],
                    op0=ALU.mult, waits=[(s_mm.h, mg)], inc=False)
                w = [(s_x[gc % 2].h, ld_r2[gc])]
                if gc >= 2 and tc == 0:
                    w.append((s_y[gc % 2].h, st_y[gc - 2]))
                if gc < 2 and tc == 0:
                    w += [(s_u[0].h, U0_P9), (s_u[1].h, U1_P9),
                          (s_gp.h, GP_AMX)]
                v12[g] = vop(E_tt, out=sf[2 + gc % 2][:, sl],
                             a=pbf[:, (gc % nrr) * T + tc * 512:
                                   (gc % nrr) * T + (tc + 1) * 512],
                             b=fb, op=ALU.add, waits=w)
            ve_out[gc] = s_ve.n
            st_y[gc] = dma("scalar", s_y[gc % 2],
                           y_d[gc * 128:(gc + 1) * 128, :],
                           sf[2 + gc % 2][:],
                           [(s_ve.h, ve_out[gc]),
                            (s_y[gc % 2].h, s_y[gc % 2].n)])

        A("sync", partial(E_wait, waits=((s_y[0].h, s_y[0].n),
                                         (s_y[1].h, s_y[1].n))))

        @block.sync
        def _(eng):
            for f in prog["sync"]:
                f(eng)

        @block.gpsimd
        def _(eng):
            for f in prog["gpsimd"]:
                f(eng)

        @block.tensor
        def _(eng):
            for f in prog["tensor"]:
                f(eng)

        @block.vector
        def _(eng):
            for f in prog["vector"]:
                f(eng)

        @block.scalar
        def _(eng):
            for f in prog["scalar"]:
                f(eng)

    return nc


def _wblocks(wt, dt):
    d_in, d_out = wt.shape
    a = np.ascontiguousarray(
        wt.reshape(d_in // 128, 128, d_out // 128, 128).transpose(2, 1, 0, 3)
    ).reshape(d_out // 128, 128, d_in)
    return np.ascontiguousarray(a.astype(dt, copy=False))


def kernel(L0_attn_hidden_states, L0_attn_q_w, L0_attn_q_b, L0_attn_k_w,
           L0_attn_k_b, L0_attn_v_w, L0_attn_v_b, L0_attn_o_w, L0_attn_o_b,
           L0_ffn_weight):
    x = np.asarray(L0_attn_hidden_states)
    B, S, D = x.shape
    ncores = 8
    Tc = (B * S) // ncores
    ND = D // 128

    nc = build_nc(D=D, T=Tc, ncores=ncores)

    wq = _wblocks(np.ascontiguousarray(np.asarray(L0_attn_q_w).T), np.float32)
    wk = _wblocks(np.ascontiguousarray(np.asarray(L0_attn_k_w).T), np.float32)
    wv = _wblocks(np.ascontiguousarray(np.asarray(L0_attn_v_w).T), np.float32)
    wo = _wblocks(np.ascontiguousarray(np.asarray(L0_attn_o_w).T), np.float32)
    wf = _wblocks(np.ascontiguousarray(np.asarray(L0_ffn_weight).T),
                  ml_dtypes.bfloat16)
    qb = np.ascontiguousarray(np.asarray(L0_attn_q_b, np.float32).reshape(ND, 128).T)
    kb = np.ascontiguousarray(np.asarray(L0_attn_k_b, np.float32).reshape(ND, 128).T)
    vb = np.ascontiguousarray(np.asarray(L0_attn_v_b, np.float32).reshape(ND, 128).T)
    ob = np.ascontiguousarray(np.asarray(L0_attn_o_b, np.float32).reshape(ND, 128).T)
    onesb = np.ones((128, 1), ml_dtypes.bfloat16)
    onp_ = np.ones((128, 128), np.float32)
    idb = np.eye(128, dtype=ml_dtypes.bfloat16)

    halves = S // 2
    in_maps = []
    for c in range(ncores):
        b, h = c >> 1, c & 1
        slab = np.asarray(x[b, h * halves:(h + 1) * halves, :], np.float32)
        x_fm = np.ascontiguousarray(slab.T)
        in_maps.append({
            "x": x_fm, "wq": wq, "wk": wk, "wv": wv, "wo": wo, "wf": wf,
            "qb": qb, "kb": kb, "vb": vb, "ob": ob, "onesb": onesb,
            "onp": onp_, "idb": idb,
        })

    res = run_bass_kernel_spmd(nc, in_maps, core_ids=list(range(ncores)))

    out = np.empty((B, S, D), np.float32)
    for c in range(ncores):
        b, h = c >> 1, c & 1
        out[b, h * halves:(h + 1) * halves, :] = res.results[c]["y"].T
    return out
